# version 07
# baseline (speedup 1.0000x reference)
import numpy as np
import ml_dtypes
from contextlib import ExitStack
import concourse.bass as bass
import concourse.mybir as mybir

F32 = mybir.dt.float32
BF16 = mybir.dt.bfloat16
AF = mybir.ActivationFunctionType
ALU = mybir.AluOpType

MAGIC = 12582912.0
CW1, CW2, CW3 = 6.28271484375, 0.0004704594612121582, 3.968374073792802e-09
NEG = -1e30
LN_EPS = 1e-5


class Ctx:
    ENGINES = ("pe", "dve", "act", "pool", "sp")

    def __init__(self, nc, es):
        self.nc = nc
        self.es = es
        self.q = {e: [] for e in self.ENGINES}
        self.sems = {}
        self.cnt = {}
        for s in ("pe", "dve", "act", "pool", "bar"):
            self.mksem(s)

    def mksem(self, s):
        self.sems[s] = self.es.enter_context(self.nc.semaphore(s))
        self.cnt[s] = 0

    def op(self, eng, fn, inc=None, incn=1):
        if inc is None:
            self.q[eng].append(fn)
            return None
        h = self.sems[inc]
        self.cnt[inc] += incn
        self.q[eng].append(lambda e, fn=fn, h=h, n=incn: fn(e).then_inc(h, n))
        return (inc, self.cnt[inc])

    def wait(self, eng, tok):
        if tok is None:
            return
        s, v = tok
        if v <= 0:
            return
        h = self.sems[s]
        self.q[eng].append(lambda e, h=h, v=v: e.wait_ge(h, v))

    def drain(self, eng):
        self.q[eng].append(lambda e: e.drain())

    def milestone(self, eng):
        return (eng, self.cnt[eng])

    def barrier(self, name="", queues=()):
        for q in queues:
            q.flush()
        h = self.sems["bar"]
        tgt = self.cnt["bar"] + len(self.ENGINES)
        for eng in self.ENGINES:
            self.q[eng].append(lambda e: e.drain())
            self.q[eng].append(lambda e, h=h: e.nop().then_inc(h, 1))
            self.q[eng].append(lambda e, h=h, t=tgt: e.wait_ge(h, t))
        self.cnt["bar"] = tgt

    def finalize(self):
        with self.nc.Block() as block:
            @block.tensor
            def _(e):
                for f in self.q["pe"]:
                    f(e)

            @block.vector
            def _(e):
                for f in self.q["dve"]:
                    f(e)

            @block.scalar
            def _(e):
                for f in self.q["act"]:
                    f(e)

            @block.gpsimd
            def _(e):
                for f in self.q["pool"]:
                    f(e)

            @block.sync
            def _(e):
                for f in self.q["sp"]:
                    f(e)


class DmaQueue:

    def __init__(self, ctx, eng, name, nlanes):
        self.ctx, self.eng = ctx, eng
        self.lanes = []
        for i in range(nlanes):
            s = f"{name}{i}"
            ctx.mksem(s)
            self.lanes.append(s)
        self.i = 0

    def issue(self, fn):
        s = self.lanes[self.i % len(self.lanes)]
        self.i += 1
        self.ctx.wait(self.eng, (s, self.ctx.cnt[s]))
        return self.ctx.op(self.eng, fn, inc=s, incn=16)

    def flush(self):
        for s in self.lanes:
            self.ctx.wait(self.eng, (s, self.ctx.cnt[s]))


class Arena:
    def __init__(self, ar, base, limit):
        self.ar, self.base, self.limit, self.off = ar, base, limit, base

    def bf(self, cols):
        o = self.off
        assert o + cols <= self.limit, f"arena overflow {o}+{cols}>{self.limit}"
        self.off = o + cols
        return self.ar[:, o:o + cols]

    def f32(self, words):
        if self.off % 2:
            self.off += 1
        o = self.off
        assert o + 2 * words <= self.limit, f"arena overflow {o}+{2*words}>{self.limit}"
        self.off = o + 2 * words
        return self.ar[:, o:o + 2 * words].bitcast(F32)


class Env:
    pass


ALL_PARTS = ("l0a", "l0f1", "l0b", "l0f2", "l1a", "l1f1", "l1b", "l1f2")


def build(T=4096, parts=ALL_PARTS):
    nc = bass.Bass("TRN2", target_bir_lowering=False)
    es = ExitStack()
    es.__enter__()
    env = Env()
    env.T, env.TC, env.NT = T, T // 512, T // 128
    D = {}

    def din(name, shape, dt=F32):
        D[name] = nc.dram_tensor(name, list(shape), dt, kind="ExternalInput").ap()

    din("x", (T, 1024))
    din("L0a_Wq", (1024, 1024))
    din("L0f_Wr", (1024, 8)); din("L0f_br", (8,))
    din("L0f_g", (1024,)); din("L0f_b", (1024,))
    din("L0f_W1", (8, 1024, 4096)); din("L0f_b1", (8, 4096))
    din("L0f_W2", (8, 4096, 1024)); din("L0f_b2", (8, 1024))
    din("L0b_WQ", (16, 1024, 64)); din("L0b_bQ", (16, 64))
    din("L0b_WK", (16, 1024, 64)); din("L0b_bK", (16, 64))
    din("L0b_WV", (16, 1024, 64)); din("L0b_bV", (16, 64))
    din("L0b_WO", (16, 64, 1024)); din("L0b_bO", (1024,))
    din("L1a_W1", (1024, 1024)); din("L1a_W2", (1024, 1024))
    din("L1a_bias", (1024,)); din("L1a_et", (21, 1024))
    din("L1f_CB", (1024, 1024)); din("L1f_SCB", (1024,)); din("L1f_bias", (1024,))
    din("L1b_Win", (4480, 1024)); din("L1b_dtb", (32,))
    din("L1b_Bb", (32, 128)); din("L1b_Cb", (32, 128))
    din("L1b_Bnw", (128,)); din("L1b_Cnw", (128,)); din("L1b_Dsk", (32,))
    din("L1b_Wout", (1024, 2048))
    din("c_eye", (128, 128))
    din("c_rope_cos", (128, T)); din("c_rope_sin", (128, T)); din("c_ropeflip", (128, 128))
    din("c_cmask_qk", (4 * 128, 512)); din("c_cmask_kq", (4 * 128, 512))
    din("c_lstrict", (128, 128))
    env.D = D

    env.out = nc.dram_tensor("out", [T, 1024], F32, kind="ExternalOutput").ap()
    env.resid = nc.dram_tensor("resid", [1024, T], F32, kind="Internal").ap()
    env.xn_hbm = nc.dram_tensor("xn_hbm", [1024, T], BF16, kind="Internal").ap()
    env.s_qT = nc.dram_tensor("s_qT", [1024, T], BF16, kind="Internal").ap()
    env.s_kT = nc.dram_tensor("s_kT", [1024, T], BF16, kind="Internal").ap()
    env.s_vT = nc.dram_tensor("s_vT", [1024, T], BF16, kind="Internal").ap()
    env.s_zT = nc.dram_tensor("s_zT", [2048, T], BF16, kind="Internal").ap()
    env.s_xtok = nc.dram_tensor("s_xtok", [T, 2048], BF16, kind="Internal").ap()
    env.s_aT = nc.dram_tensor("s_aT", [T, T], BF16, kind="Internal").ap()
    env.s_sch = nc.dram_tensor("s_sch", [448, T], F32, kind="Internal").ap()
    env.dbg = nc.dram_tensor("dbg", [128, 8192], F32, kind="Internal").ap()
    env.s_ztok = nc.dram_tensor("s_ztok", [T, 2048], BF16, kind="Internal").ap()

    ctx = Ctx(nc, es)
    env.ctx, env.nc = ctx, nc
    env.qsp = DmaQueue(ctx, "sp", "qs", 6)
    env.qwt = DmaQueue(ctx, "pool", "qw", 6)
    env.qac = DmaQueue(ctx, "pool", "qa", 4)
    env.ALLQ = (env.qsp, env.qwt, env.qac)

    ARCOLS = 101376
    env.AR = es.enter_context(nc.sbuf_tensor("AR", [128, ARCOLS], BF16))
    env.SR = es.enter_context(nc.sbuf_tensor("SR", [128, 512], F32))
    env.id_f = es.enter_context(nc.sbuf_tensor("id_f", [128, 128], F32))
    env.id_b = es.enter_context(nc.sbuf_tensor("id_b", [128, 128], BF16))
    env.ones_r_f = es.enter_context(nc.sbuf_tensor("ones_r_f", [1, 128], F32))
    env.ones_c_b = es.enter_context(nc.sbuf_tensor("ones_c_b", [128, 1], BF16))
    env.cpool = es.enter_context(nc.sbuf_tensor("cpool", [128, 16], F32))
    env.stage = es.enter_context(nc.sbuf_tensor("stage", [128, 128], F32))
    env.ones_c_f = es.enter_context(nc.sbuf_tensor("ones_c_f", [128, 1], F32))
    env.stage_free = None
    env.lvc_evac = None
    env.PS = [es.enter_context(nc.psum_tensor(f"ps{i}", [128, 512], F32)) for i in range(7)]
    env.PB = es.enter_context(nc.psum_tensor("pb", [128, 512], BF16))

    env.CONSTS = [LN_EPS, 1.0, 0.5, np.pi / 2, 1e-4, 2.0]
    for j, v in enumerate(env.CONSTS):
        nc.const_aps.register(F32, float(v), env.cpool[:, j:j + 1])

    env.xn = [env.AR[:, i * T:(i + 1) * T] for i in range(8)]
    env.WORK_BASE = 8 * T
    env.ARCOLS = ARCOLS

    emit_prelude(env)
    emit_transpose_in(env)
    for p in parts:
        EMITTERS[p](env)
    emit_transpose_out(env)
    ctx.barrier("end", queues=env.ALLQ)
    ctx.finalize()
    env.es = es
    return nc, env


def work_arena(env):
    return Arena(env.AR, env.WORK_BASE, env.ARCOLS)


def emit_prelude(env):
    ctx = env.ctx
    env.qsp.issue(lambda e: e.dma_start(out=env.id_f[:], in_=env.D["c_eye"][:]))
    tok = env.qwt.issue(lambda e: e.dma_start(out=env.id_b[:], in_=env.D["c_eye"][:]))
    ctx.op("pool", lambda e: e.memset(env.ones_r_f[:], 1.0))
    ctx.op("pool", lambda e: e.memset(env.ones_c_b[:], 1.0))
    ctx.op("pool", lambda e: e.memset(env.ones_c_f[:], 1.0))
    for j, v in enumerate(env.CONSTS):
        ctx.op("pool", lambda e, j=j, v=v: e.memset(env.cpool[:, j:j + 1], float(v)), inc="pool")
    ctx.barrier("prelude", queues=env.ALLQ)


def emit_transpose_in(env):
    ctx = env.ctx
    ar = work_arena(env)
    xin = [ar.f32(1024) for _ in range(2)]
    ob = [ar.f32(128) for _ in range(4)]
    tr_done_for_slot = [None, None]
    ev_done = [None, None]
    st_done = [None] * 4
    for tt in range(env.NT):
        sl = tt % 2
        ctx.wait("sp", tr_done_for_slot[sl])
        ld = env.qsp.issue(lambda e, tt=tt, s=xin[sl]: e.dma_start(
            out=s, in_=env.D["x"][tt * 128:(tt + 1) * 128, :]))
        ctx.wait("pe", ld)
        for dt_ in range(8):
            ob_i = (tt * 8 + dt_) % 4
            pk = env.PS[dt_ % 2]
            ctx.wait("pe", ev_done[dt_ % 2])
            pev = ctx.op("pe", lambda e, s=xin[sl], dt_=dt_, pk=pk: e.transpose(
                pk[:, 0:128], s[:, dt_ * 128:(dt_ + 1) * 128], env.id_f[:]), inc="pe")
            ctx.wait("dve", pev)
            ctx.wait("dve", st_done[ob_i])
            ev_done[dt_ % 2] = ctx.op("dve", lambda e, pk=pk, o=ob[ob_i]: e.tensor_copy(
                o, pk[:, 0:128]), inc="dve")
            ctx.wait("sp", ev_done[dt_ % 2])
            st_done[ob_i] = env.qsp.issue(lambda e, o=ob[ob_i], dt_=dt_, tt=tt: e.dma_start(
                out=env.resid[dt_ * 128:(dt_ + 1) * 128, tt * 128:(tt + 1) * 128], in_=o))
        tr_done_for_slot[sl] = ctx.milestone("pe")
    ctx.barrier("transpose_in", queues=env.ALLQ)


def emit_transpose_out(env):
    ctx = env.ctx
    ar = work_arena(env)
    strip = [ar.f32(env.T) for _ in range(2)]
    ob = [ar.f32(128) for _ in range(4)]
    tr_done_for_slot = [None, None]
    ev_done = [None, None]
    st_done = [None] * 4
    for dt_ in range(8):
        sl = dt_ % 2
        ctx.wait("sp", tr_done_for_slot[sl])
        ld = env.qsp.issue(lambda e, dt_=dt_, s=strip[sl]: e.dma_start(
            out=s, in_=env.resid[dt_ * 128:(dt_ + 1) * 128, :]))
        ctx.wait("pe", ld)
        for tt in range(env.NT):
            ob_i = (dt_ * env.NT + tt) % 4
            pk = env.PS[tt % 2]
            ctx.wait("pe", ev_done[tt % 2])
            pev = ctx.op("pe", lambda e, s=strip[sl], tt=tt, pk=pk: e.transpose(
                pk[:, 0:128], s[:, tt * 128:(tt + 1) * 128], env.id_f[:]), inc="pe")
            ctx.wait("dve", pev)
            ctx.wait("dve", st_done[ob_i])
            ev_done[tt % 2] = ctx.op("dve", lambda e, pk=pk, o=ob[ob_i]: e.tensor_copy(
                o, pk[:, 0:128]), inc="dve")
            ctx.wait("sp", ev_done[tt % 2])
            st_done[ob_i] = env.qsp.issue(lambda e, o=ob[ob_i], dt_=dt_, tt=tt: e.dma_start(
                out=env.out[tt * 128:(tt + 1) * 128, dt_ * 128:(dt_ + 1) * 128], in_=o))
        tr_done_for_slot[sl] = ctx.milestone("pe")
    ctx.barrier("transpose_out", queues=env.ALLQ)


def emit_ln(env, ar, src_bf16_tiles=None, gamma_beta_cols=None, dst_tiles=None,
            to_hbm=None, keep_rows=None):
    ctx, T, TC = env.ctx, env.T, env.TC
    dst = dst_tiles or env.xn
    NH = 2 if T >= 2048 else 1
    HW = T // NH
    CHW = HW // 512
    xsq = [ar.bf(HW) for _ in range(8)]
    bc_mu = ar.f32(HW)
    bc_rs = ar.f32(HW)
    s1 = ar.f32(T)
    s2 = ar.f32(T)
    PS = env.PS

    if src_bf16_tiles is None:
        ld = [env.qwt.issue(lambda e, k=k: e.dma_start(
            out=dst[k], in_=env.resid[k * 128:(k + 1) * 128, :])) for k in range(8)]
        srcs = dst
    else:
        ld = [None] * 8
        srcs = src_bf16_tiles

    ev1 = ev2 = None
    stats_pe = None
    for h in range(NH):
        sqd = [None] * 8
        for k in range(8):
            ctx.wait("act", ld[k])
            ctx.wait("act", stats_pe)
            sqd[k] = ctx.op("act", lambda e, k=k, h=h: e.activation(
                xsq[k], srcs[k][:, h * HW:(h + 1) * HW], AF.Square), inc="act")
        for k in range(8):
            ctx.wait("pe", sqd[k])
            ctx.wait("pe", ld[k])
        ctx.wait("pe", ctx.milestone("pool"))
        for c in range(CHW):
            ctx.wait("pe", ev1)
            ctx.wait("pe", ev2)
            for k in range(8):
                ctx.op("pe", lambda e, s=srcs[k], c=c, k=k, h=h: e.matmul(
                    PS[4][0:1, :], env.ones_c_b[:, 0:1],
                    s[:, h * HW + c * 512:h * HW + (c + 1) * 512],
                    start=(k == 0), stop=(k == 7)), inc="pe" if k == 7 else None)
                mm = ctx.op("pe", lambda e, c=c, k=k: e.matmul(
                    PS[5][0:1, :], env.ones_c_b[:, 0:1], xsq[k][:, c * 512:(c + 1) * 512],
                    start=(k == 0), stop=(k == 7)), inc="pe" if k == 7 else None)
            ctx.wait("dve", mm)
            gc0 = h * HW + c * 512
            ev1 = ctx.op("dve", lambda e, g=gc0: e.tensor_copy(
                s1[0:1, g:g + 512], PS[4][0:1, :]), inc="dve")
            ev2 = ctx.op("dve", lambda e, g=gc0: e.tensor_copy(
                s2[0:1, g:g + 512], PS[5][0:1, :]), inc="dve")
        stats_pe = ctx.milestone("pe")
    ctx.wait("dve", ev1)
    ctx.wait("dve", ev2)
    ctx.op("dve", lambda e: e.tensor_scalar(s1[0:1, 0:T], s1[0:1, 0:T], 1.0 / 1024.0, None, ALU.mult))
    ctx.drain("dve")
    ctx.op("dve", lambda e: e.tensor_scalar(s2[0:1, 0:T], s2[0:1, 0:T], 1.0 / 1024.0, None, ALU.mult))
    ctx.drain("dve")
    vsub = None
    for h in range(NH):
        ctx.op("dve", lambda e, h=h: e.tensor_mul(
            bc_mu[0:1, 0:HW], s1[0:1, h * HW:(h + 1) * HW], s1[0:1, h * HW:(h + 1) * HW]))
        ctx.drain("dve")
        vsub = ctx.op("dve", lambda e, h=h: e.tensor_sub(
            s2[0:1, h * HW:(h + 1) * HW], s2[0:1, h * HW:(h + 1) * HW], bc_mu[0:1, 0:HW]),
            inc="dve")
        ctx.drain("dve")
    if keep_rows is not None:
        ctx.op("dve", lambda e: e.tensor_scalar(
            keep_rows[0][0:1, 0:T], s1[0:1, 0:T], -1.0, None, ALU.mult))
        ctx.drain("dve")
    ctx.wait("act", vsub)
    va = ctx.op("act", lambda e: e.activation(s2[0:1, 0:T], s2[0:1, 0:T], AF.Sqrt, bias=LN_EPS), inc="act")
    ctx.wait("dve", va)
    if keep_rows is not None:
        ctx.op("dve", lambda e: e.tensor_copy(keep_rows[1][0:1, 0:T], s2[0:1, 0:T]))
        ctx.drain("dve")
    vd = ctx.op("dve", lambda e: e.reciprocal(s2[0:1, 0:T], s2[0:1, 0:T]), inc="dve")
    ctx.wait("pe", vd)
    nrm_last = None
    nrm_done = [[None] * 8 for _ in range(NH)]
    st_tok = [None] * 8
    for h in range(NH):
        bce = None
        for c in range(CHW):
            if h > 0 or c > 0:
                ctx.wait("pe", nrm_last)
            ctx.op("pe", lambda e, h=h, c=c: e.matmul(
                PS[4][:, :], env.ones_r_f[0:1, :], s2[0:1, h * HW + c * 512:h * HW + (c + 1) * 512],
                start=True, stop=True), inc="pe")
            mm = ctx.op("pe", lambda e, h=h, c=c: e.matmul(
                PS[5][:, :], env.ones_r_f[0:1, :], s1[0:1, h * HW + c * 512:h * HW + (c + 1) * 512],
                start=True, stop=True), inc="pe")
            ctx.wait("dve", mm)
            ctx.op("dve", lambda e, c=c: e.tensor_copy(bc_rs[:, c * 512:(c + 1) * 512], PS[4][:, :]), inc="dve")
            bce = ctx.op("dve", lambda e, c=c: e.tensor_copy(bc_mu[:, c * 512:(c + 1) * 512], PS[5][:, :]), inc="dve")
            for k in range(8):
                src_k = srcs[k]
                gsp = h * HW + c * 512
                ctx.drain("dve")
                ctx.op("dve", lambda e, s=src_k, c=c, g=gsp: e.tensor_sub(
                    PS[6][:, :], s[:, g:g + 512], bc_mu[:, c * 512:(c + 1) * 512]))
                ctx.drain("dve")
                if gamma_beta_cols is None:
                    nrm_done[h][k] = ctx.op("dve", lambda e, k=k, c=c, g=gsp: e.tensor_mul(
                        dst[k][:, g:g + 512], PS[6][:, :], bc_rs[:, c * 512:(c + 1) * 512]),
                        inc="dve")
                else:
                    gcl, bcl = gamma_beta_cols
                    ctx.op("dve", lambda e, c=c: e.tensor_mul(
                        PS[6][:, :], PS[6][:, :], bc_rs[:, c * 512:(c + 1) * 512]))
                    ctx.drain("dve")
                    nrm_done[h][k] = ctx.op("dve", lambda e, k=k, g=gsp, gg=gcl[k], bb=bcl[k]: e.tensor_scalar(
                        dst[k][:, g:g + 512], PS[6][:, :], gg, bb, ALU.mult, ALU.add),
                        inc="dve")
                ctx.drain("dve")
            nrm_last = ctx.milestone("dve")
    if to_hbm is not None:
        for k in range(8):
            ctx.wait("sp", nrm_done[NH - 1][k])
            ctx.wait("sp", nrm_done[0][k])
            st_tok[k] = env.qsp.issue(lambda e, k=k: e.dma_start(
                out=to_hbm[k * 128:(k + 1) * 128, :], in_=dst[k]))
    ctx.barrier("ln", queues=env.ALLQ)


def load_vec_cols(env, dram_vec, dst_block, n, pk=None):
    ctx = env.ctx
    pk = pk if pk is not None else env.PS[6]
    ctx.wait("sp", env.stage_free)
    ld = env.qsp.issue(lambda e: e.dma_start(
        out=env.stage[0:n, 0:128], in_=dram_vec.rearrange("(j p) -> j p", p=128)))
    ctx.wait("pe", ld)
    ctx.wait("pe", env.lvc_evac)
    pev = ctx.op("pe", lambda e: e.transpose(pk[:, 0:n], env.stage[0:n, 0:128],
                                             env.id_f[0:n, 0:n]), inc="pe")
    env.stage_free = pev
    ctx.wait("dve", pev)
    tok = ctx.op("dve", lambda e: e.tensor_copy(dst_block, pk[:, 0:n]), inc="dve")
    env.lvc_evac = tok
    return tok


def load_cols(env, ctx, dram_vec, dst_sr_cols, n=8):
    toks = []
    for k in range(n):
        toks.append(env.qsp.issue(lambda e, k=k: e.dma_start(
            out=dst_sr_cols[k], in_=dram_vec[k * 128:(k + 1) * 128].rearrange("(p o) -> p o", o=1))))
    return toks


def emit_l1_ffn(env):
    ctx, T, TC = env.ctx, env.T, env.TC
    ar = work_arena(env)
    emit_ln(env, ar)
    ar2 = work_arena(env)
    cbl = [ar2.bf(1024) for _ in range(8)]
    cbt = [ar2.bf(1024) for _ in range(8)]
    outs = [ar2.f32(512) for _ in range(4)]
    scb = [env.SR[:, k:k + 1] for k in range(8)]
    bia = [env.SR[:, 8 + k:9 + k] for k in range(8)]
    ld_cb = []
    for ntile in range(8):
        ld_cb.append(env.qwt.issue(lambda e, n=ntile: e.dma_start(
            out=cbl[n], in_=env.D["L1f_CB"][n * 128:(n + 1) * 128, :])))
    t1 = load_cols(env, ctx, env.D["L1f_SCB"], scb)
    t2 = load_cols(env, ctx, env.D["L1f_bias"], bia)
    for t in t1 + t2:
        ctx.wait("act", t)
    va = ctx.op("act", lambda e: e.activation(
        env.SR[:, 0:8], env.SR[:, 0:8], AF.Copy, scale=1.0 / 127.0), inc="act")
    ev = None
    for ntile in range(8):
        ctx.wait("pe", ld_cb[ntile])
        for k in range(8):
            ctx.wait("pe", ev)
            pev = ctx.op("pe", lambda e, n=ntile, k=k: e.transpose(
                env.PB[:, 0:128], cbl[n][:, k * 128:(k + 1) * 128], env.id_b[:]), inc="pe")
            ctx.wait("dve", pev)
            ev = ctx.op("dve", lambda e, n=ntile, k=k: e.tensor_copy(
                cbt[k][:, n * 128:(n + 1) * 128], env.PB[:, 0:128]), inc="dve")
    ctx.wait("pe", ev)
    ctx.wait("pe", va)
    ctx.wait("dve", va)
    ev_done = [None, None]
    st_done = [None] * 4
    for c in range(TC):
        for ntile in range(8):
            pk = env.PS[(c * 8 + ntile) % 2]
            ctx.wait("pe", ev_done[(c * 8 + ntile) % 2])
            for k in range(8):
                mm = ctx.op("pe", lambda e, n=ntile, k=k, c=c, pk=pk: e.matmul(
                    pk[:, :], cbt[k][:, n * 128:(n + 1) * 128],
                    env.xn[k][:, c * 512:(c + 1) * 512],
                    start=(k == 0), stop=(k == 7)), inc="pe" if k == 7 else None)
            o_i = (c * 8 + ntile) % 4
            ctx.wait("dve", mm)
            ctx.wait("dve", st_done[o_i])
            ev_done[(c * 8 + ntile) % 2] = ctx.op("dve", lambda e, pk=pk, o=outs[o_i], n=ntile: e.tensor_scalar(
                o, pk[:, :], scb[n], bia[n], ALU.mult, ALU.add), inc="dve")
            ctx.wait("pool", ev_done[(c * 8 + ntile) % 2])
            st_done[o_i] = env.qac.issue(lambda e, o=outs[o_i], n=ntile, c=c: e.dma_start(
                out=env.resid[n * 128:(n + 1) * 128, c * 512:(c + 1) * 512], in_=o,
                accum_op=ALU.add))
    ctx.barrier("l1ffn", queues=env.ALLQ)


def emit_l1_attn_a(env):
    ctx, T, TC = env.ctx, env.T, env.TC
    ar = work_arena(env)
    emit_ln(env, ar)
    ar2 = work_arena(env)
    w1 = [ar2.bf(1024) for _ in range(8)]
    w2 = [ar2.bf(1024) for _ in range(8)]
    etl = ar2.f32(1024)
    ett = [ar2.f32(21) for _ in range(8)]
    accD = ar2.f32(512)
    accP = ar2.f32(512)
    h = [ar2.bf(512) for _ in range(8)]
    outs = [ar2.f32(512) for _ in range(4)]
    bia = [env.SR[:, 16 + k:17 + k] for k in range(8)]

    ldw = []
    for k in range(8):
        ldw.append(env.qwt.issue(lambda e, k=k: e.dma_start(
            out=w1[k], in_=env.D["L1a_W1"][k * 128:(k + 1) * 128, :])))
    for k in range(8):
        ldw.append(env.qwt.issue(lambda e, k=k: e.dma_start(
            out=w2[k], in_=env.D["L1a_W2"][k * 128:(k + 1) * 128, :])))
    ldb = load_cols(env, ctx, env.D["L1a_bias"], bia)
    lde = env.qsp.issue(lambda e: e.dma_start(out=etl[0:21, :], in_=env.D["L1a_et"][:, :]))
    ctx.wait("pe", lde)
    ev = None
    for k in range(8):
        pk = env.PS[k % 2]
        ctx.wait("pe", ev)
        pev = ctx.op("pe", lambda e, k=k, pk=pk: e.transpose(
            pk[:, 0:21], etl[0:21, k * 128:(k + 1) * 128], env.id_f[0:21, 0:21]), inc="pe")
        ctx.wait("dve", pev)
        ev = ctx.op("dve", lambda e, k=k, pk=pk: e.tensor_copy(ett[k], pk[:, 0:21]), inc="dve")
    conv_barrier = ev

    mm_prev = None
    ev_done = [None, None]
    st_done = [None] * 4
    h_done = {}
    for c in range(TC):
        for k in range(8):
            eng = "dve"
            acc = accD if k % 2 == 0 else accP
            ctx.wait(eng, conv_barrier)
            ctx.wait(eng, mm_prev)
            ctx.drain(eng)
            ctx.op(eng, lambda e, acc=acc: e.memset(acc, 0.0))
            ctx.drain(eng)
            for j in range(1, 21):
                lo = c * 512 - j
                if lo + 512 <= 0:
                    continue
                srccol = max(0, lo)
                dstcol = c * 512 - j - srccol
                width = 512 - (srccol - lo)
                dst0 = 512 - width
                ctx.op(eng, lambda e, acc=acc, k=k, j=j, srccol=srccol, w=width, d0=dst0:
                       e.scalar_tensor_tensor(acc[:, d0:d0 + w], env.xn[k][:, srccol:srccol + w],
                                              ett[k][:, j:j + 1], acc[:, d0:d0 + w],
                                              ALU.mult, ALU.add))
                ctx.drain(eng)
            h_done[(c, k)] = ctx.op(eng, lambda e, acc=acc, k=k: e.tensor_copy(h[k], acc),
                                    inc=eng)
        for t in ldw:
            ctx.wait("pe", t)
        for k in range(8):
            ctx.wait("pe", h_done[(c, k)])
        for mt in range(8):
            pk = env.PS[(c * 8 + mt) % 2]
            ctx.wait("pe", ev_done[(c * 8 + mt) % 2])
            for k in range(8):
                ctx.op("pe", lambda e, k=k, mt=mt, c=c, pk=pk: e.matmul(
                    pk[:, :], w1[k][:, mt * 128:(mt + 1) * 128],
                    env.xn[k][:, c * 512:(c + 1) * 512],
                    start=(k == 0), stop=False))
            for k in range(8):
                mm = ctx.op("pe", lambda e, k=k, mt=mt, c=c, pk=pk: e.matmul(
                    pk[:, :], w2[k][:, mt * 128:(mt + 1) * 128], h[k],
                    start=False, stop=(k == 7)), inc="pe" if k == 7 else None)
            o_i = (c * 8 + mt) % 4
            ctx.wait("dve", mm)
            ctx.wait("dve", st_done[o_i])
            for t in ldb:
                ctx.wait("dve", t)
            ev_done[(c * 8 + mt) % 2] = ctx.op("dve", lambda e, pk=pk, o=outs[o_i], mt=mt:
                e.tensor_scalar(o, pk[:, :], bia[mt], None, ALU.add), inc="dve")
            ctx.wait("pool", ev_done[(c * 8 + mt) % 2])
            st_done[o_i] = env.qac.issue(lambda e, o=outs[o_i], mt=mt, c=c: e.dma_start(
                out=env.resid[mt * 128:(mt + 1) * 128, c * 512:(c + 1) * 512], in_=o,
                accum_op=ALU.add))
        mm_prev = ctx.milestone("pe")
    ctx.barrier("l1attn_a", queues=env.ALLQ)


def emit_l0_attn_a(env):
    ctx, T, TC, NT = env.ctx, env.T, env.TC, env.NT
    ar = work_arena(env)
    emit_ln(env, ar)
    ar2 = work_arena(env)
    wql = [ar2.bf(1024) for _ in range(8)]
    wqt = [ar2.bf(1024) for _ in range(8)]
    qt = [ar2.bf(512) for _ in range(8)]
    ab = [ar2.bf(512) for _ in range(4)]
    ldw = []
    for ot in range(8):
        ldw.append(env.qwt.issue(lambda e, o=ot: e.dma_start(
            out=wql[o], in_=env.D["L0a_Wq"][o * 128:(o + 1) * 128, :])))
    ev = None
    for ot in range(8):
        ctx.wait("pe", ldw[ot])
        for hc in range(8):
            ctx.wait("pe", ev)
            pev = ctx.op("pe", lambda e, o=ot, hc=hc: e.transpose(
                env.PB[:, 0:128], wql[o][:, hc * 128:(hc + 1) * 128], env.id_b[:]), inc="pe")
            ctx.wait("dve", pev)
            ev = ctx.op("dve", lambda e, o=ot, hc=hc: e.tensor_copy(
                wqt[hc][:, o * 128:(o + 1) * 128], env.PB[:, 0:128]), inc="dve")
    ctx.wait("pe", ev)
    qt_done = [None] * 8
    ev_done = [None, None]
    st_done = [None] * 4
    for c in range(TC):
        for ot in range(8):
            pk = env.PS[ot % 2]
            ctx.wait("pe", qt_done[ot - 2] if ot >= 2 else None)
            for kh in range(8):
                mm = ctx.op("pe", lambda e, o=ot, kh=kh, c=c, pk=pk: e.matmul(
                    pk[:, :], wqt[kh][:, o * 128:(o + 1) * 128],
                    env.xn[kh][:, c * 512:(c + 1) * 512],
                    start=(kh == 0), stop=(kh == 7)), inc="pe" if kh == 7 else None)
            ctx.wait("act", mm)
            qt_done[ot] = ctx.op("act", lambda e, o=ot, pk=pk: e.copy(qt[o], pk[:, :]), inc="act")
        for ot in range(8):
            ctx.wait("pe", qt_done[ot])
        for tt in range(NT):
            pk = env.PS[2 + tt % 2]
            ctx.wait("pe", ev_done[tt % 2])
            for oc in range(8):
                mm = ctx.op("pe", lambda e, oc=oc, tt=tt, pk=pk: e.matmul(
                    pk[:, :], env.xn[oc][:, tt * 128:(tt + 1) * 128], qt[oc],
                    start=(oc == 0), stop=(oc == 7)), inc="pe" if oc == 7 else None)
            b_i = tt % 4
            ctx.wait("dve", mm)
            ctx.wait("dve", st_done[b_i])
            ev_done[tt % 2] = ctx.op("dve", lambda e, pk=pk, b=ab[b_i]: e.tensor_copy(
                b, pk[:, :]), inc="dve")
            ctx.wait("sp", ev_done[tt % 2])
            st_done[b_i] = env.qsp.issue(lambda e, b=ab[b_i], tt=tt, c=c: e.dma_start(
                out=env.s_aT[tt * 128:(tt + 1) * 128, c * 512:(c + 1) * 512], in_=b))
    ctx.barrier("l0a_ab", queues=env.ALLQ)
    ar3 = work_arena(env)
    xtok = [ar3.bf(1024) for _ in range(NT)]
    abuf = [ar3.bf(NT * 256) for _ in range(2)]
    outs = [ar3.f32(512) for _ in range(4)]
    ev = None
    for oc in range(8):
        for tt in range(NT):
            ctx.wait("pe", ev)
            pev = ctx.op("pe", lambda e, oc=oc, tt=tt: e.transpose(
                env.PB[:, 0:128], env.xn[oc][:, tt * 128:(tt + 1) * 128], env.id_b[:]), inc="pe")
            ctx.wait("dve", pev)
            ev = ctx.op("dve", lambda e, oc=oc, tt=tt: e.tensor_copy(
                xtok[tt][:, oc * 128:(oc + 1) * 128], env.PB[:, 0:128]), inc="dve")
    ctx.wait("pe", ev)
    HT = NT // 2
    ld_half = [None, None]
    mm_half_done = [None, None]
    ev_done = [None, None]
    st_done = [None] * 4
    for c in range(TC):
        for half in range(2):
            ctx.wait("sp", mm_half_done[half])
            ld_half[half] = env.qsp.issue(lambda e, c=c, h=half: e.dma_start(
                out=abuf[h].rearrange("p (i s) -> p i s", s=512),
                in_=env.s_aT[h * HT * 128:(h + 1) * HT * 128,
                             c * 512:(c + 1) * 512].rearrange("(i p) s -> p i s", p=128)))
        for ht in range(8):
            pk = env.PS[ht % 2]
            ctx.wait("pe", ev_done[ht % 2])
            for half in range(2):
                ctx.wait("pe", ld_half[half])
                for i in range(HT):
                    tt = half * HT + i
                    mm = ctx.op("pe", lambda e, ht=ht, tt=tt, i=i, h=half, pk=pk: e.matmul(
                        pk[:, :], xtok[tt][:, ht * 128:(ht + 1) * 128],
                        abuf[h][:, i * 512:(i + 1) * 512],
                        start=(tt == 0), stop=(tt == NT - 1)),
                        inc="pe" if tt == NT - 1 else None)
            mm_half_done[0] = mm_half_done[1] = ctx.milestone("pe")
            o_i = ht % 4
            ctx.wait("dve", mm)
            ctx.wait("dve", st_done[o_i])
            ev_done[ht % 2] = ctx.op("dve", lambda e, pk=pk, o=outs[o_i]: e.tensor_copy(
                o, pk[:, :]), inc="dve")
            ctx.wait("pool", ev_done[ht % 2])
            st_done[o_i] = env.qac.issue(lambda e, o=outs[o_i], ht=ht, c=c: e.dma_start(
                out=env.resid[ht * 128:(ht + 1) * 128, c * 512:(c + 1) * 512], in_=o,
                accum_op=ALU.add))
    ctx.barrier("l0a", queues=env.ALLQ)



def emit_l0_ffn(env):
    ctx, T, TC, NT = env.ctx, env.T, env.TC, env.NT
    TCM = T // 256
    PS = env.PS
    AXX = mybir.AxisListType.X
    MGT_COLS = 2 * 8 * NT + 8
    arp = Arena(env.AR, env.ARCOLS - MGT_COLS, env.ARCOLS)
    mgt = arp.f32(8 * NT)
    ar_rows = Arena(env.AR, env.WORK_BASE, env.ARCOLS - MGT_COLS)
    rowsA = ar_rows.f32(T)
    rowsB = ar_rows.f32(T)
    ar_ln = Arena(env.AR, ar_rows.off, env.ARCOLS - MGT_COLS)
    emit_ln(env, ar_ln, keep_rows=(rowsA, rowsB))
    arR = Arena(env.AR, ar_rows.off, env.ARCOLS - MGT_COLS)
    wr = arR.f32(64)
    brrow = arR.f32(8)
    wsumr = arR.f32(8)
    lgc = [arR.f32(512) for _ in range(2)]
    ltok = arR.f32(8 * NT)
    rstdtok = arR.f32(NT)
    emt = arR.f32(8)
    cls = arR.f32(4)
    xfR = [arR.f32(T) for _ in range(2)]
    c_lmax, c_nlm, c_sum, c_gate = (cls[:, j:j + 1] for j in range(4))

    ldwr = env.qsp.issue(lambda e: e.dma_start(
        out=wr.rearrange("p (k e) -> p k e", e=8),
        in_=env.D["L0f_Wr"].rearrange("(k p) e -> p k e", p=128)))
    ldbr = env.qsp.issue(lambda e: e.dma_start(
        out=brrow[0:1, 0:8], in_=env.D["L0f_br"].rearrange("(o e) -> o e", o=1)))
    ctx.wait("pe", ldwr)
    for k in range(8):
        mm = ctx.op("pe", lambda e, k=k: e.matmul(
            PS[5][0:1, 0:8], env.ones_c_f[:, 0:1], wr[:, k * 8:(k + 1) * 8],
            start=(k == 0), stop=(k == 7)), inc="pe" if k == 7 else None)
    ctx.wait("dve", mm)
    vw = ctx.op("dve", lambda e: e.tensor_copy(wsumr[0:1, 0:8], PS[5][0:1, 0:8]), inc="dve")
    rtr = None
    for tt in range(NT):
        ctx.wait("pe", rtr)
        pev = ctx.op("pe", lambda e, tt=tt: e.transpose(
            PS[6][:, 0:1], rowsB[0:1, tt * 128:(tt + 1) * 128], env.id_f[0:1, 0:1]), inc="pe")
        ctx.wait("dve", pev)
        rtr = ctx.op("dve", lambda e, tt=tt: e.tensor_copy(
            rstdtok[:, tt:tt + 1], PS[6][:, 0:1]), inc="dve")
    ctx.wait("dve", rtr)
    rtr = ctx.op("dve", lambda e: e.reciprocal(rstdtok[:, 0:NT], rstdtok[:, 0:NT]), inc="dve")
    import os as _os
    NH = 2 if TC >= int(_os.environ.get("RNH", "4")) else 1
    CH = TC // NH
    HW = T // NH
    ctx.wait("pe", vw)
    ctx.wait("pe", ldbr)
    slot_pe = [None, None]
    lg_evac = {}
    tok_ev = None
    for h in range(NH):
        ld = [None, None]
        for k in range(8):
            sl = k % 2
            ctx.wait("sp", slot_pe[sl])
            ld[sl] = env.qsp.issue(lambda e, k=k, h=h, s=xfR[sl]: e.dma_start(
                out=s[:, 0:HW], in_=env.resid[k * 128:(k + 1) * 128, h * HW:(h + 1) * HW]))
            ctx.wait("pe", ld[sl])
            if k == 0:
                for cl in range(CH):
                    ctx.wait("pe", lg_evac.get(cl))
            for cl in range(CH):
                mm = ctx.op("pe", lambda e, k=k, cl=cl, s=xfR[sl]: e.matmul(
                    PS[cl][0:8, :], wr[:, k * 8:(k + 1) * 8], s[:, cl * 512:(cl + 1) * 512],
                    start=(k == 0), stop=False), inc="pe" if cl == CH - 1 else None)
            slot_pe[sl] = mm
        for cl in range(CH):
            span0 = h * HW + cl * 512
            ctx.op("pe", lambda e, cl=cl, s0=span0: e.matmul(
                PS[cl][0:8, :], wsumr[0:1, 0:8], rowsA[0:1, s0:s0 + 512],
                start=False, stop=False))
            mm = ctx.op("pe", lambda e, cl=cl, s0=span0: e.matmul(
                PS[cl][0:8, :], brrow[0:1, 0:8], rowsB[0:1, s0:s0 + 512],
                start=False, stop=True), inc="pe")
            ctx.wait("dve", mm)
            vlg = ctx.op("dve", lambda e, cl=cl, b=lgc[cl % 2]: e.tensor_copy(
                b[0:8, :], PS[cl][0:8, :]), inc="dve")
            lg_evac[cl] = vlg
            for j in range(4):
                tt = (span0 // 128) + j
                ctx.wait("pe", vlg)
                ctx.wait("pe", tok_ev)
                pev = ctx.op("pe", lambda e, j=j, b=lgc[cl % 2]: e.transpose(
                    PS[4][:, 0:8], b[0:8, j * 128:(j + 1) * 128], env.id_f[0:8, 0:8]), inc="pe")
                ctx.wait("dve", pev)
                tok_ev = ctx.op("dve", lambda e, tt=tt: e.tensor_copy(
                    ltok[:, tt * 8:(tt + 1) * 8], PS[4][:, 0:8]), inc="dve")
    ctx.wait("dve", rtr)
    ctx.wait("dve", tok_ev)
    mg_done = None
    for tt in range(NT):
        lt = ltok[:, tt * 8:(tt + 1) * 8]
        ctx.drain("dve")
        ctx.op("dve", lambda e, lt=lt, tt=tt: e.tensor_scalar(
            lt, lt, rstdtok[:, tt:tt + 1], None, ALU.mult))
        ctx.drain("dve")
        ctx.op("dve", lambda e, lt=lt: e.reduce_max(c_lmax, lt, AXX))
        ctx.drain("dve")
        va = ctx.op("dve", lambda e: e.tensor_scalar(c_nlm, c_lmax, -1.0, None, ALU.mult),
                    inc="dve")
        ctx.wait("act", va)
        vex = ctx.op("act", lambda e, lt=lt: e.activation(
            emt[:, 0:8], lt, AF.Exp, bias=c_nlm), inc="act")
        ctx.wait("dve", vex)
        ctx.op("dve", lambda e: e.reduce_sum(c_sum, emt[:, 0:8], AXX))
        ctx.drain("dve")
        ctx.op("dve", lambda e: e.reciprocal(c_gate, c_sum))
        ctx.drain("dve")
        mg_done = ctx.op("dve", lambda e, lt=lt, tt=tt: e.tensor_scalar(
            mgt[:, tt * 8:(tt + 1) * 8], lt, c_lmax, c_gate, ALU.is_equal, ALU.mult),
            inc="dve")
    ctx.barrier("l0f_router", queues=env.ALLQ)
    gtok = load_vec_cols(env, env.D["L0f_g"], env.SR[:, 24:32], 8)
    btok = load_vec_cols(env, env.D["L0f_b"], env.SR[:, 32:40], 8)
    ctx.wait("dve", gtok)
    ctx.wait("dve", btok)
    gcols = [env.SR[:, 24 + k:25 + k] for k in range(8)]
    bcols = [env.SR[:, 32 + k:33 + k] for k in range(8)]
    ar_I = Arena(env.AR, ar_rows.off, env.ARCOLS - MGT_COLS)
    emit_ln(env, ar_I, src_bf16_tiles=env.xn, gamma_beta_cols=(gcols, bcols),
            dst_tiles=env.xn, to_hbm=env.xn_hbm)
    ar_E = Arena(env.AR, 0, env.ARCOLS - MGT_COLS)
    w1 = [ar_E.bf(4096) for _ in range(8)]
    w2 = [ar_E.bf(1024) for _ in range(32)]
    hT = [ar_E.bf(256) for _ in range(32)]
    xinA = [ar_E.bf(2048) for _ in range(2)]
    outs = [ar_E.f32(256) for _ in range(4)]
    mrow = ar_E.f32(T)
    mbc = [ar_E.f32(256) for _ in range(2)]
    b1c = [env.SR[:, 48 + j:49 + j] for j in range(32)]
    b2c = [env.SR[:, 40 + j:41 + j] for j in range(8)]
    h_ev = [None, None]
    y_ev = [None, None]
    st_done = [None] * 4
    mb_ev = [None, None]
    xin_pe = [None, None]
    mrow_done = None
    w_pe = None
    bias_users = None
    for ex in range(8):
        ldw1, ldw2 = [], []
        if ex > 0:
            ctx.wait("pool", w_pe)
            ctx.wait("sp", bias_users)
        for kd in range(8):
            ldw1.append(env.qwt.issue(lambda e, ex=ex, kd=kd: e.dma_start(
                out=w1[kd], in_=env.D["L0f_W1"][ex, kd * 128:(kd + 1) * 128, :])))
        for ft in range(32):
            ldw2.append(env.qwt.issue(lambda e, ex=ex, ft=ft: e.dma_start(
                out=w2[ft], in_=env.D["L0f_W2"][ex, ft * 128:(ft + 1) * 128, :])))
        t1 = load_vec_cols(env, env.D["L0f_b1"][ex, :], env.SR[:, 48:80], 32, pk=PS[6])
        t2 = load_vec_cols(env, env.D["L0f_b2"][ex, :], env.SR[:, 40:48], 8, pk=PS[6])
        ctx.wait("pe", mg_done)
        for tt in range(NT):
            ctx.wait("pe", mrow_done)
            pev = ctx.op("pe", lambda e, tt=tt, ex=ex: e.transpose(
                PS[5][0:1, 0:128], mgt[:, tt * 8 + ex:tt * 8 + ex + 1], env.id_f[:, :]), inc="pe")
            ctx.wait("dve", pev)
            mrow_done = ctx.op("dve", lambda e, tt=tt: e.tensor_copy(
                mrow[0:1, tt * 128:(tt + 1) * 128], PS[5][0:1, 0:128]), inc="dve")
        for t in ldw1 + ldw2:
            ctx.wait("pe", t)
        ctx.wait("act", t1)
        ctx.wait("dve", t2)
        ctx.wait("pe", mrow_done)
        for tc in range(TCM):
            sl = tc % 2
            ctx.wait("sp", xin_pe[sl])
            ldx = env.qsp.issue(lambda e, tc=tc, sl=sl: e.dma_start(
                out=xinA[sl].rearrange("p (k t) -> p k t", t=256),
                in_=env.xn_hbm[:, tc * 256:(tc + 1) * 256].rearrange("(k p) t -> p k t", p=128)))
            ctx.wait("pe", ldx)
            ctx.wait("pe", mb_ev[sl])
            mmb = ctx.op("pe", lambda e, tc=tc: e.matmul(
                PS[4][:, 0:256], env.ones_r_f[0:1, :], mrow[0:1, tc * 256:(tc + 1) * 256],
                start=True, stop=True), inc="pe")
            ctx.wait("dve", mmb)
            mb_ev[sl] = ctx.op("dve", lambda e, sl=sl: e.tensor_copy(
                mbc[sl], PS[4][:, 0:256]), inc="dve")
            for ft in range(32):
                pk = PS[ft % 2]
                ctx.wait("pe", h_ev[ft % 2])
                for kd in range(8):
                    mm = ctx.op("pe", lambda e, ft=ft, kd=kd, sl=sl, pk=pk: e.matmul(
                        pk[:, 0:256], w1[kd][:, ft * 128:(ft + 1) * 128],
                        xinA[sl][:, kd * 256:(kd + 1) * 256],
                        start=(kd == 0), stop=(kd == 7)), inc="pe" if kd == 7 else None)
                ctx.wait("act", mm)
                h_ev[ft % 2] = ctx.op("act", lambda e, ft=ft, pk=pk: e.activation(
                    hT[ft], pk[:, 0:256], AF.Relu, bias=b1c[ft]), inc="act")
            for ft in range(32):
                ctx.wait("pe", h_ev[ft % 2])
            ctx.wait("dve", mb_ev[sl])
            for dt_ in range(8):
                pk = PS[2 + dt_ % 2]
                ctx.wait("pe", y_ev[dt_ % 2])
                for ft in range(32):
                    mm = ctx.op("pe", lambda e, ft=ft, dt_=dt_, pk=pk: e.matmul(
                        pk[:, 0:256], w2[ft][:, dt_ * 128:(dt_ + 1) * 128], hT[ft],
                        start=(ft == 0), stop=(ft == 31)), inc="pe" if ft == 31 else None)
                o_i = (tc * 8 + dt_) % 4
                ctx.wait("dve", mm)
                ctx.wait("dve", st_done[o_i])
                y_ev[dt_ % 2] = ctx.op("dve", lambda e, pk=pk, o=outs[o_i], dt_=dt_, sl=sl:
                    e.scalar_tensor_tensor(o, pk[:, 0:256], b2c[dt_], mbc[sl],
                                           ALU.add, ALU.mult), inc="dve")
                ctx.wait("pool", y_ev[dt_ % 2])
                st_done[o_i] = env.qac.issue(lambda e, o=outs[o_i], dt_=dt_, tc=tc: e.dma_start(
                    out=env.resid[dt_ * 128:(dt_ + 1) * 128, tc * 256:(tc + 1) * 256], in_=o,
                    accum_op=ALU.add))
            xin_pe[sl] = ctx.milestone("pe")
        w_pe = ctx.milestone("pe")
        bias_users = ctx.milestone("dve")
    ctx.barrier("l0f", queues=env.ALLQ)



def emit_l0_attn_b(env):
    ctx, T, TC, NT = env.ctx, env.T, env.TC, env.NT
    PS = env.PS
    AXX = mybir.AxisListType.X
    ar = work_arena(env)
    emit_ln(env, ar)
    arP = work_arena(env)
    cosb = arP.f32(T)
    sinb = arP.f32(T)
    flipb = arP.bf(128)
    wq = [arP.bf(128) for _ in range(8)]
    wk = [arP.bf(128) for _ in range(8)]
    wv = [arP.bf(128) for _ in range(8)]
    qb = arP.bf(512)
    t1 = arP.f32(512)
    t2 = arP.f32(512)
    rot = [arP.bf(512) for _ in range(2)]
    bqc = env.SR[:, 0:8]
    bkc = env.SR[:, 8:16]
    bvc = env.SR[:, 16:24]
    lc1 = env.qsp.issue(lambda e: e.dma_start(out=cosb, in_=env.D["c_rope_cos"][:, :]))
    lc2 = env.qsp.issue(lambda e: e.dma_start(out=sinb, in_=env.D["c_rope_sin"][:, :]))
    lc3 = env.qwt.issue(lambda e: e.dma_start(out=flipb, in_=env.D["c_ropeflip"][:, :]))
    tb1 = load_vec_cols(env, env.D["L0b_bQ"].rearrange("h d -> (h d)"), bqc, 8)
    tb2 = load_vec_cols(env, env.D["L0b_bK"].rearrange("h d -> (h d)"), bkc, 8)
    tb3 = load_vec_cols(env, env.D["L0b_bV"].rearrange("h d -> (h d)"), bvc, 8)
    ctx.wait("dve", lc1)
    ctx.wait("dve", lc2)
    ctx.wait("pe", lc3)
    w_pe = None
    st_tok = [None, None]
    pk_free = [None, None]
    for hp in range(8):
        ldq, ldk, ldv = [], [], []
        ctx.wait("pool", w_pe)
        for kd in range(8):
            for wt, nm, lst in ((wq, "L0b_WQ", ldq), (wk, "L0b_WK", ldk), (wv, "L0b_WV", ldv)):
                for j in range(2):
                    lst.append(env.qwt.issue(lambda e, wt=wt, nm=nm, kd=kd, j=j, hp=hp: e.dma_start(
                        out=wt[kd][:, j * 64:(j + 1) * 64],
                        in_=env.D[nm][2 * hp + j, kd * 128:(kd + 1) * 128, :])))
        for t in ldq + ldk + ldv:
            ctx.wait("pe", t)
        ctx.wait("dve", tb1); ctx.wait("dve", tb2); ctx.wait("dve", tb3)
        for tc in range(TC):
            for mat, wt, bc, dst, dorope in (
                    (0, wq, bqc, env.s_qT, True), (1, wk, bkc, env.s_kT, True),
                    (2, wv, bvc, env.s_vT, False)):
                pk = PS[mat % 2]
                ctx.wait("pe", pk_free[mat % 2])
                for kd in range(8):
                    mm = ctx.op("pe", lambda e, wt=wt, kd=kd, tc=tc, pk=pk: e.matmul(
                        pk[:, :], wt[kd], env.xn[kd][:, tc * 512:(tc + 1) * 512],
                        start=(kd == 0), stop=(kd == 7)), inc="pe" if kd == 7 else None)
                ctx.wait("dve", mm)
                ctx.drain("dve")
                va = ctx.op("dve", lambda e, pk=pk, bc=bc, hp=hp: e.tensor_scalar(
                    qb, pk[:, :], bc[:, hp:hp + 1], None, ALU.add), inc="dve")
                pk_free[mat % 2] = va
                if dorope:
                    pf = PS[4]
                    ctx.wait("pe", va)
                    mmf = ctx.op("pe", lambda e, pf=pf: e.matmul(
                        pf[:, :], flipb, qb, start=True, stop=True), inc="pe")
                    ctx.wait("dve", va)
                    ctx.wait("dve", mmf)
                    ctx.op("dve", lambda e, tc=tc: e.tensor_mul(
                        t1, qb, cosb[:, tc * 512:(tc + 1) * 512]))
                    ctx.drain("dve")
                    ctx.op("dve", lambda e, tc=tc, pf=pf: e.tensor_mul(
                        t2, pf[:, :], sinb[:, tc * 512:(tc + 1) * 512]))
                    ctx.drain("dve")
                    ctx.wait("dve", st_tok[mat % 2])
                    vr = ctx.op("dve", lambda e, mat=mat: e.tensor_add(rot[mat % 2], t1, t2), inc="dve")
                    src = rot[mat % 2]
                    ctx.wait("sp", vr)
                else:
                    ctx.wait("dve", va)
                    ctx.wait("dve", st_tok[0])
                    vr = ctx.op("dve", lambda e: e.tensor_copy(rot[0], qb), inc="dve")
                    src = rot[0]
                    ctx.wait("sp", vr)
                st_tok[mat % 2 if dorope else 0] = env.qsp.issue(
                    lambda e, src=src, dst=dst, hp=hp, tc=tc: e.dma_start(
                        out=dst[hp * 128:(hp + 1) * 128, tc * 512:(tc + 1) * 512], in_=src))
        w_pe = ctx.milestone("pe")
    ctx.barrier("l0b_proj", queues=env.ALLQ)
    arA = work_arena(env)
    ka = arA.bf(T)
    va_ = arA.bf(T)
    qa = arA.bf(T)
    vtok = [arA.bf(128) for _ in range(NT)]
    pbuf = [arA.bf(512) for _ in range(4)]
    nqmax = [arA.f32(T), arA.f32(T)]
    sumr = [arA.f32(512), arA.f32(512)]
    rbc = arA.f32(512)
    chalf = [arA.f32(128), arA.f32(128)]
    zev = arA.bf(512)
    cmq = arA.f32(4 * 512)
    cmk = arA.f32(4 * 512)
    ctx.op("pool", lambda e: e.memset(chalf[0][0:1, 0:64], 1.0))
    ctx.op("pool", lambda e: e.memset(chalf[0][0:1, 64:128], 0.0))
    ctx.op("pool", lambda e: e.memset(chalf[1][0:1, 0:64], 0.0))
    vhp = ctx.op("pool", lambda e: e.memset(chalf[1][0:1, 64:128], 1.0), inc="pool")
    lm1 = env.qsp.issue(lambda e: e.dma_start(
        out=cmq.rearrange("p (m s) -> p m s", s=512),
        in_=env.D["c_cmask_qk"].rearrange("(m p) s -> p m s", p=128)))
    lm2 = env.qsp.issue(lambda e: e.dma_start(
        out=cmk.rearrange("p (m s) -> p m s", s=512),
        in_=env.D["c_cmask_kq"].rearrange("(m p) s -> p m s", p=128)))
    ctx.wait("dve", lm1)
    ctx.wait("dve", lm2)
    zst = None
    zst_ev = None
    sum_free = [None, None]
    rbc_ev = None
    nq_ev = None
    stats_bank = [None, None]
    arena_pe = None
    arena_dve = None
    for hp in range(8):
        ctx.wait("sp", arena_pe)
        ctx.wait("sp", zst)
        l1 = env.qsp.issue(lambda e, hp=hp: e.dma_start(out=ka, in_=env.s_kT[hp * 128:(hp + 1) * 128, :]))
        l2 = env.qsp.issue(lambda e, hp=hp: e.dma_start(out=va_, in_=env.s_vT[hp * 128:(hp + 1) * 128, :]))
        l3 = env.qsp.issue(lambda e, hp=hp: e.dma_start(out=qa, in_=env.s_qT[hp * 128:(hp + 1) * 128, :]))
        ctx.wait("pe", l2)
        ev = None
        for tt in range(NT):
            ctx.wait("pe", ev)
            pev = ctx.op("pe", lambda e, tt=tt: e.transpose(
                env.PB[:, 0:128], va_[:, tt * 128:(tt + 1) * 128], env.id_b[:]), inc="pe")
            ctx.wait("dve", pev)
            ctx.wait("dve", arena_pe)
            ev = ctx.op("dve", lambda e, tt=tt: e.tensor_copy(vtok[tt], env.PB[:, 0:128]), inc="dve")
        vtok_ev = ev
        ctx.wait("pe", l1)
        ctx.wait("pe", l3)
        def emit_stats(qt, hh):
            nonlocal nq_ev, stats_bank
            hr = slice(hh * 64, (hh + 1) * 64)
            nkc = qt // 4 + 1
            for kc in range(nkc):
                pk = PS[3 + (kc % 2)]
                ctx.wait("pe", stats_bank[kc % 2])
                mm = ctx.op("pe", lambda e, hr=hr, qt=qt, kc=kc, pk=pk: e.matmul(
                    pk[:, :], qa[hr, qt * 128:(qt + 1) * 128], ka[hr, kc * 512:(kc + 1) * 512],
                    start=True, stop=True), inc="pe")
                ctx.wait("dve", mm)
                if kc == nkc - 1:
                    m = qt - 4 * kc
                    ctx.op("dve", lambda e, pk=pk, m=m: e.tensor_add(
                        pk[:, :], pk[:, :], cmq[:, m * 512:(m + 1) * 512]))
                    ctx.drain("dve")
                ctx.drain("dve")
                if kc == 0:
                    vm = ctx.op("dve", lambda e, pk=pk: e.reduce_max(
                        env.SR[:, 30:31], pk[:, :], AXX), inc="dve")
                else:
                    ctx.op("dve", lambda e, pk=pk: e.reduce_max(
                        env.SR[:, 31:32], pk[:, :], AXX))
                    ctx.drain("dve")
                    vm = ctx.op("dve", lambda e: e.tensor_max(
                        env.SR[:, 30:31], env.SR[:, 30:31], env.SR[:, 31:32]), inc="dve")
                stats_bank[kc % 2] = vm
            ctx.drain("dve")
            vneg = ctx.op("dve", lambda e: e.tensor_scalar(
                env.SR[:, 30:31], env.SR[:, 30:31], -1.0, None, ALU.mult), inc="dve")
            ctx.wait("pe", vneg)
            ctx.wait("pe", nq_ev)
            ptr = ctx.op("pe", lambda e: e.transpose(
                PS[3][0:1, 0:128], env.SR[:, 30:31], env.id_f[:, :]), inc="pe")
            ctx.wait("dve", ptr)
            nq_ev = ctx.op("dve", lambda e, hh=hh, qt=qt: e.tensor_copy(
                nqmax[hh][0:1, qt * 128:(qt + 1) * 128], PS[3][0:1, 0:128]), inc="dve")
            stats_bank[0] = nq_ev

        ctx.wait("pe", vtok_ev)
        ctx.wait("pe", vhp)
        for qt in range(4):
            emit_stats(qt, 0)
            emit_stats(qt, 1)
        for qc in range(TC):
            nkt = 4 * qc + 4
            pz = PS[2]
            ctx.wait("pe", zst_ev)
            ctx.wait("pe", nq_ev)
            score_free = [None, None]
            pbuf_free = [None] * 4
            pv_last = None
            stats_iter = [(qt, hh) for qt in range(4 * (qc + 1), min(4 * (qc + 2), NT))
                          for hh in range(2)]
            si = 0
            for hh in range(2):
                hr = slice(hh * 64, (hh + 1) * 64)
                psum_s = PS[5] if hh == 0 else PS[6]
                ctx.wait("pe", sum_free[hh])
                for kt in range(nkt):
                    pk = PS[kt % 2]
                    ctx.wait("pe", score_free[kt % 2])
                    ctx.op("pe", lambda e, hr=hr, kt=kt, qc=qc, pk=pk: e.matmul(
                        pk[:, :], ka[hr, kt * 128:(kt + 1) * 128], qa[hr, qc * 512:(qc + 1) * 512],
                        start=True, stop=False))
                    mm = ctx.op("pe", lambda e, hh=hh, qc=qc, pk=pk: e.matmul(
                        pk[:, :], env.ones_r_f[0:1, :],
                        nqmax[hh][0:1, qc * 512:(qc + 1) * 512],
                        start=False, stop=True), inc="pe")
                    ctx.wait("act", mm)
                    m = kt - 4 * qc
                    if 0 <= m < 4:
                        ctx.wait("dve", mm)
                        vmadd = ctx.op("dve", lambda e, pk=pk, m=m: e.tensor_add(
                            pk[:, :], pk[:, :], cmk[:, m * 512:(m + 1) * 512]), inc="dve")
                        ctx.wait("act", vmadd)
                    ctx.wait("act", pbuf_free[kt % 4])
                    vex = ctx.op("act", lambda e, pk=pk, b=pbuf[kt % 4]: e.activation(
                        b, pk[:, :], AF.Exp, scale=0.125), inc="act")
                    score_free[kt % 2] = vex
                    ctx.wait("pe", vex)
                    ctx.op("pe", lambda e, kt=kt, hh=hh, b=pbuf[kt % 4], pz=pz, nkt=nkt: e.matmul(
                        pz[hh * 64:(hh + 1) * 64, :], vtok[kt][:, hh * 64:(hh + 1) * 64], b,
                        start=(kt == 0), stop=(kt == nkt - 1),
                        tile_position=(0, 64) if hh else None))
                    mms = ctx.op("pe", lambda e, kt=kt, b=pbuf[kt % 4], ps_=psum_s, nkt=nkt: e.matmul(
                        ps_[0:1, :], env.ones_c_b[:, 0:1], b,
                        start=(kt == 0), stop=(kt == nkt - 1)), inc="pe")
                    pbuf_free[kt % 4] = mms
                    if si < len(stats_iter) and kt % 3 == 2:
                        emit_stats(*stats_iter[si])
                        si += 1
                pv_last = mms
                ctx.wait("dve", mms)
                ctx.drain("dve")
                ctx.op("dve", lambda e, hh=hh, ps_=psum_s: e.tensor_copy(
                    sumr[hh][0:1, :], ps_[0:1, :]))
                ctx.drain("dve")
                vrc = ctx.op("dve", lambda e, hh=hh: e.reciprocal(
                    sumr[hh][0:1, :], sumr[hh][0:1, :]), inc="dve")
                sum_free[hh] = vrc
            while si < len(stats_iter):
                emit_stats(*stats_iter[si])
                si += 1
            ctx.wait("pe", vrc)
            ctx.wait("pe", rbc_ev)
            ctx.op("pe", lambda e: e.matmul(
                PS[5][:, :], chalf[0][0:1, :], sumr[0][0:1, :], start=True, stop=False))
            mmb = ctx.op("pe", lambda e: e.matmul(
                PS[5][:, :], chalf[1][0:1, :], sumr[1][0:1, :], start=False, stop=True), inc="pe")
            ctx.wait("dve", mmb)
            ctx.wait("dve", pv_last)
            vb = ctx.op("dve", lambda e: e.tensor_copy(rbc, PS[5][:, :]), inc="dve")
            rbc_ev = vb
            ctx.drain("dve")
            ctx.wait("dve", zst)
            vz = ctx.op("dve", lambda e, pz=pz: e.tensor_mul(zev, pz[:, :], rbc), inc="dve")
            zst_ev = vz
            ctx.wait("sp", vz)
            zst = env.qsp.issue(lambda e, hp=hp, qc=qc: e.dma_start(
                out=env.s_qT[hp * 128:(hp + 1) * 128, qc * 512:(qc + 1) * 512], in_=zev))
        arena_pe = ctx.milestone("pe")
        arena_dve = ctx.milestone("dve")
    ctx.barrier("l0b_attn", queues=env.ALLQ)
    arO = work_arena(env)
    wo = [arO.bf(1024) for _ in range(8)]
    zbuf = [arO.bf(4096) for _ in range(2)]
    outs = [arO.f32(512) for _ in range(4)]
    boc = env.SR[:, 0:8]
    ldo = []
    for hp in range(8):
        for j in range(2):
            ldo.append(env.qwt.issue(lambda e, hp=hp, j=j: e.dma_start(
                out=wo[hp][j * 64:(j + 1) * 64, :], in_=env.D["L0b_WO"][2 * hp + j, :, :])))
    tbo = load_vec_cols(env, env.D["L0b_bO"], boc, 8)
    for t in ldo:
        ctx.wait("pe", t)
    zld = [None, None]
    ev_done = [None, None]
    st_done = [None] * 4
    for qc in range(TC):
        sl = qc % 2
        ctx.wait("sp", zld[sl])
        zl = env.qsp.issue(lambda e, qc=qc, sl=sl: e.dma_start(
            out=zbuf[sl].rearrange("p (hp q) -> p hp q", q=512),
            in_=env.s_qT[:, qc * 512:(qc + 1) * 512].rearrange("(hp p) q -> p hp q", p=128)))
        ctx.wait("pe", zl)
        for mt in range(8):
            pk = PS[mt % 2]
            ctx.wait("pe", ev_done[mt % 2])
            for hp in range(8):
                mm = ctx.op("pe", lambda e, hp=hp, mt=mt, sl=sl, pk=pk: e.matmul(
                    pk[:, :], wo[hp][:, mt * 128:(mt + 1) * 128],
                    zbuf[sl][:, hp * 512:(hp + 1) * 512],
                    start=(hp == 0), stop=(hp == 7)), inc="pe" if hp == 7 else None)
            o_i = (qc * 8 + mt) % 4
            ctx.wait("dve", mm)
            ctx.wait("dve", st_done[o_i])
            ctx.wait("dve", tbo)
            ev_done[mt % 2] = ctx.op("dve", lambda e, pk=pk, o=outs[o_i], mt=mt: e.tensor_scalar(
                o, pk[:, :], boc[:, mt:mt + 1], None, ALU.add), inc="dve")
            ctx.wait("pool", ev_done[mt % 2])
            st_done[o_i] = env.qac.issue(lambda e, o=outs[o_i], mt=mt, qc=qc: e.dma_start(
                out=env.resid[mt * 128:(mt + 1) * 128, qc * 512:(qc + 1) * 512], in_=o,
                accum_op=ALU.add))
        zld[sl] = ctx.milestone("pe")
    ctx.barrier("l0b", queues=env.ALLQ)



EMITTERS = {
    "l1f1": emit_l1_ffn,
    "l1f2": emit_l1_ffn,
    "l1a": emit_l1_attn_a,
    "l0a": emit_l0_attn_a,
    "l0f1": emit_l0_ffn,
    "l0f2": emit_l0_ffn,
    "l0b": emit_l0_attn_b,
}


EMITTERS["l1b"] = lambda env: emit_l1_attn_b(env)


def emit_l1_attn_b(env):
    ctx, T, TC, NT = env.ctx, env.T, env.TC, env.NT
    PS = env.PS
    ar0 = work_arena(env)
    rowsA = ar0.f32(T)
    rowsB = ar0.f32(T)
    ar_ln = Arena(env.AR, ar0.off, env.ARCOLS)
    emit_ln(env, ar_ln, keep_rows=(rowsA, rowsB))

    ar1 = Arena(env.AR, ar0.off, env.ARCOLS)
    wl = [ar1.bf(1024) for _ in range(2)]
    wzx = [ar1.bf(4096) for _ in range(8)]
    oev = [ar1.bf(512) for _ in range(4)]
    ld = [None, None]
    tr_ev = None
    for rt in range(32):
        sl = rt % 2
        ctx.wait("pool", tr_ev)
        ld[sl] = env.qwt.issue(lambda e, rt=rt, s=wl[sl]: e.dma_start(
            out=s, in_=env.D["L1b_Win"][rt * 128:(rt + 1) * 128, :]))
        ctx.wait("pe", ld[sl])
        for kd in range(8):
            ctx.wait("pe", tr_ev)
            pev = ctx.op("pe", lambda e, s=wl[sl], kd=kd: e.transpose(
                env.PB[:, 0:128], s[:, kd * 128:(kd + 1) * 128], env.id_b[:]), inc="pe")
            ctx.wait("dve", pev)
            tr_ev = ctx.op("dve", lambda e, kd=kd, rt=rt: e.tensor_copy(
                wzx[kd][:, rt * 128:(rt + 1) * 128], env.PB[:, 0:128]), inc="dve")
    ctx.wait("pe", tr_ev)
    ev_done = [None, None]
    st_done = [None] * 4
    for tt in range(NT):
        for rc in range(8):
            pk = PS[rc % 2]
            ctx.wait("pe", ev_done[rc % 2])
            for kd in range(8):
                mm = ctx.op("pe", lambda e, tt=tt, rc=rc, kd=kd, pk=pk: e.matmul(
                    pk[:, :], env.xn[kd][:, tt * 128:(tt + 1) * 128],
                    wzx[kd][:, rc * 512:(rc + 1) * 512],
                    start=(kd == 0), stop=(kd == 7)), inc="pe" if kd == 7 else None)
            o_i = (tt * 8 + rc) % 4
            ctx.wait("dve", mm)
            ctx.wait("dve", st_done[o_i])
            ev_done[rc % 2] = ctx.op("dve", lambda e, pk=pk, o=oev[o_i]: e.tensor_copy(
                o, pk[:, :]), inc="dve")
            ctx.wait("sp", ev_done[rc % 2])
            dst = env.s_ztok if rc < 4 else env.s_xtok
            c0 = (rc % 4) * 512
            st_done[o_i] = env.qsp.issue(lambda e, o=oev[o_i], tt=tt, c0=c0, dst=dst: e.dma_start(
                out=dst[tt * 128:(tt + 1) * 128, c0:c0 + 512], in_=o))
    ctx.barrier("l1b_s1a", queues=env.ALLQ)

    ar2 = Arena(env.AR, ar0.off, env.ARCOLS)
    wsc_l = ar2.f32(1024)
    wsc = [ar2.f32(448) for _ in range(8)]
    sevac = [ar2.f32(512) for _ in range(2)]
    bcm = ar2.f32(512)
    bcr = ar2.f32(512)
    rr = ar2.f32(512)
    xf32 = [ar2.f32(512) for _ in range(2)]
    blocks = [(4096, 64, 0, 2), (4097, 63, 32, 2), (4160, 64, 64, 1),
              (4224, 64, 128, 2), (4225, 63, 160, 2), (4288, 64, 192, 1),
              (4448, 32, 256, 1), (4448, 32, 288, 1), (4352, 32, 320, 1),
              (4384, 32, 352, 1), (4416, 32, 384, 1)]
    ld_st = None
    tr_ev = None
    for (r0, span, dcol, step) in blocks:
        ctx.wait("sp", tr_ev)
        if step == 2:
            n = 32
            src = env.D["L1b_Win"][r0:r0 + span, :].rearrange(
                "(d two) k -> d two k", two=2)[:, 0] if span % 2 == 0 else None
            if src is None:
                src = env.D["L1b_Win"][r0:r0 + span + 1, :].rearrange(
                    "(d two) k -> d two k", two=2)[:, 0]
        else:
            n = span
            src = env.D["L1b_Win"][r0:r0 + span, :]
        ld_st = env.qsp.issue(lambda e, src=src, n=n: e.dma_start(out=wsc_l[0:n, :], in_=src))
        ctx.wait("pe", ld_st)
        for kd in range(8):
            ctx.wait("pe", tr_ev)
            pev = ctx.op("pe", lambda e, n=n, kd=kd: e.transpose(
                PS[4][:, 0:n], wsc_l[0:n, kd * 128:(kd + 1) * 128], env.id_f[0:n, 0:n]), inc="pe")
            ctx.wait("dve", pev)
            tr_ev = ctx.op("dve", lambda e, n=n, kd=kd, dcol=dcol: e.tensor_copy(
                wsc[kd][:, dcol:dcol + n], PS[4][:, 0:n]), inc="dve")
    ctx.wait("pe", tr_ev)
    GROUPS = ((0, 128), (128, 128), (256, 128), (384, 32))
    ev = [None, None]
    stv = [None, None]
    nrm_prev = None
    ld2 = [None, None]
    for tc in range(TC):
        ctx.drain("dve")
        vr = ctx.op("dve", lambda e, tc=tc: e.reciprocal(
            rr[0:1, :], rowsB[0:1, tc * 512:(tc + 1) * 512]), inc="dve")
        ctx.wait("pe", vr)
        ctx.op("pe", lambda e, tc=tc: e.matmul(
            PS[5][:, :], env.ones_r_f[0:1, :], rowsA[0:1, tc * 512:(tc + 1) * 512],
            start=True, stop=True))
        mmq = ctx.op("pe", lambda e: e.matmul(
            PS[6][:, :], env.ones_r_f[0:1, :], rr[0:1, :], start=True, stop=True), inc="pe")
        ctx.wait("dve", mmq)
        ctx.op("dve", lambda e: e.tensor_copy(bcm, PS[5][:, :]))
        ctx.drain("dve")
        vb = ctx.op("dve", lambda e: e.tensor_copy(bcr, PS[6][:, :]), inc="dve")
        ctx.wait("pe", vb)
        for kd in range(8):
            sl2 = kd % 2
            ctx.wait("sp", nrm_prev)
            ld2[sl2] = env.qsp.issue(lambda e, kd=kd, tc=tc, s=xf32[sl2]: e.dma_start(
                out=s, in_=env.resid[kd * 128:(kd + 1) * 128, tc * 512:(tc + 1) * 512]))
            ctx.wait("dve", ld2[sl2])
            ctx.drain("dve")
            ctx.op("dve", lambda e, s=xf32[sl2]: e.tensor_add(s, s, bcm))
            ctx.drain("dve")
            vn2 = ctx.op("dve", lambda e, s=xf32[sl2]: e.tensor_mul(s, s, bcr), inc="dve")
            ctx.wait("pe", vn2)
            for st_i, (c0, nr) in enumerate(GROUPS):
                mm = ctx.op("pe", lambda e, kd=kd, c0=c0, nr=nr, st_i=st_i, s=xf32[sl2]: e.matmul(
                    PS[st_i][0:nr, :], wsc[kd][:, c0:c0 + nr], s,
                    start=(kd == 0), stop=(kd == 7)), inc="pe" if st_i == 3 else None)
            nrm_prev = mm
        for st_i, (c0, nr) in enumerate(GROUPS):
            ctx.wait("dve", mm)
            ctx.wait("dve", stv[st_i % 2])
            ev[st_i % 2] = ctx.op("dve", lambda e, sl=st_i % 2, st_i=st_i, nr=nr: e.tensor_copy(
                sevac[sl][0:nr, :], PS[st_i][0:nr, :]), inc="dve")
            ctx.wait("sp", ev[st_i % 2])
            stv[st_i % 2] = env.qsp.issue(lambda e, sl=st_i % 2, c0=c0, nr=nr, tc=tc: e.dma_start(
                out=env.s_sch[c0:c0 + nr, tc * 512:(tc + 1) * 512], in_=sevac[sl][0:nr, :]))
            if st_i < 3:
                ctx.wait("pe", ev[st_i % 2])
    ctx.barrier("l1b_s1b", queues=env.ALLQ)

    arp = Arena(env.AR, 0, env.ARCOLS)
    bnt = arp.bf(T)
    cnt = arp.bf(T)
    lgct = arp.f32(32 * NT)
    cat = arp.f32(32 * NT)
    gt = arp.f32(32 * NT)
    sht = arp.f32(32 * NT)
    wq2 = arp.f32(32)
    PERSIST_END = arp.off

    T2 = min(1024, T)
    NSP = T // T2
    C2 = T2 // 512
    dtbc = env.SR[0:32, 100:101]
    dskc = env.SR[0:32, 101:102]
    bnwc = env.SR[:, 102:103]
    cnwc_ = env.SR[:, 103:104]
    l1 = env.qsp.issue(lambda e: e.dma_start(
        out=env.stage[0:32, 0:1], in_=env.D["L1b_dtb"].rearrange("(a b) -> a b", b=1)))
    l2 = env.qsp.issue(lambda e: e.dma_start(
        out=env.stage[0:32, 1:2], in_=env.D["L1b_Dsk"].rearrange("(a b) -> a b", b=1)))
    ctx.wait("dve", l1)
    ctx.wait("dve", l2)
    vc = ctx.op("dve", lambda e: e.tensor_copy(env.SR[0:32, 100:102], env.stage[0:32, 0:2]), inc="dve")
    for vecn, col in (("L1b_Bnw", 102), ("L1b_Cnw", 103)):
        for (o0, nn, p0, stp) in ((0, 64, 0, 2), (1, 63, 32, 2), (64, 64, 64, 1)):
            if stp == 2:
                src = env.D[vecn][o0:o0 + nn + (nn % 2)].rearrange("(d two) -> d two", two=2)[:, 0:1]
                nrows = 32
            else:
                src = env.D[vecn][o0:o0 + nn].rearrange("(d o) -> d o", o=1)
                nrows = nn
            env.qsp.issue(lambda e, src=src, col=col, p0=p0, nr=nrows: e.dma_start(
                out=env.SR[p0:p0 + nr, col:col + 1], in_=src,
                allow_slow_non_contiguous=True))
    ctx.barrier("l1b_s2pre", queues=env.ALLQ)

    ar3 = Arena(env.AR, PERSIST_END, env.ARCOLS)
    W2b = T2 + 8
    schh = ar3.f32(448 if False else 0)
    sB = ar3.f32(T2)
    sC = ar3.f32(T2)
    sRD = ar3.f32(W2b)
    sAT = ar3.f32(W2b)
    bDT = ar3.f32(W2b)
    bASP = ar3.f32(W2b)
    bTR = ar3.f32(W2b)
    bLAM = ar3.f32(W2b)
    bEN = ar3.f32(W2b)
    bG = ar3.f32(W2b)
    bCA = ar3.f32(W2b)
    bLGC = ar3.f32(W2b)
    bSC = ar3.f32(W2b)
    bZ = ar3.f32(W2b)
    bAN = ar3.f32(W2b)
    bRA = ar3.f32(W2b)
    sinE = ar3.f32(T2)
    cosE = ar3.f32(T2)
    sqb = ar3.bf(T2)
    rmsr = ar3.f32(T2)
    ibc = ar3.f32(T2)
    p2b = ar3.f32(T2)
    bDM = ar3.f32(T2)
    t1s = ar3.f32(512)
    t2s = ar3.f32(512)
    flip2 = ar3.bf(128)
    carry = ar3.f32(4)

    lf2 = env.qwt.issue(lambda e: e.dma_start(out=flip2, in_=env.D["c_ropeflip"][:, :]))
    ctx.op("pool", lambda e: e.memset(sinE[64:128, 0:T2], 0.0))
    vp = ctx.op("pool", lambda e: e.memset(cosE[64:128, 0:T2], 1.0), inc="pool")
    wq_ev = None
    for mi, vecn in enumerate(("L1b_Bb", "L1b_Cb")):
        ctx.wait("sp", env.stage_free)
        toks = []
        for (o0, nn, p0, stp) in ((0, 64, 0, 2), (1, 63, 32, 2), (64, 64, 64, 1)):
            if stp == 2:
                src = env.D[vecn][:, o0:o0 + nn + (nn % 2)].rearrange(
                    "h (d two) -> h d two", two=2)[:, :, 0]
                ncols = 32
            else:
                src = env.D[vecn][:, o0:o0 + nn]
                ncols = nn
            toks.append(env.qsp.issue(lambda e, src=src, p0=p0, nc=ncols: e.dma_start(
                out=env.stage[0:32, p0:p0 + nc], in_=src,
                allow_slow_non_contiguous=True)))
        for t in toks:
            ctx.wait("pe", t)
        ctx.wait("pe", env.lvc_evac)
        pev = ctx.op("pe", lambda e: e.transpose(
            PS[4][:, 0:32], env.stage[0:32, 0:128], env.id_f[0:32, 0:32]), inc="pe")
        env.stage_free = pev
        ctx.wait("dve", pev)
        if mi == 0:
            wq_ev = ctx.op("dve", lambda e: e.tensor_copy(env.stage[:, 64:96].bitcast(F32)[:, 0:16] if False else env.SR[:, 104:120], PS[4][:, 0:32]) if False else e.tensor_copy(env.SR[:, 104:136], PS[4][:, 0:32]), inc="dve")
            env.lvc_evac = wq_ev
        else:
            ctx.wait("dve", wq_ev)
            wq_ev = ctx.op("dve", lambda e: e.tensor_mul(wq2, PS[4][:, 0:32], env.SR[:, 104:136]), inc="dve")
            env.lvc_evac = wq_ev
    bnt_done = None
    ctx.wait("pe", wq_ev)
    ctx.wait("dve", vp)
    span_dve = None
    span_act = None
    for sp in range(NSP):
        t0 = sp * T2
        ext = 1 if (sp + 1 < NSP) else 0
        ctx.wait("sp", span_dve)
        ctx.wait("sp", span_act)
        lds = []
        lds.append(env.qsp.issue(lambda e, t0=t0: e.dma_start(
            out=sB, in_=env.s_sch[0:128, t0:t0 + T2])))
        lds.append(env.qsp.issue(lambda e, t0=t0: e.dma_start(
            out=sC, in_=env.s_sch[128:256, t0:t0 + T2])))
        lds.append(env.qsp.issue(lambda e, t0=t0, ext=ext: e.dma_start(
            out=sRD[0:96, 0:T2 + ext], in_=env.s_sch[256:352, t0:t0 + T2 + ext])))
        lds.append(env.qsp.issue(lambda e, t0=t0, ext=ext: e.dma_start(
            out=sAT[0:64, 0:T2 + ext], in_=env.s_sch[352:416, t0:t0 + T2 + ext])))
        for t in lds:
            ctx.wait("dve", t)
            ctx.wait("act", t)
        W = T2 + ext
        dtr = sRD[64:96, 0:W]

        def softplus(dst, srcap, w, biascol=None, addconst=None):
            scr = bSC[0:32, 0:w]
            if biascol is not None:
                v = ctx.op("dve", lambda e: e.tensor_scalar(scr, srcap, biascol, None, ALU.add), inc="dve")
            else:
                v = ctx.op("dve", lambda e: e.tensor_copy(scr, srcap), inc="dve")
            ctx.wait("act", v)
            ctx.op("act", lambda e: e.activation(dst, scr, AF.Abs))
            ctx.drain("act")
            ctx.op("act", lambda e: e.activation(dst, dst, AF.Exp, scale=-1.0))
            ctx.drain("act")
            va = ctx.op("act", lambda e: e.activation(dst, dst, AF.Ln, bias=1.0), inc="act")
            ctx.wait("dve", va)
            vm = ctx.op("dve", lambda e: e.tensor_scalar_max(scr, scr, 0.0), inc="dve")
            ctx.drain("dve")
            v2 = ctx.op("dve", lambda e: e.tensor_add(dst, dst, scr), inc="dve")
            ctx.drain("dve")
            if addconst is not None:
                v2 = ctx.op("dve", lambda e: e.tensor_scalar(dst, dst, addconst, None, ALU.add), inc="dve")
                ctx.drain("dve")
            return v2

        ctx.wait("dve", vc)
        ctx.wait("dve", bnt_done)
        ctx.drain("dve")
        softplus(bDT[0:32, 0:W], dtr, W, biascol=dtbc)
        softplus(bASP[0:32, 0:W], sAT[0:32, 0:W], W, addconst=1e-4)
        ctx.op("dve", lambda e, W=W: e.scalar_tensor_tensor(
            bASP[0:32, 0:W], bDT[0:32, 0:W], -1.0, bASP[0:32, 0:W], ALU.mult, ALU.mult))
        ctx.drain("dve")
        v = ctx.op("dve", lambda e, W=W: e.tensor_copy(bTR[0:32, 0:W], sAT[32:64, 0:W]), inc="dve")
        ctx.wait("act", v)
        va = ctx.op("act", lambda e, W=W: e.activation(bTR[0:32, 0:W], bTR[0:32, 0:W], AF.Sigmoid), inc="act")
        ctx.wait("dve", va)
        ctx.op("dve", lambda e, W=W: e.tensor_mul(bLAM[0:32, 0:W], bDT[0:32, 0:W], bTR[0:32, 0:W]))
        ctx.drain("dve")
        ctx.op("dve", lambda e, W=W: e.tensor_scalar(bLAM[0:32, 0:W], bLAM[0:32, 0:W], 0.5, None, ALU.mult))
        ctx.drain("dve")
        ctx.op("dve", lambda e, W=W: e.tensor_sub(bDT[0:32, 0:W], bDT[0:32, 0:W], bLAM[0:32, 0:W]))
        ctx.drain("dve")
        v = ctx.op("dve", lambda e, W=W: e.tensor_copy(bEN[0:32, 0:W], bASP[0:32, 0:W]), inc="dve")
        ctx.wait("act", v)
        va = ctx.op("act", lambda e, W=W: e.activation(bEN[0:32, 0:W], bEN[0:32, 0:W], AF.Exp, scale=-1.0), inc="act")
        ctx.wait("dve", va)
        ctx.op("dve", lambda e, W=W: e.tensor_mul(bSC[0:32, 0:W], bLAM[0:32, 0:W], bEN[0:32, 0:W]))
        ctx.drain("dve")
        ctx.op("dve", lambda e, W=W: e.tensor_copy(bG[0:32, 0:T2], bDT[0:32, 0:T2]))
        ctx.drain("dve")
        nsh = T2 - 1 + ext
        ctx.op("dve", lambda e, nsh=nsh: e.tensor_add(
            bG[0:32, 0:nsh], bG[0:32, 0:nsh], bSC[0:32, 1:1 + nsh]))
        ctx.drain("dve")
        if sp == 0:
            ctx.op("pool", lambda e: e.memset(bZ[0:64, 0:T2], 0.0), inc="pool")
            ctx.wait("dve", ctx.milestone("pool"))
        init1 = 0.0 if sp == 0 else carry[0:32, 0:1]
        ctx.op("dve", lambda e, init1=init1: e.tensor_tensor_scan(
            bCA[0:32, 0:T2], bASP[0:32, 0:T2], bZ[0:32, 0:T2], init1, ALU.add, ALU.add))
        ctx.drain("dve")
        init2 = 0.0 if sp == 0 else carry[0:64, 2:3]
        ctx.op("dve", lambda e, init2=init2: e.tensor_tensor_scan(
            bAN[0:64, 0:T2], sRD[0:64, 0:T2], bZ[0:64, 0:T2], init2, ALU.add, ALU.add))
        ctx.drain("dve")
        ctx.op("dve", lambda e: e.tensor_copy(carry[0:32, 0:1], bCA[0:32, T2 - 1:T2]))
        ctx.drain("dve")
        ctx.op("dve", lambda e: e.tensor_copy(carry[0:64, 2:3], bAN[0:64, T2 - 1:T2]))
        ctx.drain("dve")
        v = ctx.op("dve", lambda e: e.tensor_copy(bLGC[0:32, 0:T2], bG[0:32, 0:T2]), inc="dve")
        ctx.wait("act", v)
        va = ctx.op("act", lambda e: e.activation(bLGC[0:32, 0:T2], bLGC[0:32, 0:T2], AF.Ln), inc="act")
        ctx.wait("dve", va)
        ctx.op("dve", lambda e: e.tensor_sub(bLGC[0:32, 0:T2], bLGC[0:32, 0:T2], bCA[0:32, 0:T2]))
        ctx.drain("dve")
        kbuf = bSC[0:64, 0:T2] if False else None
        ctx.op("dve", lambda e: e.tensor_scalar(bRA[0:64, 0:T2], bAN[0:64, 0:T2], 0.15915494309189535, None, ALU.mult))
        ctx.drain("dve")
        ctx.op("dve", lambda e: e.tensor_scalar(bRA[0:64, 0:T2], bRA[0:64, 0:T2], MAGIC, MAGIC, ALU.add, ALU.subtract))
        ctx.drain("dve")
        ctx.op("dve", lambda e: e.scalar_tensor_tensor(
            sinE[0:64, 0:T2], bRA[0:64, 0:T2], -CW1, bAN[0:64, 0:T2], ALU.mult, ALU.add))
        ctx.drain("dve")
        ctx.op("dve", lambda e: e.scalar_tensor_tensor(
            sinE[0:64, 0:T2], bRA[0:64, 0:T2], -CW2, sinE[0:64, 0:T2], ALU.mult, ALU.add))
        ctx.drain("dve")
        v = ctx.op("dve", lambda e: e.scalar_tensor_tensor(
            bRA[0:64, 0:T2], bRA[0:64, 0:T2], -CW3, sinE[0:64, 0:T2], ALU.mult, ALU.add), inc="dve")
        ctx.wait("act", v)
        vs = ctx.op("act", lambda e: e.activation(sinE[0:64, 0:T2], bRA[0:64, 0:T2], AF.Sin), inc="act")
        ctx.wait("dve", vs)
        ctx.op("dve", lambda e: e.tensor_scalar(bRA[0:64, 0:T2], bRA[0:64, 0:T2],
                                                1.5707963267948966, None, ALU.add))
        ctx.drain("dve")
        ctx.op("dve", lambda e: e.tensor_scalar(cosE[0:64, 0:T2], bRA[0:64, 0:T2],
                                                3.141592653589793, None, ALU.is_gt))
        ctx.drain("dve")
        v = ctx.op("dve", lambda e: e.scalar_tensor_tensor(
            bRA[0:64, 0:T2], cosE[0:64, 0:T2], -6.283185307179586, bRA[0:64, 0:T2],
            ALU.mult, ALU.add), inc="dve")
        ctx.wait("act", v)
        vc2 = ctx.op("act", lambda e: e.activation(cosE[0:64, 0:T2], bRA[0:64, 0:T2], AF.Sin), inc="act")
        ctx.wait("dve", vc2)
        rope_done = None
        for (srcT, dstT, wcol) in ((sB, bnt, bnwc), (sC, cnt, cnwc_)):
            va = ctx.op("act", lambda e, s=srcT: e.activation(sqb, s[0:128, 0:T2], AF.Square), inc="act")
            ctx.wait("pe", va)
            ctx.wait("pe", rope_done)
            ve = None
            for c in range(C2):
                mm = ctx.op("pe", lambda e, c=c: e.matmul(
                    PS[4][0:1, :], env.ones_c_b[:, 0:1], sqb[:, c * 512:(c + 1) * 512],
                    start=True, stop=True), inc="pe")
                ctx.wait("dve", mm)
                ve = ctx.op("dve", lambda e, c=c: e.tensor_copy(
                    rmsr[0:1, c * 512:(c + 1) * 512], PS[4][0:1, :]), inc="dve")
                ctx.wait("pe", ve)
            ctx.wait("act", ve)
            va = ctx.op("act", lambda e: e.activation(
                rmsr[0:1, 0:T2], rmsr[0:1, 0:T2], AF.Sqrt, scale=1.0 / 128.0, bias=LN_EPS), inc="act")
            ctx.wait("dve", va)
            vr = ctx.op("dve", lambda e: e.reciprocal(rmsr[0:1, 0:T2], rmsr[0:1, 0:T2]), inc="dve")
            ctx.wait("pe", vr)
            vb = None
            for c in range(C2):
                mm = ctx.op("pe", lambda e, c=c: e.matmul(
                    PS[4][:, :], env.ones_r_f[0:1, :], rmsr[0:1, c * 512:(c + 1) * 512],
                    start=True, stop=True), inc="pe")
                ctx.wait("dve", mm)
                vb = ctx.op("dve", lambda e, c=c: e.tensor_copy(
                    ibc[:, c * 512:(c + 1) * 512], PS[4][:, :]), inc="dve")
                ctx.wait("pe", vb)
            ctx.drain("dve")
            vn = ctx.op("dve", lambda e, s=srcT, wcol=wcol: e.scalar_tensor_tensor(
                sqb, s[0:128, 0:T2], wcol, ibc, ALU.mult, ALU.mult), inc="dve")
            ctx.wait("pe", vn)
            ctx.wait("pe", lf2)
            vvv = None
            for c in range(C2):
                mm = ctx.op("pe", lambda e, c=c: e.matmul(
                    PS[5][:, :], flip2, sqb[:, c * 512:(c + 1) * 512], start=True, stop=True), inc="pe")
                ctx.wait("dve", mm)
                ctx.op("dve", lambda e, c=c: e.tensor_mul(
                    t1s, sqb[:, c * 512:(c + 1) * 512], cosE[:, c * 512:(c + 1) * 512]))
                ctx.drain("dve")
                ctx.op("dve", lambda e, c=c: e.tensor_mul(
                    t2s, PS[5][:, :], sinE[:, c * 512:(c + 1) * 512]))
                ctx.drain("dve")
                vvv = ctx.op("dve", lambda e, d=dstT, c=c, t0=t0: e.tensor_add(
                    d[:, t0 + c * 512:t0 + (c + 1) * 512], t1s, t2s), inc="dve")
                ctx.wait("pe", vvv)
            ctx.wait("act", vvv)
            rope_done = vvv
        ctx.drain("dve")
        v = ctx.op("dve", lambda e, t0=t0: e.tensor_mul(
            p2b, bnt[:, t0:t0 + T2], cnt[:, t0:t0 + T2]), inc="dve")
        ctx.wait("pe", v)
        ctx.wait("pe", wq_ev)
        vdm = None
        for c in range(C2):
            mm = ctx.op("pe", lambda e, c=c: e.matmul(
                PS[6][0:32, :], wq2, p2b[:, c * 512:(c + 1) * 512], start=True, stop=True), inc="pe")
            ctx.wait("dve", mm)
            vdm = ctx.op("dve", lambda e, c=c: e.tensor_copy(
                bDM[0:32, c * 512:(c + 1) * 512], PS[6][0:32, :]), inc="dve")
            ctx.wait("pe", vdm)
        ctx.drain("dve")
        ctx.op("dve", lambda e: e.tensor_mul(bDM[0:32, 0:T2], bDM[0:32, 0:T2], bDT[0:32, 0:T2]))
        ctx.drain("dve")
        ctx.op("dve", lambda e: e.tensor_scalar(bDM[0:32, 0:T2], bDM[0:32, 0:T2], dskc[0:32, 0:1], None, ALU.add))
        ctx.drain("dve")
        tok_ev = None
        for i0 in range(T2 // 128):
            i = (t0 // 128) + i0
            for (srcap, dst) in ((bLGC[0:32, :], lgct), (bCA[0:32, :], cat),
                                 (bG[0:32, :], gt), (bDM[0:32, :], sht)):
                ctx.wait("pe", tok_ev)
                pev = ctx.op("pe", lambda e, s=srcap, i0=i0: e.transpose(
                    PS[4][:, 0:32], s[:, i0 * 128:(i0 + 1) * 128], env.id_f[0:32, 0:32]), inc="pe")
                ctx.wait("dve", pev)
                tok_ev = ctx.op("dve", lambda e, d=dst, i=i: e.tensor_copy(
                    d[:, 32 * i:32 * (i + 1)], PS[4][:, 0:32]), inc="dve")
        bnt_done = tok_ev
        span_dve = ctx.milestone("dve")
        span_act = ctx.milestone("act")
    ctx.barrier("l1b_s2", queues=env.ALLQ)

    ar5 = Arena(env.AR, PERSIST_END, env.ARCOLS)
    wol = [ar5.bf(2048) for _ in range(2)]
    woT = [ar5.bf(1024) for _ in range(16)]
    bntok = [ar5.bf(128) for _ in range(NT)]
    xa = [ar5.bf(2048) for _ in range(2)]
    za = [ar5.bf(2048) for _ in range(2)]
    yt = [ar5.bf(2048) for _ in range(2)]
    ytT = [ar5.bf(512) for _ in range(16)]
    hS = ar5.f32(2048)
    hB = ar5.bf(2048)
    lstr = ar5.f32(128)
    carow = ar5.f32(128)
    caec = ar5.f32(2)
    caep = ar5.f32(32)
    e2 = ar5.f32(128)
    cnw_c = ar5.bf(128)
    Fb = ar5.bf(128)
    xcs = ar5.bf(64)
    edc = ar5.f32(1)
    ccc = ar5.f32(1)
    ys1 = ar5.f32(64)
    ys2 = ar5.f32(64)
    zsa = [ar5.f32(2048) for _ in range(2)]
    outs = [ar5.f32(512) for _ in range(2)]
    lw = []
    tr_ev = None
    for mt in range(8):
        sl = mt % 2
        ctx.wait("pool", tr_ev)
        ldm = env.qwt.issue(lambda e, mt=mt, s=wol[sl]: e.dma_start(
            out=s, in_=env.D["L1b_Wout"][mt * 128:(mt + 1) * 128, :]))
        ctx.wait("pe", ldm)
        for rc in range(16):
            ctx.wait("pe", tr_ev)
            pev = ctx.op("pe", lambda e, s=wol[sl], rc=rc: e.transpose(
                env.PB[:, 0:128], s[:, rc * 128:(rc + 1) * 128], env.id_b[:]), inc="pe")
            ctx.wait("dve", pev)
            tr_ev = ctx.op("dve", lambda e, rc=rc, mt=mt: e.tensor_copy(
                woT[rc][:, mt * 128:(mt + 1) * 128], env.PB[:, 0:128]), inc="dve")
    lls = env.qsp.issue(lambda e: e.dma_start(out=lstr, in_=env.D["c_lstrict"][:, :]))
    for i in range(NT):
        ctx.wait("pe", tr_ev)
        pev = ctx.op("pe", lambda e, i=i: e.transpose(
            env.PB[:, 0:128], bnt[:, i * 128:(i + 1) * 128], env.id_b[:]), inc="pe")
        ctx.wait("dve", pev)
        tr_ev = ctx.op("dve", lambda e, i=i: e.tensor_copy(bntok[i], env.PB[:, 0:128]), inc="dve")
    vh0 = ctx.op("pool", lambda e: e.memset(hS[:, :], 0.0), inc="pool")
    ctx.wait("pe", tr_ev)
    ctx.wait("dve", lls)
    ctx.wait("dve", vh0)
    xld = [None, None]
    xfree = [None, None]
    zsa_free = [None, None]
    ost = [None, None]
    oev_t = [None, None]
    grp_pe = None
    head_dve = None
    for i in range(NT):
        sl = i % 2
        ctx.wait("sp", xfree[sl])
        xld[sl] = env.qsp.issue(lambda e, i=i, s=xa[sl]: e.dma_start(
            out=s, in_=env.s_xtok[i * 128:(i + 1) * 128, :]))
        zl = env.qsp.issue(lambda e, i=i, s=za[sl]: e.dma_start(
            out=s, in_=env.s_ztok[i * 128:(i + 1) * 128, :]))
        ctx.wait("dve", xld[sl])
        ctx.wait("act", zl)
        ctx.wait("act", zsa_free[sl])
        vzs = ctx.op("act", lambda e, sl=sl: e.activation(zsa[sl], za[sl], AF.Silu), inc="act")
        ctx.wait("dve", vzs)
        for h in range(32):
            if h == 0:
                ctx.wait("pe", oev_t[0])
                ctx.wait("pe", oev_t[1])
            lgc_col = lgct[:, 32 * i + h:32 * i + h + 1]
            ca_col = cat[:, 32 * i + h:32 * i + h + 1]
            g_col = gt[:, 32 * i + h:32 * i + h + 1]
            sh_col = sht[:, 32 * i + h:32 * i + h + 1]
            hsl = hS[:, 64 * h:64 * (h + 1)]
            hbl = hB[:, 64 * h:64 * (h + 1)]
            ctx.wait("pe", head_dve)
            ptr = ctx.op("pe", lambda e, c=ca_col: e.transpose(
                PS[4][0:1, 0:128], c, env.id_f[:, :]), inc="pe")
            ctx.wait("dve", ptr)
            vcr = ctx.op("dve", lambda e: e.tensor_copy(carow[0:1, 0:128], PS[4][0:1, 0:128]), inc="dve")
            ctx.wait("pe", vcr)
            mmb = ctx.op("pe", lambda e: e.matmul(
                PS[4][:, 0:128], env.ones_r_f[0:1, :], carow[0:1, 0:128],
                start=True, stop=True), inc="pe")
            mmc = ctx.op("pe", lambda e: e.matmul(
                PS[6][:, 0:1], env.ones_r_f[0:1, :], carow[0:1, 127:128],
                start=True, stop=True), inc="pe")
            ctx.wait("dve", mmc)
            ctx.drain("dve")
            vce = ctx.op("dve", lambda e: e.tensor_copy(caec[:, 0:1], PS[6][:, 0:1]), inc="dve")
            v2 = ctx.op("dve", lambda e, lc=lgc_col: e.scalar_tensor_tensor(
                e2, PS[4][:, 0:128], lc, lstr, ALU.add, ALU.add), inc="dve")
            ctx.wait("act", v2)
            vE = ctx.op("act", lambda e: e.activation(e2, e2, AF.Exp), inc="act")
            vcw = ctx.op("dve", lambda e, i=i, h=h: e.tensor_scalar(
                cnw_c, cnt[:, i * 128:(i + 1) * 128], wq2[:, h:h + 1], None, ALU.mult), inc="dve")
            ctx.wait("pe", vcw)
            mmg = ctx.op("pe", lambda e, i=i: e.matmul(
                PS[5][:, 0:128], bnt[:, i * 128:(i + 1) * 128], cnw_c,
                start=True, stop=True), inc="pe")
            ctx.wait("dve", mmg)
            ctx.wait("dve", vE)
            vF = ctx.op("dve", lambda e: e.tensor_mul(Fb, PS[5][:, 0:128], e2), inc="dve")
            ctx.drain("dve")
            if i > 0:
                vsub = ctx.op("dve", lambda e, c=ca_col, h=h: e.tensor_sub(
                    edc[:, 0:1], c, caep[:, h:h + 1]), inc="dve")
                ctx.wait("act", vsub)
                ved = ctx.op("act", lambda e: e.activation(edc[:, 0:1], edc[:, 0:1], AF.Exp), inc="act")
            ctx.wait("dve", vce)
            vcc = ctx.op("dve", lambda e, c=ca_col: e.tensor_sub(ccc[:, 0:1], caec[:, 0:1], c), inc="dve")
            ctx.wait("act", vcc)
            vcc = ctx.op("act", lambda e: e.activation(ccc[:, 0:1], ccc[:, 0:1], AF.Exp), inc="act")
            ctx.wait("dve", vcc)
            vcc = ctx.op("dve", lambda e, g=g_col: e.tensor_mul(ccc[:, 0:1], ccc[:, 0:1], g), inc="dve")
            ctx.drain("dve")
            vxc = ctx.op("dve", lambda e, s=xa[sl], h=h: e.tensor_scalar(
                xcs, s[:, h * 64:(h + 1) * 64], ccc[:, 0:1], None, ALU.mult), inc="dve")
            ctx.wait("pe", vF)
            ctx.wait("pe", xld[sl])
            mmy = ctx.op("pe", lambda e, s=xa[sl], h=h: e.matmul(
                PS[0][0:128, 0:64], Fb, s[:, h * 64:(h + 1) * 64], start=True, stop=True), inc="pe")
            if i > 0:
                mmyi = ctx.op("pe", lambda e, hbl=hbl: e.matmul(
                    PS[1][0:128, 0:64], cnw_c, hbl, start=True, stop=True), inc="pe")
            ctx.wait("pe", vxc)
            mmh = ctx.op("pe", lambda e, i=i: e.matmul(
                PS[2][0:128, 0:64], bntok[i], xcs, start=True, stop=True), inc="pe")
            ctx.wait("dve", mmy)
            if i > 0:
                ctx.wait("dve", mmyi)
                ctx.wait("dve", ved)
                vyi = ctx.op("dve", lambda e: e.tensor_scalar(ys1, PS[1][0:128, 0:64], edc[:, 0:1], None, ALU.mult), inc="dve")
                ctx.drain("dve")
                ctx.op("dve", lambda e, s=xa[sl], h=h, sh=sh_col: e.scalar_tensor_tensor(
                    ys2, s[:, h * 64:(h + 1) * 64], sh, ys1, ALU.mult, ALU.add))
            else:
                ctx.op("dve", lambda e, s=xa[sl], h=h, sh=sh_col: e.tensor_scalar(
                    ys2, s[:, h * 64:(h + 1) * 64], sh, None, ALU.mult))
            ctx.drain("dve")
            v = ctx.op("dve", lambda e: e.tensor_add(ys1, ys2, PS[0][0:128, 0:64]), inc="dve")
            ctx.drain("dve")
            vy = ctx.op("dve", lambda e, d=yt[sl], h=h, sl=sl: e.tensor_mul(
                d[:, h * 64:(h + 1) * 64], ys1, zsa[sl][:, h * 64:(h + 1) * 64]), inc="dve")
            ctx.wait("dve", mmh)
            if i > 0:
                ctx.op("dve", lambda e, h=h: e.tensor_sub(edc[:, 0:1], caec[:, 0:1], caep[:, h:h + 1]))
                ctx.drain("dve")
                v = ctx.op("dve", lambda e: e.tensor_copy(ccc[:, 0:1], edc[:, 0:1]), inc="dve")
                ctx.wait("act", v)
                vdt = ctx.op("act", lambda e: e.activation(ccc[:, 0:1], ccc[:, 0:1], AF.Exp), inc="act")
                ctx.wait("dve", vdt)
                ctx.op("dve", lambda e, hsl=hsl: e.scalar_tensor_tensor(
                    hsl, hsl, ccc[:, 0:1], PS[2][0:128, 0:64], ALU.mult, ALU.add))
            else:
                ctx.op("dve", lambda e, hsl=hsl: e.tensor_copy(hsl, PS[2][0:128, 0:64]))
            ctx.drain("dve")
            vhb = ctx.op("dve", lambda e, hsl=hsl, hbl=hbl: e.tensor_copy(hbl, hsl), inc="dve")
            ctx.drain("dve")
            vcp = ctx.op("dve", lambda e, h=h: e.tensor_copy(caep[:, h:h + 1], caec[:, 0:1]), inc="dve")
            ctx.drain("dve")
            head_dve = vcp
        ctx.wait("pe", head_dve)
        for rc in range(16):
            if i % 4 == 0 and rc == 0:
                ctx.wait("pe", grp_pe)
            ctx.wait("pe", tr_ev)
            pev = ctx.op("pe", lambda e, rc=rc, sl=sl: e.transpose(
                env.PB[:, 0:128], yt[sl][:, rc * 128:(rc + 1) * 128], env.id_b[:]), inc="pe")
            ctx.wait("dve", pev)
            tr_ev = ctx.op("dve", lambda e, rc=rc, i=i: e.tensor_copy(
                ytT[rc][:, (i % 4) * 128:((i % 4) + 1) * 128], env.PB[:, 0:128]), inc="dve")
        xfree[sl] = ctx.milestone("dve")
        zsa_free[sl] = ctx.milestone("dve")
        if i % 4 == 3:
            qg = i // 4
            ctx.wait("pe", tr_ev)
            for mt in range(8):
                pk = PS[3 + (mt % 2)]
                ctx.wait("pe", oev_t[mt % 2])
                for rc in range(16):
                    mm = ctx.op("pe", lambda e, rc=rc, mt=mt, pk=pk: e.matmul(
                        pk[:, :], woT[rc][:, mt * 128:(mt + 1) * 128], ytT[rc],
                        start=(rc == 0), stop=(rc == 15)), inc="pe" if rc == 15 else None)
                ctx.wait("dve", mm)
                ctx.wait("dve", ost[mt % 2])
                oev_t[mt % 2] = ctx.op("dve", lambda e, pk=pk, o=outs[mt % 2]: e.tensor_copy(
                    o, pk[:, :]), inc="dve")
                ctx.wait("pool", oev_t[mt % 2])
                ost[mt % 2] = env.qac.issue(lambda e, o=outs[mt % 2], mt=mt, qg=qg: e.dma_start(
                    out=env.resid[mt * 128:(mt + 1) * 128, qg * 512:(qg + 1) * 512], in_=o,
                    accum_op=ALU.add))
            grp_pe = ctx.milestone("pe")
    ctx.barrier("l1b", queues=env.ALLQ)


NAME_MAP = {
    "L0a_Wq": "L0_attn_a_W_q",
    "L0f_Wr": "L0_ffn_Wr", "L0f_br": "L0_ffn_br",
    "L0f_g": "L0_ffn_ln_g", "L0f_b": "L0_ffn_ln_b",
    "L0f_W1": "L0_ffn_W1", "L0f_b1": "L0_ffn_b1",
    "L0f_W2": "L0_ffn_W2", "L0f_b2": "L0_ffn_b2",
    "L0b_WQ": "L0_attn_b_W_Q", "L0b_bQ": "L0_attn_b_b_Q",
    "L0b_WK": "L0_attn_b_W_K", "L0b_bK": "L0_attn_b_b_K",
    "L0b_WV": "L0_attn_b_W_V", "L0b_bV": "L0_attn_b_b_V",
    "L0b_WO": "L0_attn_b_W_O", "L0b_bO": "L0_attn_b_b_O",
    "L1a_W1": "L1_attn_a_W1", "L1a_W2": "L1_attn_a_W2",
    "L1a_bias": "L1_attn_a_bias", "L1a_et": "L1_attn_a_embeddings_table",
    "L1f_CB": "L1_ffn_CB", "L1f_SCB": "L1_ffn_SCB", "L1f_bias": "L1_ffn_bias",
    "L1b_Win": "L1_attn_b_W_in", "L1b_dtb": "L1_attn_b_dt_bias",
    "L1b_Bb": "L1_attn_b_B_bias", "L1b_Cb": "L1_attn_b_C_bias",
    "L1b_Bnw": "L1_attn_b_B_norm_w", "L1b_Cnw": "L1_attn_b_C_norm_w",
    "L1b_Dsk": "L1_attn_b_D_skip", "L1b_Wout": "L1_attn_b_W_out",
}


def build_consts(T):
    c = {}
    c["c_eye"] = np.eye(128, dtype=np.float32)
    pos = np.arange(T, dtype=np.float32)
    j = np.arange(64)
    freq = (10000.0 ** ((j % 32) / 32.0)).astype(np.float32)
    ang = pos[None, :] / freq[:, None]
    cos = np.cos(ang).astype(np.float32)
    sin = np.sin(ang).astype(np.float32)
    c["c_rope_cos"] = np.concatenate([cos, cos], axis=0)
    c["c_rope_sin"] = np.concatenate([sin, sin], axis=0)
    F = np.zeros((128, 128), np.float32)
    for base in (0, 64):
        for r in range(32):
            F[base + r, base + r + 32] = -1.0
            F[base + r + 32, base + r] = 1.0
    c["c_ropeflip"] = np.ascontiguousarray(F.T)
    qk = np.zeros((4 * 128, 512), np.float32)
    kq = np.zeros((4 * 128, 512), np.float32)
    for m in range(4):
        q_idx = np.arange(128)[:, None] + m * 128
        k_idx = np.arange(512)[None, :]
        qk[m * 128:(m + 1) * 128] = np.where(k_idx <= q_idx, 0.0, NEG)
        k_idx2 = np.arange(128)[:, None] + m * 128
        q_idx2 = np.arange(512)[None, :]
        kq[m * 128:(m + 1) * 128] = np.where(q_idx2 >= k_idx2, 0.0, NEG)
    c["c_cmask_qk"] = qk
    c["c_cmask_kq"] = kq
    s_ = np.arange(128)[:, None]
    t_ = np.arange(128)[None, :]
    c["c_lstrict"] = np.where(t_ >= s_ + 1, 0.0, NEG).astype(np.float32)
    return c


_CACHE = {}


def kernel(**inputs):
    from concourse.bass_utils import run_bass_kernel_spmd
    T = 4096
    if "nc" not in _CACHE:
        nc, env = build(T=T, parts=ALL_PARTS)
        _CACHE["nc"] = nc
        _CACHE["env"] = env
    nc = _CACHE["nc"]
    base = {k: np.ascontiguousarray(np.asarray(inputs[v], dtype=np.float32))
            for k, v in NAME_MAP.items()}
    base.update(build_consts(T))
    x = np.asarray(inputs["L0_attn_a_hidden_states"], dtype=np.float32)
    in_maps = []
    for b in range(8):
        m = dict(base)
        m["x"] = np.ascontiguousarray(x[b])
        in_maps.append(m)
    res = run_bass_kernel_spmd(nc, in_maps, core_ids=list(range(8)))
    out = np.stack([np.asarray(r["out"]) for r in res.results], axis=0)
    return out.astype(np.float32)


# version 09
# speedup vs baseline: 1.2555x; 1.2555x over previous
import numpy as np
import ml_dtypes
from contextlib import ExitStack
import concourse.bass as bass
import concourse.mybir as mybir

F32 = mybir.dt.float32
BF16 = mybir.dt.bfloat16
AF = mybir.ActivationFunctionType
ALU = mybir.AluOpType

MAGIC = 12582912.0
CW1, CW2, CW3 = 6.28271484375, 0.0004704594612121582, 3.968374073792802e-09
NEG = -1e30
LN_EPS = 1e-5


class Ctx:
    ENGINES = ("pe", "dve", "act", "pool", "sp")

    def __init__(self, nc, es):
        self.nc = nc
        self.es = es
        self.q = {e: [] for e in self.ENGINES}
        self.sems = {}
        self.cnt = {}
        for s in ("pe", "dve", "act", "pool", "bar"):
            self.mksem(s)

    def mksem(self, s):
        self.sems[s] = self.es.enter_context(self.nc.semaphore(s))
        self.cnt[s] = 0

    def op(self, eng, fn, inc=None, incn=1):
        if inc is None:
            self.q[eng].append(fn)
            return None
        h = self.sems[inc]
        self.cnt[inc] += incn
        self.q[eng].append(lambda e, fn=fn, h=h, n=incn: fn(e).then_inc(h, n))
        return (inc, self.cnt[inc])

    def wait(self, eng, tok):
        if tok is None:
            return
        s, v = tok
        if v <= 0:
            return
        h = self.sems[s]
        self.q[eng].append(lambda e, h=h, v=v: e.wait_ge(h, v))

    def drain(self, eng):
        self.q[eng].append(lambda e: e.drain())

    def milestone(self, eng):
        return (eng, self.cnt[eng])

    def barrier(self, name="", queues=()):
        for q in queues:
            q.flush()
        h = self.sems["bar"]
        tgt = self.cnt["bar"] + len(self.ENGINES)
        for eng in self.ENGINES:
            self.q[eng].append(lambda e: e.drain())
            self.q[eng].append(lambda e, h=h: e.nop().then_inc(h, 1))
            self.q[eng].append(lambda e, h=h, t=tgt: e.wait_ge(h, t))
        self.cnt["bar"] = tgt

    def finalize(self):
        with self.nc.Block() as block:
            @block.tensor
            def _(e):
                for f in self.q["pe"]:
                    f(e)

            @block.vector
            def _(e):
                for f in self.q["dve"]:
                    f(e)

            @block.scalar
            def _(e):
                for f in self.q["act"]:
                    f(e)

            @block.gpsimd
            def _(e):
                for f in self.q["pool"]:
                    f(e)

            @block.sync
            def _(e):
                for f in self.q["sp"]:
                    f(e)


class DmaQueue:

    def __init__(self, ctx, eng, name, nlanes):
        self.ctx, self.eng = ctx, eng
        self.lanes = []
        for i in range(nlanes):
            s = f"{name}{i}"
            ctx.mksem(s)
            self.lanes.append(s)
        self.i = 0

    def issue(self, fn):
        s = self.lanes[self.i % len(self.lanes)]
        self.i += 1
        self.ctx.wait(self.eng, (s, self.ctx.cnt[s]))
        return self.ctx.op(self.eng, fn, inc=s, incn=16)

    def flush(self):
        for s in self.lanes:
            self.ctx.wait(self.eng, (s, self.ctx.cnt[s]))


class Arena:
    def __init__(self, ar, base, limit):
        self.ar, self.base, self.limit, self.off = ar, base, limit, base

    def bf(self, cols):
        o = self.off
        assert o + cols <= self.limit, f"arena overflow {o}+{cols}>{self.limit}"
        self.off = o + cols
        return self.ar[:, o:o + cols]

    def f32(self, words):
        if self.off % 2:
            self.off += 1
        o = self.off
        assert o + 2 * words <= self.limit, f"arena overflow {o}+{2*words}>{self.limit}"
        self.off = o + 2 * words
        return self.ar[:, o:o + 2 * words].bitcast(F32)


class Env:
    pass


ALL_PARTS = ("l0a", "l0f1", "l0b", "l0f2", "l1a", "l1f1", "l1b", "l1f2")


def build(T=4096, parts=ALL_PARTS):
    nc = bass.Bass("TRN2", target_bir_lowering=False)
    es = ExitStack()
    es.__enter__()
    env = Env()
    env.T, env.TC, env.NT = T, T // 512, T // 128
    D = {}

    def din(name, shape, dt=F32):
        D[name] = nc.dram_tensor(name, list(shape), dt, kind="ExternalInput").ap()

    din("x", (T, 1024))
    din("L0a_Wq", (1024, 1024))
    din("L0f_Wr", (1024, 8)); din("L0f_br", (8,))
    din("L0f_g", (1024,)); din("L0f_b", (1024,))
    din("L0f_W1", (8, 1024, 4096)); din("L0f_b1", (8, 4096))
    din("L0f_W2", (8, 4096, 1024)); din("L0f_b2", (8, 1024))
    din("L0b_WQ", (16, 1024, 64)); din("L0b_bQ", (16, 64))
    din("L0b_WK", (16, 1024, 64)); din("L0b_bK", (16, 64))
    din("L0b_WV", (16, 1024, 64)); din("L0b_bV", (16, 64))
    din("L0b_WO", (16, 64, 1024)); din("L0b_bO", (1024,))
    din("L1a_W1", (1024, 1024)); din("L1a_W2", (1024, 1024))
    din("L1a_bias", (1024,)); din("L1a_et", (21, 1024))
    din("L1f_CB", (1024, 1024)); din("L1f_SCB", (1024,)); din("L1f_bias", (1024,))
    din("L1b_Win", (4480, 1024)); din("L1b_dtb", (32,))
    din("L1b_Bb", (32, 128)); din("L1b_Cb", (32, 128))
    din("L1b_Bnw", (128,)); din("L1b_Cnw", (128,)); din("L1b_Dsk", (32,))
    din("L1b_Wout", (1024, 2048))
    din("c_eye", (128, 128))
    din("c_rope_cos", (128, T)); din("c_rope_sin", (128, T)); din("c_ropeflip", (128, 128))
    din("c_cmask_qk", (4 * 128, 512)); din("c_cmask_kq", (4 * 128, 512))
    din("c_lstrict", (128, 128))
    din("c_sel8", (32, 1024)); din("c_selmask", (32, 4))
    env.D = D

    env.out = nc.dram_tensor("out", [T, 1024], F32, kind="ExternalOutput").ap()
    env.resid = nc.dram_tensor("resid", [1024, T], F32, kind="Internal").ap()
    env.xn_hbm = nc.dram_tensor("xn_hbm", [1024, T], BF16, kind="Internal").ap()
    env.s_qT = nc.dram_tensor("s_qT", [1024, T], BF16, kind="Internal").ap()
    env.s_kT = nc.dram_tensor("s_kT", [1024, T], BF16, kind="Internal").ap()
    env.s_vT = nc.dram_tensor("s_vT", [1024, T], BF16, kind="Internal").ap()
    env.s_zT = nc.dram_tensor("s_zT", [2048, T], BF16, kind="Internal").ap()
    env.s_xtok = nc.dram_tensor("s_xtok", [T, 2048], BF16, kind="Internal").ap()
    env.s_aT = nc.dram_tensor("s_aT", [T, T], BF16, kind="Internal").ap()
    env.s_sch = nc.dram_tensor("s_sch", [448, T], F32, kind="Internal").ap()
    env.dbg = nc.dram_tensor("dbg", [128, 8192], F32, kind="Internal").ap()
    env.s_ztok = nc.dram_tensor("s_ztok", [T, 2048], BF16, kind="Internal").ap()

    ctx = Ctx(nc, es)
    env.ctx, env.nc = ctx, nc
    env.qsp = DmaQueue(ctx, "sp", "qs", 6)
    env.qwt = DmaQueue(ctx, "pool", "qw", 6)
    env.qac = DmaQueue(ctx, "pool", "qa", 4)
    env.ALLQ = (env.qsp, env.qwt, env.qac)

    ARCOLS = 101376
    env.AR = es.enter_context(nc.sbuf_tensor("AR", [128, ARCOLS], BF16))
    env.SR = es.enter_context(nc.sbuf_tensor("SR", [128, 512], F32))
    env.id_f = es.enter_context(nc.sbuf_tensor("id_f", [128, 128], F32))
    env.id_b = es.enter_context(nc.sbuf_tensor("id_b", [128, 128], BF16))
    env.ones_r_f = es.enter_context(nc.sbuf_tensor("ones_r_f", [1, 128], F32))
    env.ones_c_b = es.enter_context(nc.sbuf_tensor("ones_c_b", [128, 1], BF16))
    env.cpool = es.enter_context(nc.sbuf_tensor("cpool", [128, 16], F32))
    env.stage = es.enter_context(nc.sbuf_tensor("stage", [128, 128], F32))
    env.ones_c_f = es.enter_context(nc.sbuf_tensor("ones_c_f", [128, 1], F32))
    env.stage_free = None
    env.lvc_evac = None
    env.PS = [es.enter_context(nc.psum_tensor(f"ps{i}", [128, 512], F32)) for i in range(7)]
    env.PB = es.enter_context(nc.psum_tensor("pb", [128, 512], BF16))

    env.CONSTS = [LN_EPS, 1.0, 0.5, np.pi / 2, 1e-4, 2.0]
    for j, v in enumerate(env.CONSTS):
        nc.const_aps.register(F32, float(v), env.cpool[:, j:j + 1])

    env.xn = [env.AR[:, i * T:(i + 1) * T] for i in range(8)]
    env.WORK_BASE = 8 * T
    env.ARCOLS = ARCOLS

    emit_prelude(env)
    emit_transpose_in(env)
    for p in parts:
        EMITTERS[p](env)
    emit_transpose_out(env)
    ctx.barrier("end", queues=env.ALLQ)
    ctx.finalize()
    env.es = es
    return nc, env


def work_arena(env):
    return Arena(env.AR, env.WORK_BASE, env.ARCOLS)


def emit_prelude(env):
    ctx = env.ctx
    env.qsp.issue(lambda e: e.dma_start(out=env.id_f[:], in_=env.D["c_eye"][:]))
    tok = env.qwt.issue(lambda e: e.dma_start(out=env.id_b[:], in_=env.D["c_eye"][:]))
    ctx.op("pool", lambda e: e.memset(env.ones_r_f[:], 1.0))
    ctx.op("pool", lambda e: e.memset(env.ones_c_b[:], 1.0))
    ctx.op("pool", lambda e: e.memset(env.ones_c_f[:], 1.0))
    for j, v in enumerate(env.CONSTS):
        ctx.op("pool", lambda e, j=j, v=v: e.memset(env.cpool[:, j:j + 1], float(v)), inc="pool")
    ctx.barrier("prelude", queues=env.ALLQ)


def emit_transpose_in(env):
    ctx = env.ctx
    ar = work_arena(env)
    xin = [ar.f32(1024) for _ in range(2)]
    ob = [ar.f32(128) for _ in range(4)]
    tr_done_for_slot = [None, None]
    ev_done = [None, None]
    st_done = [None] * 4
    for tt in range(env.NT):
        sl = tt % 2
        ctx.wait("sp", tr_done_for_slot[sl])
        ld = env.qsp.issue(lambda e, tt=tt, s=xin[sl]: e.dma_start(
            out=s, in_=env.D["x"][tt * 128:(tt + 1) * 128, :]))
        ctx.wait("pe", ld)
        for dt_ in range(8):
            ob_i = (tt * 8 + dt_) % 4
            pk = env.PS[dt_ % 2]
            ctx.wait("pe", ev_done[dt_ % 2])
            pev = ctx.op("pe", lambda e, s=xin[sl], dt_=dt_, pk=pk: e.transpose(
                pk[:, 0:128], s[:, dt_ * 128:(dt_ + 1) * 128], env.id_f[:]), inc="pe")
            ctx.wait("dve", pev)
            ctx.wait("dve", st_done[ob_i])
            ev_done[dt_ % 2] = ctx.op("dve", lambda e, pk=pk, o=ob[ob_i]: e.tensor_copy(
                o, pk[:, 0:128]), inc="dve")
            ctx.wait("sp", ev_done[dt_ % 2])
            st_done[ob_i] = env.qsp.issue(lambda e, o=ob[ob_i], dt_=dt_, tt=tt: e.dma_start(
                out=env.resid[dt_ * 128:(dt_ + 1) * 128, tt * 128:(tt + 1) * 128], in_=o))
        tr_done_for_slot[sl] = ctx.milestone("pe")
    ctx.barrier("transpose_in", queues=env.ALLQ)


def emit_transpose_out(env):
    ctx = env.ctx
    ar = work_arena(env)
    strip = [ar.f32(env.T) for _ in range(2)]
    ob = [ar.f32(128) for _ in range(4)]
    tr_done_for_slot = [None, None]
    ev_done = [None, None]
    st_done = [None] * 4
    for dt_ in range(8):
        sl = dt_ % 2
        ctx.wait("sp", tr_done_for_slot[sl])
        ld = env.qsp.issue(lambda e, dt_=dt_, s=strip[sl]: e.dma_start(
            out=s, in_=env.resid[dt_ * 128:(dt_ + 1) * 128, :]))
        ctx.wait("pe", ld)
        for tt in range(env.NT):
            ob_i = (dt_ * env.NT + tt) % 4
            pk = env.PS[tt % 2]
            ctx.wait("pe", ev_done[tt % 2])
            pev = ctx.op("pe", lambda e, s=strip[sl], tt=tt, pk=pk: e.transpose(
                pk[:, 0:128], s[:, tt * 128:(tt + 1) * 128], env.id_f[:]), inc="pe")
            ctx.wait("dve", pev)
            ctx.wait("dve", st_done[ob_i])
            ev_done[tt % 2] = ctx.op("dve", lambda e, pk=pk, o=ob[ob_i]: e.tensor_copy(
                o, pk[:, 0:128]), inc="dve")
            ctx.wait("sp", ev_done[tt % 2])
            st_done[ob_i] = env.qsp.issue(lambda e, o=ob[ob_i], dt_=dt_, tt=tt: e.dma_start(
                out=env.out[tt * 128:(tt + 1) * 128, dt_ * 128:(dt_ + 1) * 128], in_=o))
        tr_done_for_slot[sl] = ctx.milestone("pe")
    ctx.barrier("transpose_out", queues=env.ALLQ)


def emit_ln(env, ar, src_bf16_tiles=None, gamma_beta_cols=None, dst_tiles=None,
            to_hbm=None, keep_rows=None):
    ctx, T, TC = env.ctx, env.T, env.TC
    dst = dst_tiles or env.xn
    NH = 2 if T >= 2048 else 1
    HW = T // NH
    CHW = HW // 512
    xsq = [ar.bf(HW) for _ in range(8)]
    bc_mu = ar.f32(HW)
    bc_rs = ar.f32(HW)
    s1 = ar.f32(T)
    s2 = ar.f32(T)
    PS = env.PS

    if src_bf16_tiles is None:
        ld = [env.qwt.issue(lambda e, k=k: e.dma_start(
            out=dst[k], in_=env.resid[k * 128:(k + 1) * 128, :])) for k in range(8)]
        srcs = dst
    else:
        ld = [None] * 8
        srcs = src_bf16_tiles

    ev1 = ev2 = None
    stats_pe = None
    for h in range(NH):
        sqd = [None] * 8
        for k in range(8):
            ctx.wait("act", ld[k])
            ctx.wait("act", stats_pe)
            sqd[k] = ctx.op("act", lambda e, k=k, h=h: e.activation(
                xsq[k], srcs[k][:, h * HW:(h + 1) * HW], AF.Square), inc="act")
        for k in range(8):
            ctx.wait("pe", sqd[k])
            ctx.wait("pe", ld[k])
        ctx.wait("pe", ctx.milestone("pool"))
        for c in range(CHW):
            ctx.wait("pe", ev1)
            ctx.wait("pe", ev2)
            for k in range(8):
                ctx.op("pe", lambda e, s=srcs[k], c=c, k=k, h=h: e.matmul(
                    PS[4][0:1, :], env.ones_c_b[:, 0:1],
                    s[:, h * HW + c * 512:h * HW + (c + 1) * 512],
                    start=(k == 0), stop=(k == 7)), inc="pe" if k == 7 else None)
                mm = ctx.op("pe", lambda e, c=c, k=k: e.matmul(
                    PS[5][0:1, :], env.ones_c_b[:, 0:1], xsq[k][:, c * 512:(c + 1) * 512],
                    start=(k == 0), stop=(k == 7)), inc="pe" if k == 7 else None)
            ctx.wait("dve", mm)
            gc0 = h * HW + c * 512
            ev1 = ctx.op("dve", lambda e, g=gc0: e.tensor_copy(
                s1[0:1, g:g + 512], PS[4][0:1, :]), inc="dve")
            ev2 = ctx.op("dve", lambda e, g=gc0: e.tensor_copy(
                s2[0:1, g:g + 512], PS[5][0:1, :]), inc="dve")
        stats_pe = ctx.milestone("pe")
    ctx.wait("dve", ev1)
    ctx.wait("dve", ev2)
    ctx.op("dve", lambda e: e.tensor_scalar(s1[0:1, 0:T], s1[0:1, 0:T], 1.0 / 1024.0, None, ALU.mult))
    ctx.drain("dve")
    ctx.op("dve", lambda e: e.tensor_scalar(s2[0:1, 0:T], s2[0:1, 0:T], 1.0 / 1024.0, None, ALU.mult))
    ctx.drain("dve")
    vsub = None
    for h in range(NH):
        ctx.op("dve", lambda e, h=h: e.tensor_mul(
            bc_mu[0:1, 0:HW], s1[0:1, h * HW:(h + 1) * HW], s1[0:1, h * HW:(h + 1) * HW]))
        ctx.drain("dve")
        vsub = ctx.op("dve", lambda e, h=h: e.tensor_sub(
            s2[0:1, h * HW:(h + 1) * HW], s2[0:1, h * HW:(h + 1) * HW], bc_mu[0:1, 0:HW]),
            inc="dve")
        ctx.drain("dve")
    if keep_rows is not None:
        ctx.op("dve", lambda e: e.tensor_scalar(
            keep_rows[0][0:1, 0:T], s1[0:1, 0:T], -1.0, None, ALU.mult))
        ctx.drain("dve")
    ctx.wait("act", vsub)
    va = ctx.op("act", lambda e: e.activation(s2[0:1, 0:T], s2[0:1, 0:T], AF.Sqrt, bias=LN_EPS), inc="act")
    ctx.wait("dve", va)
    if keep_rows is not None:
        ctx.op("dve", lambda e: e.tensor_copy(keep_rows[1][0:1, 0:T], s2[0:1, 0:T]))
        ctx.drain("dve")
    vd = ctx.op("dve", lambda e: e.reciprocal(s2[0:1, 0:T], s2[0:1, 0:T]), inc="dve")
    ctx.wait("pe", vd)
    nrm_last = None
    nrm_done = [[None] * 8 for _ in range(NH)]
    st_tok = [None] * 8
    for h in range(NH):
        bce = None
        for c in range(CHW):
            if h > 0 or c > 0:
                ctx.wait("pe", nrm_last)
            ctx.op("pe", lambda e, h=h, c=c: e.matmul(
                PS[4][:, :], env.ones_r_f[0:1, :], s2[0:1, h * HW + c * 512:h * HW + (c + 1) * 512],
                start=True, stop=True), inc="pe")
            mm = ctx.op("pe", lambda e, h=h, c=c: e.matmul(
                PS[5][:, :], env.ones_r_f[0:1, :], s1[0:1, h * HW + c * 512:h * HW + (c + 1) * 512],
                start=True, stop=True), inc="pe")
            ctx.wait("dve", mm)
            ctx.op("dve", lambda e, c=c: e.tensor_copy(bc_rs[:, c * 512:(c + 1) * 512], PS[4][:, :]), inc="dve")
            bce = ctx.op("dve", lambda e, c=c: e.tensor_copy(bc_mu[:, c * 512:(c + 1) * 512], PS[5][:, :]), inc="dve")
            for k in range(8):
                src_k = srcs[k]
                gsp = h * HW + c * 512
                ctx.drain("dve")
                ctx.op("dve", lambda e, s=src_k, c=c, g=gsp: e.tensor_sub(
                    PS[6][:, :], s[:, g:g + 512], bc_mu[:, c * 512:(c + 1) * 512]))
                ctx.drain("dve")
                if gamma_beta_cols is None:
                    nrm_done[h][k] = ctx.op("dve", lambda e, k=k, c=c, g=gsp: e.tensor_mul(
                        dst[k][:, g:g + 512], PS[6][:, :], bc_rs[:, c * 512:(c + 1) * 512]),
                        inc="dve")
                else:
                    gcl, bcl = gamma_beta_cols
                    ctx.op("dve", lambda e, c=c: e.tensor_mul(
                        PS[6][:, :], PS[6][:, :], bc_rs[:, c * 512:(c + 1) * 512]))
                    ctx.drain("dve")
                    nrm_done[h][k] = ctx.op("dve", lambda e, k=k, g=gsp, gg=gcl[k], bb=bcl[k]: e.tensor_scalar(
                        dst[k][:, g:g + 512], PS[6][:, :], gg, bb, ALU.mult, ALU.add),
                        inc="dve")
                ctx.drain("dve")
            nrm_last = ctx.milestone("dve")
    if to_hbm is not None:
        for k in range(8):
            ctx.wait("sp", nrm_done[NH - 1][k])
            ctx.wait("sp", nrm_done[0][k])
            st_tok[k] = env.qsp.issue(lambda e, k=k: e.dma_start(
                out=to_hbm[k * 128:(k + 1) * 128, :], in_=dst[k]))
    ctx.barrier("ln", queues=env.ALLQ)


def load_vec_cols(env, dram_vec, dst_block, n, pk=None):
    ctx = env.ctx
    pk = pk if pk is not None else env.PS[6]
    ctx.wait("sp", env.stage_free)
    ld = env.qsp.issue(lambda e: e.dma_start(
        out=env.stage[0:n, 0:128], in_=dram_vec.rearrange("(j p) -> j p", p=128)))
    ctx.wait("pe", ld)
    ctx.wait("pe", env.lvc_evac)
    pev = ctx.op("pe", lambda e: e.transpose(pk[:, 0:n], env.stage[0:n, 0:128],
                                             env.id_f[0:n, 0:n]), inc="pe")
    env.stage_free = pev
    ctx.wait("dve", pev)
    tok = ctx.op("dve", lambda e: e.tensor_copy(dst_block, pk[:, 0:n]), inc="dve")
    env.lvc_evac = tok
    return tok


def load_cols(env, ctx, dram_vec, dst_sr_cols, n=8):
    toks = []
    for k in range(n):
        toks.append(env.qsp.issue(lambda e, k=k: e.dma_start(
            out=dst_sr_cols[k], in_=dram_vec[k * 128:(k + 1) * 128].rearrange("(p o) -> p o", o=1))))
    return toks


def emit_l1_ffn(env):
    ctx, T, TC = env.ctx, env.T, env.TC
    ar = work_arena(env)
    emit_ln(env, ar)
    ar2 = work_arena(env)
    cbl = [ar2.bf(1024) for _ in range(8)]
    cbt = [ar2.bf(1024) for _ in range(8)]
    outs = [ar2.f32(512) for _ in range(4)]
    scb = [env.SR[:, k:k + 1] for k in range(8)]
    bia = [env.SR[:, 8 + k:9 + k] for k in range(8)]
    ld_cb = []
    for ntile in range(8):
        ld_cb.append(env.qwt.issue(lambda e, n=ntile: e.dma_start(
            out=cbl[n], in_=env.D["L1f_CB"][n * 128:(n + 1) * 128, :])))
    t1 = load_cols(env, ctx, env.D["L1f_SCB"], scb)
    t2 = load_cols(env, ctx, env.D["L1f_bias"], bia)
    for t in t1 + t2:
        ctx.wait("act", t)
    va = ctx.op("act", lambda e: e.activation(
        env.SR[:, 0:8], env.SR[:, 0:8], AF.Copy, scale=1.0 / 127.0), inc="act")
    ev = None
    for ntile in range(8):
        ctx.wait("pe", ld_cb[ntile])
        for k in range(8):
            ctx.wait("pe", ev)
            pev = ctx.op("pe", lambda e, n=ntile, k=k: e.transpose(
                env.PB[:, 0:128], cbl[n][:, k * 128:(k + 1) * 128], env.id_b[:]), inc="pe")
            ctx.wait("dve", pev)
            ev = ctx.op("dve", lambda e, n=ntile, k=k: e.tensor_copy(
                cbt[k][:, n * 128:(n + 1) * 128], env.PB[:, 0:128]), inc="dve")
    ctx.wait("pe", ev)
    ctx.wait("pe", va)
    ctx.wait("dve", va)
    ev_done = [None, None]
    st_done = [None] * 4
    for c in range(TC):
        for ntile in range(8):
            pk = env.PS[(c * 8 + ntile) % 2]
            ctx.wait("pe", ev_done[(c * 8 + ntile) % 2])
            for k in range(8):
                mm = ctx.op("pe", lambda e, n=ntile, k=k, c=c, pk=pk: e.matmul(
                    pk[:, :], cbt[k][:, n * 128:(n + 1) * 128],
                    env.xn[k][:, c * 512:(c + 1) * 512],
                    start=(k == 0), stop=(k == 7)), inc="pe" if k == 7 else None)
            o_i = (c * 8 + ntile) % 4
            ctx.wait("dve", mm)
            ctx.wait("dve", st_done[o_i])
            ev_done[(c * 8 + ntile) % 2] = ctx.op("dve", lambda e, pk=pk, o=outs[o_i], n=ntile: e.tensor_scalar(
                o, pk[:, :], scb[n], bia[n], ALU.mult, ALU.add), inc="dve")
            ctx.wait("pool", ev_done[(c * 8 + ntile) % 2])
            st_done[o_i] = env.qac.issue(lambda e, o=outs[o_i], n=ntile, c=c: e.dma_start(
                out=env.resid[n * 128:(n + 1) * 128, c * 512:(c + 1) * 512], in_=o,
                accum_op=ALU.add))
    ctx.barrier("l1ffn", queues=env.ALLQ)


def emit_l1_attn_a(env):
    ctx, T, TC = env.ctx, env.T, env.TC
    ar = work_arena(env)
    emit_ln(env, ar)
    ar2 = work_arena(env)
    w1 = [ar2.bf(1024) for _ in range(8)]
    w2 = [ar2.bf(1024) for _ in range(8)]
    etl = ar2.f32(1024)
    ett = [ar2.f32(21) for _ in range(8)]
    accD = ar2.f32(512)
    accP = ar2.f32(512)
    h = [ar2.bf(512) for _ in range(8)]
    outs = [ar2.f32(512) for _ in range(4)]
    bia = [env.SR[:, 16 + k:17 + k] for k in range(8)]

    ldw = []
    for k in range(8):
        ldw.append(env.qwt.issue(lambda e, k=k: e.dma_start(
            out=w1[k], in_=env.D["L1a_W1"][k * 128:(k + 1) * 128, :])))
    for k in range(8):
        ldw.append(env.qwt.issue(lambda e, k=k: e.dma_start(
            out=w2[k], in_=env.D["L1a_W2"][k * 128:(k + 1) * 128, :])))
    ldb = load_cols(env, ctx, env.D["L1a_bias"], bia)
    lde = env.qsp.issue(lambda e: e.dma_start(out=etl[0:21, :], in_=env.D["L1a_et"][:, :]))
    ctx.wait("pe", lde)
    ev = None
    for k in range(8):
        pk = env.PS[k % 2]
        ctx.wait("pe", ev)
        pev = ctx.op("pe", lambda e, k=k, pk=pk: e.transpose(
            pk[:, 0:21], etl[0:21, k * 128:(k + 1) * 128], env.id_f[0:21, 0:21]), inc="pe")
        ctx.wait("dve", pev)
        ev = ctx.op("dve", lambda e, k=k, pk=pk: e.tensor_copy(ett[k], pk[:, 0:21]), inc="dve")
    conv_barrier = ev

    mm_prev = None
    ev_done = [None, None]
    st_done = [None] * 4
    h_done = {}
    for c in range(TC):
        for k in range(8):
            eng = "dve"
            acc = accD if k % 2 == 0 else accP
            ctx.wait(eng, conv_barrier)
            ctx.wait(eng, mm_prev)
            ctx.drain(eng)
            ctx.op(eng, lambda e, acc=acc: e.memset(acc, 0.0))
            ctx.drain(eng)
            for j in range(1, 21):
                lo = c * 512 - j
                if lo + 512 <= 0:
                    continue
                srccol = max(0, lo)
                dstcol = c * 512 - j - srccol
                width = 512 - (srccol - lo)
                dst0 = 512 - width
                ctx.op(eng, lambda e, acc=acc, k=k, j=j, srccol=srccol, w=width, d0=dst0:
                       e.scalar_tensor_tensor(acc[:, d0:d0 + w], env.xn[k][:, srccol:srccol + w],
                                              ett[k][:, j:j + 1], acc[:, d0:d0 + w],
                                              ALU.mult, ALU.add))
                ctx.drain(eng)
            h_done[(c, k)] = ctx.op(eng, lambda e, acc=acc, k=k: e.tensor_copy(h[k], acc),
                                    inc=eng)
        for t in ldw:
            ctx.wait("pe", t)
        for k in range(8):
            ctx.wait("pe", h_done[(c, k)])
        for mt in range(8):
            pk = env.PS[(c * 8 + mt) % 2]
            ctx.wait("pe", ev_done[(c * 8 + mt) % 2])
            for k in range(8):
                ctx.op("pe", lambda e, k=k, mt=mt, c=c, pk=pk: e.matmul(
                    pk[:, :], w1[k][:, mt * 128:(mt + 1) * 128],
                    env.xn[k][:, c * 512:(c + 1) * 512],
                    start=(k == 0), stop=False))
            for k in range(8):
                mm = ctx.op("pe", lambda e, k=k, mt=mt, c=c, pk=pk: e.matmul(
                    pk[:, :], w2[k][:, mt * 128:(mt + 1) * 128], h[k],
                    start=False, stop=(k == 7)), inc="pe" if k == 7 else None)
            o_i = (c * 8 + mt) % 4
            ctx.wait("dve", mm)
            ctx.wait("dve", st_done[o_i])
            for t in ldb:
                ctx.wait("dve", t)
            ev_done[(c * 8 + mt) % 2] = ctx.op("dve", lambda e, pk=pk, o=outs[o_i], mt=mt:
                e.tensor_scalar(o, pk[:, :], bia[mt], None, ALU.add), inc="dve")
            ctx.wait("pool", ev_done[(c * 8 + mt) % 2])
            st_done[o_i] = env.qac.issue(lambda e, o=outs[o_i], mt=mt, c=c: e.dma_start(
                out=env.resid[mt * 128:(mt + 1) * 128, c * 512:(c + 1) * 512], in_=o,
                accum_op=ALU.add))
        mm_prev = ctx.milestone("pe")
    ctx.barrier("l1attn_a", queues=env.ALLQ)


def emit_l0_attn_a(env):
    ctx, T, TC, NT = env.ctx, env.T, env.TC, env.NT
    ar = work_arena(env)
    emit_ln(env, ar)
    ar2 = work_arena(env)
    wql = [ar2.bf(1024) for _ in range(8)]
    wqt = [ar2.bf(1024) for _ in range(8)]
    qt = [ar2.bf(512) for _ in range(8)]
    ab = [ar2.bf(512) for _ in range(4)]
    ldw = []
    for ot in range(8):
        ldw.append(env.qwt.issue(lambda e, o=ot: e.dma_start(
            out=wql[o], in_=env.D["L0a_Wq"][o * 128:(o + 1) * 128, :])))
    ev = None
    for ot in range(8):
        ctx.wait("pe", ldw[ot])
        for hc in range(8):
            ctx.wait("pe", ev)
            pev = ctx.op("pe", lambda e, o=ot, hc=hc: e.transpose(
                env.PB[:, 0:128], wql[o][:, hc * 128:(hc + 1) * 128], env.id_b[:]), inc="pe")
            ctx.wait("dve", pev)
            ev = ctx.op("dve", lambda e, o=ot, hc=hc: e.tensor_copy(
                wqt[hc][:, o * 128:(o + 1) * 128], env.PB[:, 0:128]), inc="dve")
    ctx.wait("pe", ev)
    qt_done = [None] * 8
    ev_done = [None, None]
    st_done = [None] * 4
    for c in range(TC):
        for ot in range(8):
            pk = env.PS[ot % 2]
            ctx.wait("pe", qt_done[ot - 2] if ot >= 2 else None)
            for kh in range(8):
                mm = ctx.op("pe", lambda e, o=ot, kh=kh, c=c, pk=pk: e.matmul(
                    pk[:, :], wqt[kh][:, o * 128:(o + 1) * 128],
                    env.xn[kh][:, c * 512:(c + 1) * 512],
                    start=(kh == 0), stop=(kh == 7)), inc="pe" if kh == 7 else None)
            ctx.wait("act", mm)
            qt_done[ot] = ctx.op("act", lambda e, o=ot, pk=pk: e.copy(qt[o], pk[:, :]), inc="act")
        for ot in range(8):
            ctx.wait("pe", qt_done[ot])
        for tt in range(NT):
            pk = env.PS[2 + tt % 2]
            ctx.wait("pe", ev_done[tt % 2])
            for oc in range(8):
                mm = ctx.op("pe", lambda e, oc=oc, tt=tt, pk=pk: e.matmul(
                    pk[:, :], env.xn[oc][:, tt * 128:(tt + 1) * 128], qt[oc],
                    start=(oc == 0), stop=(oc == 7)), inc="pe" if oc == 7 else None)
            b_i = tt % 4
            ctx.wait("dve", mm)
            ctx.wait("dve", st_done[b_i])
            ev_done[tt % 2] = ctx.op("dve", lambda e, pk=pk, b=ab[b_i]: e.tensor_copy(
                b, pk[:, :]), inc="dve")
            ctx.wait("sp", ev_done[tt % 2])
            st_done[b_i] = env.qsp.issue(lambda e, b=ab[b_i], tt=tt, c=c: e.dma_start(
                out=env.s_aT[tt * 128:(tt + 1) * 128, c * 512:(c + 1) * 512], in_=b))
    ctx.barrier("l0a_ab", queues=env.ALLQ)
    ar3 = work_arena(env)
    xtok = [ar3.bf(1024) for _ in range(NT)]
    abuf = [ar3.bf(NT * 256) for _ in range(2)]
    outs = [ar3.f32(512) for _ in range(4)]
    ev = None
    for oc in range(8):
        for tt in range(NT):
            ctx.wait("pe", ev)
            pev = ctx.op("pe", lambda e, oc=oc, tt=tt: e.transpose(
                env.PB[:, 0:128], env.xn[oc][:, tt * 128:(tt + 1) * 128], env.id_b[:]), inc="pe")
            ctx.wait("dve", pev)
            ev = ctx.op("dve", lambda e, oc=oc, tt=tt: e.tensor_copy(
                xtok[tt][:, oc * 128:(oc + 1) * 128], env.PB[:, 0:128]), inc="dve")
    ctx.wait("pe", ev)
    HT = NT // 2
    ld_half = [None, None]
    mm_half_done = [None, None]
    ev_done = [None, None]
    st_done = [None] * 4
    for c in range(TC):
        for half in range(2):
            ctx.wait("sp", mm_half_done[half])
            ld_half[half] = env.qsp.issue(lambda e, c=c, h=half: e.dma_start(
                out=abuf[h].rearrange("p (i s) -> p i s", s=512),
                in_=env.s_aT[h * HT * 128:(h + 1) * HT * 128,
                             c * 512:(c + 1) * 512].rearrange("(i p) s -> p i s", p=128)))
        for ht in range(8):
            pk = env.PS[ht % 2]
            ctx.wait("pe", ev_done[ht % 2])
            for half in range(2):
                ctx.wait("pe", ld_half[half])
                for i in range(HT):
                    tt = half * HT + i
                    mm = ctx.op("pe", lambda e, ht=ht, tt=tt, i=i, h=half, pk=pk: e.matmul(
                        pk[:, :], xtok[tt][:, ht * 128:(ht + 1) * 128],
                        abuf[h][:, i * 512:(i + 1) * 512],
                        start=(tt == 0), stop=(tt == NT - 1)),
                        inc="pe" if tt == NT - 1 else None)
            mm_half_done[0] = mm_half_done[1] = ctx.milestone("pe")
            o_i = ht % 4
            ctx.wait("dve", mm)
            ctx.wait("dve", st_done[o_i])
            ev_done[ht % 2] = ctx.op("dve", lambda e, pk=pk, o=outs[o_i]: e.tensor_copy(
                o, pk[:, :]), inc="dve")
            ctx.wait("pool", ev_done[ht % 2])
            st_done[o_i] = env.qac.issue(lambda e, o=outs[o_i], ht=ht, c=c: e.dma_start(
                out=env.resid[ht * 128:(ht + 1) * 128, c * 512:(c + 1) * 512], in_=o,
                accum_op=ALU.add))
    ctx.barrier("l0a", queues=env.ALLQ)



def emit_l0_ffn(env):
    ctx, T, TC, NT = env.ctx, env.T, env.TC, env.NT
    TCM = T // 256
    PS = env.PS
    AXX = mybir.AxisListType.X
    MGT_COLS = 2 * 8 * NT + 8
    arp = Arena(env.AR, env.ARCOLS - MGT_COLS, env.ARCOLS)
    mgt = arp.f32(8 * NT)
    ar_rows = Arena(env.AR, env.WORK_BASE, env.ARCOLS - MGT_COLS)
    rowsA = ar_rows.f32(T)
    rowsB = ar_rows.f32(T)
    ar_ln = Arena(env.AR, ar_rows.off, env.ARCOLS - MGT_COLS)
    emit_ln(env, ar_ln, keep_rows=(rowsA, rowsB))
    arR = Arena(env.AR, ar_rows.off, env.ARCOLS - MGT_COLS)
    wr = arR.f32(64)
    brrow = arR.f32(8)
    wsumr = arR.f32(8)
    lgc = [arR.f32(512) for _ in range(2)]
    ltok = arR.f32(8 * NT)
    rstdtok = arR.f32(NT)
    emt = arR.f32(8)
    cls = arR.f32(4)
    xfR = [arR.f32(T) for _ in range(2)]
    c_lmax, c_nlm, c_sum, c_gate = (cls[:, j:j + 1] for j in range(4))

    ldwr = env.qsp.issue(lambda e: e.dma_start(
        out=wr.rearrange("p (k e) -> p k e", e=8),
        in_=env.D["L0f_Wr"].rearrange("(k p) e -> p k e", p=128)))
    ldbr = env.qsp.issue(lambda e: e.dma_start(
        out=brrow[0:1, 0:8], in_=env.D["L0f_br"].rearrange("(o e) -> o e", o=1)))
    ctx.wait("pe", ldwr)
    for k in range(8):
        mm = ctx.op("pe", lambda e, k=k: e.matmul(
            PS[5][0:1, 0:8], env.ones_c_f[:, 0:1], wr[:, k * 8:(k + 1) * 8],
            start=(k == 0), stop=(k == 7)), inc="pe" if k == 7 else None)
    ctx.wait("dve", mm)
    vw = ctx.op("dve", lambda e: e.tensor_copy(wsumr[0:1, 0:8], PS[5][0:1, 0:8]), inc="dve")
    rtr = None
    for tt in range(NT):
        ctx.wait("pe", rtr)
        pev = ctx.op("pe", lambda e, tt=tt: e.transpose(
            PS[6][:, 0:1], rowsB[0:1, tt * 128:(tt + 1) * 128], env.id_f[0:1, 0:1]), inc="pe")
        ctx.wait("dve", pev)
        rtr = ctx.op("dve", lambda e, tt=tt: e.tensor_copy(
            rstdtok[:, tt:tt + 1], PS[6][:, 0:1]), inc="dve")
    ctx.wait("dve", rtr)
    rtr = ctx.op("dve", lambda e: e.reciprocal(rstdtok[:, 0:NT], rstdtok[:, 0:NT]), inc="dve")
    import os as _os
    NH = 2 if TC >= int(_os.environ.get("RNH", "4")) else 1
    CH = TC // NH
    HW = T // NH
    ctx.wait("pe", vw)
    ctx.wait("pe", ldbr)
    slot_pe = [None, None]
    lg_evac = {}
    tok_ev = None
    for h in range(NH):
        ld = [None, None]
        for k in range(8):
            sl = k % 2
            ctx.wait("sp", slot_pe[sl])
            ld[sl] = env.qsp.issue(lambda e, k=k, h=h, s=xfR[sl]: e.dma_start(
                out=s[:, 0:HW], in_=env.resid[k * 128:(k + 1) * 128, h * HW:(h + 1) * HW]))
            ctx.wait("pe", ld[sl])
            if k == 0:
                for cl in range(CH):
                    ctx.wait("pe", lg_evac.get(cl))
            for cl in range(CH):
                mm = ctx.op("pe", lambda e, k=k, cl=cl, s=xfR[sl]: e.matmul(
                    PS[cl][0:8, :], wr[:, k * 8:(k + 1) * 8], s[:, cl * 512:(cl + 1) * 512],
                    start=(k == 0), stop=False), inc="pe" if cl == CH - 1 else None)
            slot_pe[sl] = mm
        for cl in range(CH):
            span0 = h * HW + cl * 512
            ctx.op("pe", lambda e, cl=cl, s0=span0: e.matmul(
                PS[cl][0:8, :], wsumr[0:1, 0:8], rowsA[0:1, s0:s0 + 512],
                start=False, stop=False))
            mm = ctx.op("pe", lambda e, cl=cl, s0=span0: e.matmul(
                PS[cl][0:8, :], brrow[0:1, 0:8], rowsB[0:1, s0:s0 + 512],
                start=False, stop=True), inc="pe")
            ctx.wait("dve", mm)
            vlg = ctx.op("dve", lambda e, cl=cl, b=lgc[cl % 2]: e.tensor_copy(
                b[0:8, :], PS[cl][0:8, :]), inc="dve")
            lg_evac[cl] = vlg
            for j in range(4):
                tt = (span0 // 128) + j
                ctx.wait("pe", vlg)
                ctx.wait("pe", tok_ev)
                pev = ctx.op("pe", lambda e, j=j, b=lgc[cl % 2]: e.transpose(
                    PS[4][:, 0:8], b[0:8, j * 128:(j + 1) * 128], env.id_f[0:8, 0:8]), inc="pe")
                ctx.wait("dve", pev)
                tok_ev = ctx.op("dve", lambda e, tt=tt: e.tensor_copy(
                    ltok[:, tt * 8:(tt + 1) * 8], PS[4][:, 0:8]), inc="dve")
    ctx.wait("dve", rtr)
    ctx.wait("dve", tok_ev)
    mg_done = None
    for tt in range(NT):
        lt = ltok[:, tt * 8:(tt + 1) * 8]
        ctx.drain("dve")
        ctx.op("dve", lambda e, lt=lt, tt=tt: e.tensor_scalar(
            lt, lt, rstdtok[:, tt:tt + 1], None, ALU.mult))
        ctx.drain("dve")
        ctx.op("dve", lambda e, lt=lt: e.reduce_max(c_lmax, lt, AXX))
        ctx.drain("dve")
        va = ctx.op("dve", lambda e: e.tensor_scalar(c_nlm, c_lmax, -1.0, None, ALU.mult),
                    inc="dve")
        ctx.wait("act", va)
        vex = ctx.op("act", lambda e, lt=lt: e.activation(
            emt[:, 0:8], lt, AF.Exp, bias=c_nlm), inc="act")
        ctx.wait("dve", vex)
        ctx.op("dve", lambda e: e.reduce_sum(c_sum, emt[:, 0:8], AXX))
        ctx.drain("dve")
        ctx.op("dve", lambda e: e.reciprocal(c_gate, c_sum))
        ctx.drain("dve")
        mg_done = ctx.op("dve", lambda e, lt=lt, tt=tt: e.tensor_scalar(
            mgt[:, tt * 8:(tt + 1) * 8], lt, c_lmax, c_gate, ALU.is_equal, ALU.mult),
            inc="dve")
    ctx.barrier("l0f_router", queues=env.ALLQ)
    gtok = load_vec_cols(env, env.D["L0f_g"], env.SR[:, 24:32], 8)
    btok = load_vec_cols(env, env.D["L0f_b"], env.SR[:, 32:40], 8)
    ctx.wait("dve", gtok)
    ctx.wait("dve", btok)
    gcols = [env.SR[:, 24 + k:25 + k] for k in range(8)]
    bcols = [env.SR[:, 32 + k:33 + k] for k in range(8)]
    ar_I = Arena(env.AR, ar_rows.off, env.ARCOLS - MGT_COLS)
    emit_ln(env, ar_I, src_bf16_tiles=env.xn, gamma_beta_cols=(gcols, bcols),
            dst_tiles=env.xn, to_hbm=env.xn_hbm)
    ar_E = Arena(env.AR, 0, env.ARCOLS - MGT_COLS)
    w1 = [ar_E.bf(4096) for _ in range(8)]
    w2 = [ar_E.bf(1024) for _ in range(32)]
    hT = [ar_E.bf(256) for _ in range(32)]
    xinA = [ar_E.bf(2048) for _ in range(2)]
    outs = [ar_E.f32(256) for _ in range(4)]
    mrow = ar_E.f32(T)
    mbc = [ar_E.f32(256) for _ in range(2)]
    b1c = [env.SR[:, 48 + j:49 + j] for j in range(32)]
    b2c = [env.SR[:, 40 + j:41 + j] for j in range(8)]
    h_ev = [None, None]
    y_ev = [None, None]
    st_done = [None] * 4
    mb_ev = [None, None]
    xin_pe = [None, None]
    mrow_done = None
    w_pe = None
    bias_users = None
    for ex in range(8):
        ldw1, ldw2 = [], []
        if ex > 0:
            ctx.wait("pool", w_pe)
            ctx.wait("sp", bias_users)
        for kd in range(8):
            ldw1.append(env.qwt.issue(lambda e, ex=ex, kd=kd: e.dma_start(
                out=w1[kd], in_=env.D["L0f_W1"][ex, kd * 128:(kd + 1) * 128, :])))
        for ft in range(32):
            ldw2.append(env.qwt.issue(lambda e, ex=ex, ft=ft: e.dma_start(
                out=w2[ft], in_=env.D["L0f_W2"][ex, ft * 128:(ft + 1) * 128, :])))
        t1 = load_vec_cols(env, env.D["L0f_b1"][ex, :], env.SR[:, 48:80], 32, pk=PS[6])
        t2 = load_vec_cols(env, env.D["L0f_b2"][ex, :], env.SR[:, 40:48], 8, pk=PS[6])
        ctx.wait("pe", mg_done)
        for tt in range(NT):
            ctx.wait("pe", mrow_done)
            pev = ctx.op("pe", lambda e, tt=tt, ex=ex: e.transpose(
                PS[5][0:1, 0:128], mgt[:, tt * 8 + ex:tt * 8 + ex + 1], env.id_f[:, :]), inc="pe")
            ctx.wait("dve", pev)
            mrow_done = ctx.op("dve", lambda e, tt=tt: e.tensor_copy(
                mrow[0:1, tt * 128:(tt + 1) * 128], PS[5][0:1, 0:128]), inc="dve")
        for t in ldw1 + ldw2:
            ctx.wait("pe", t)
        ctx.wait("act", t1)
        ctx.wait("dve", t2)
        ctx.wait("pe", mrow_done)
        for tc in range(TCM):
            sl = tc % 2
            ctx.wait("sp", xin_pe[sl])
            ldx = env.qsp.issue(lambda e, tc=tc, sl=sl: e.dma_start(
                out=xinA[sl].rearrange("p (k t) -> p k t", t=256),
                in_=env.xn_hbm[:, tc * 256:(tc + 1) * 256].rearrange("(k p) t -> p k t", p=128)))
            ctx.wait("pe", ldx)
            ctx.wait("pe", mb_ev[sl])
            mmb = ctx.op("pe", lambda e, tc=tc: e.matmul(
                PS[4][:, 0:256], env.ones_r_f[0:1, :], mrow[0:1, tc * 256:(tc + 1) * 256],
                start=True, stop=True), inc="pe")
            ctx.wait("dve", mmb)
            mb_ev[sl] = ctx.op("dve", lambda e, sl=sl: e.tensor_copy(
                mbc[sl], PS[4][:, 0:256]), inc="dve")
            for ft in range(32):
                pk = PS[ft % 2]
                ctx.wait("pe", h_ev[ft % 2])
                for kd in range(8):
                    mm = ctx.op("pe", lambda e, ft=ft, kd=kd, sl=sl, pk=pk: e.matmul(
                        pk[:, 0:256], w1[kd][:, ft * 128:(ft + 1) * 128],
                        xinA[sl][:, kd * 256:(kd + 1) * 256],
                        start=(kd == 0), stop=(kd == 7)), inc="pe" if kd == 7 else None)
                ctx.wait("act", mm)
                h_ev[ft % 2] = ctx.op("act", lambda e, ft=ft, pk=pk: e.activation(
                    hT[ft], pk[:, 0:256], AF.Relu, bias=b1c[ft]), inc="act")
            for ft in range(32):
                ctx.wait("pe", h_ev[ft % 2])
            ctx.wait("dve", mb_ev[sl])
            for dt_ in range(8):
                pk = PS[2 + dt_ % 2]
                ctx.wait("pe", y_ev[dt_ % 2])
                for ft in range(32):
                    mm = ctx.op("pe", lambda e, ft=ft, dt_=dt_, pk=pk: e.matmul(
                        pk[:, 0:256], w2[ft][:, dt_ * 128:(dt_ + 1) * 128], hT[ft],
                        start=(ft == 0), stop=(ft == 31)), inc="pe" if ft == 31 else None)
                o_i = (tc * 8 + dt_) % 4
                ctx.wait("dve", mm)
                ctx.wait("dve", st_done[o_i])
                y_ev[dt_ % 2] = ctx.op("dve", lambda e, pk=pk, o=outs[o_i], dt_=dt_, sl=sl:
                    e.scalar_tensor_tensor(o, pk[:, 0:256], b2c[dt_], mbc[sl],
                                           ALU.add, ALU.mult), inc="dve")
                ctx.wait("pool", y_ev[dt_ % 2])
                st_done[o_i] = env.qac.issue(lambda e, o=outs[o_i], dt_=dt_, tc=tc: e.dma_start(
                    out=env.resid[dt_ * 128:(dt_ + 1) * 128, tc * 256:(tc + 1) * 256], in_=o,
                    accum_op=ALU.add))
            xin_pe[sl] = ctx.milestone("pe")
        w_pe = ctx.milestone("pe")
        bias_users = ctx.milestone("dve")
    ctx.barrier("l0f", queues=env.ALLQ)



def emit_l0_attn_b(env):
    ctx, T, TC, NT = env.ctx, env.T, env.TC, env.NT
    PS = env.PS
    AXX = mybir.AxisListType.X
    ar = work_arena(env)
    emit_ln(env, ar)
    arP = work_arena(env)
    cosb = arP.f32(T)
    sinb = arP.f32(T)
    flipb = arP.bf(128)
    wq = [arP.bf(128) for _ in range(8)]
    wk = [arP.bf(128) for _ in range(8)]
    wv = [arP.bf(128) for _ in range(8)]
    qb = arP.bf(512)
    t1 = arP.f32(512)
    t2 = arP.f32(512)
    rot = [arP.bf(512) for _ in range(2)]
    bqc = env.SR[:, 0:8]
    bkc = env.SR[:, 8:16]
    bvc = env.SR[:, 16:24]
    lc1 = env.qsp.issue(lambda e: e.dma_start(out=cosb, in_=env.D["c_rope_cos"][:, :]))
    lc2 = env.qsp.issue(lambda e: e.dma_start(out=sinb, in_=env.D["c_rope_sin"][:, :]))
    lc3 = env.qwt.issue(lambda e: e.dma_start(out=flipb, in_=env.D["c_ropeflip"][:, :]))
    tb1 = load_vec_cols(env, env.D["L0b_bQ"].rearrange("h d -> (h d)"), bqc, 8)
    tb2 = load_vec_cols(env, env.D["L0b_bK"].rearrange("h d -> (h d)"), bkc, 8)
    tb3 = load_vec_cols(env, env.D["L0b_bV"].rearrange("h d -> (h d)"), bvc, 8)
    ctx.wait("dve", lc1)
    ctx.wait("dve", lc2)
    ctx.wait("pe", lc3)
    w_pe = None
    st_tok = [None, None]
    pk_free = [None, None]
    for hp in range(8):
        ldq, ldk, ldv = [], [], []
        ctx.wait("pool", w_pe)
        for kd in range(8):
            for wt, nm, lst in ((wq, "L0b_WQ", ldq), (wk, "L0b_WK", ldk), (wv, "L0b_WV", ldv)):
                for j in range(2):
                    lst.append(env.qwt.issue(lambda e, wt=wt, nm=nm, kd=kd, j=j, hp=hp: e.dma_start(
                        out=wt[kd][:, j * 64:(j + 1) * 64],
                        in_=env.D[nm][2 * hp + j, kd * 128:(kd + 1) * 128, :])))
        for t in ldq + ldk + ldv:
            ctx.wait("pe", t)
        ctx.wait("dve", tb1); ctx.wait("dve", tb2); ctx.wait("dve", tb3)
        for tc in range(TC):
            for mat, wt, bc, dst, dorope in (
                    (0, wq, bqc, env.s_qT, True), (1, wk, bkc, env.s_kT, True),
                    (2, wv, bvc, env.s_vT, False)):
                pk = PS[mat % 2]
                ctx.wait("pe", pk_free[mat % 2])
                for kd in range(8):
                    mm = ctx.op("pe", lambda e, wt=wt, kd=kd, tc=tc, pk=pk: e.matmul(
                        pk[:, :], wt[kd], env.xn[kd][:, tc * 512:(tc + 1) * 512],
                        start=(kd == 0), stop=(kd == 7)), inc="pe" if kd == 7 else None)
                ctx.wait("dve", mm)
                ctx.drain("dve")
                va = ctx.op("dve", lambda e, pk=pk, bc=bc, hp=hp: e.tensor_scalar(
                    qb, pk[:, :], bc[:, hp:hp + 1], None, ALU.add), inc="dve")
                pk_free[mat % 2] = va
                if dorope:
                    pf = PS[4]
                    ctx.wait("pe", va)
                    mmf = ctx.op("pe", lambda e, pf=pf: e.matmul(
                        pf[:, :], flipb, qb, start=True, stop=True), inc="pe")
                    ctx.wait("dve", va)
                    ctx.wait("dve", mmf)
                    ctx.op("dve", lambda e, tc=tc: e.tensor_mul(
                        t1, qb, cosb[:, tc * 512:(tc + 1) * 512]))
                    ctx.drain("dve")
                    ctx.op("dve", lambda e, tc=tc, pf=pf: e.tensor_mul(
                        t2, pf[:, :], sinb[:, tc * 512:(tc + 1) * 512]))
                    ctx.drain("dve")
                    ctx.wait("dve", st_tok[mat % 2])
                    vr = ctx.op("dve", lambda e, mat=mat: e.tensor_add(rot[mat % 2], t1, t2), inc="dve")
                    src = rot[mat % 2]
                    ctx.wait("sp", vr)
                else:
                    ctx.wait("dve", va)
                    ctx.wait("dve", st_tok[0])
                    vr = ctx.op("dve", lambda e: e.tensor_copy(rot[0], qb), inc="dve")
                    src = rot[0]
                    ctx.wait("sp", vr)
                st_tok[mat % 2 if dorope else 0] = env.qsp.issue(
                    lambda e, src=src, dst=dst, hp=hp, tc=tc: e.dma_start(
                        out=dst[hp * 128:(hp + 1) * 128, tc * 512:(tc + 1) * 512], in_=src))
        w_pe = ctx.milestone("pe")
    ctx.barrier("l0b_proj", queues=env.ALLQ)
    arA = work_arena(env)
    ka = arA.bf(2 * T)
    va_ = arA.bf(T)
    qa = arA.bf(2 * T)
    vtok = [arA.bf(128) for _ in range(NT)]
    pbuf = [arA.bf(512) for _ in range(4)]
    sumr = [arA.f32(512), arA.f32(512)]
    rbc = arA.f32(512)
    chalf = [arA.f32(128), arA.f32(128)]
    zev = arA.bf(512)
    cmq = arA.f32(4 * 512)
    cmk = arA.f32(4 * 512)
    v1s = ctx.op("dve", lambda e: e.memset(ka[64:65, :], 1.0), inc="dve")
    ctx.wait("pe", v1s)
    ctx.op("pool", lambda e: e.memset(chalf[0][0:1, 0:64], 1.0))
    ctx.op("pool", lambda e: e.memset(chalf[0][0:1, 64:128], 0.0))
    ctx.op("pool", lambda e: e.memset(chalf[1][0:1, 0:64], 0.0))
    vhp = ctx.op("pool", lambda e: e.memset(chalf[1][0:1, 64:128], 1.0), inc="pool")
    lm1 = env.qsp.issue(lambda e: e.dma_start(
        out=cmq.rearrange("p (m s) -> p m s", s=512),
        in_=env.D["c_cmask_qk"].rearrange("(m p) s -> p m s", p=128)))
    lm2 = env.qsp.issue(lambda e: e.dma_start(
        out=cmk.rearrange("p (m s) -> p m s", s=512),
        in_=env.D["c_cmask_kq"].rearrange("(m p) s -> p m s", p=128)))
    ctx.wait("dve", lm1)
    ctx.wait("dve", lm2)
    zst = None
    zst_ev = [None, None]
    sum_free = [None, None]
    rbc_ev = None
    nq_ev = None
    arena_pe = None
    arena_dve = None
    for hp in range(8):
        ctx.wait("sp", arena_pe)
        ctx.wait("sp", zst)
        l1a = env.qsp.issue(lambda e, hp=hp: e.dma_start(
            out=ka[0:64, 0:T], in_=env.s_kT[hp * 128:hp * 128 + 64, :]))
        l1 = env.qsp.issue(lambda e, hp=hp: e.dma_start(
            out=ka[0:64, T:2 * T], in_=env.s_kT[hp * 128 + 64:(hp + 1) * 128, :]))
        l2 = env.qsp.issue(lambda e, hp=hp: e.dma_start(out=va_, in_=env.s_vT[hp * 128:(hp + 1) * 128, :]))
        l3a = env.qsp.issue(lambda e, hp=hp: e.dma_start(
            out=qa[0:64, 0:T], in_=env.s_qT[hp * 128:hp * 128 + 64, :]))
        l3 = env.qsp.issue(lambda e, hp=hp: e.dma_start(
            out=qa[0:64, T:2 * T], in_=env.s_qT[hp * 128 + 64:(hp + 1) * 128, :]))
        ctx.wait("pe", l2)
        ev = None
        for tt in range(NT):
            ctx.wait("pe", ev)
            pev = ctx.op("pe", lambda e, tt=tt: e.transpose(
                env.PB[:, 0:128], va_[:, tt * 128:(tt + 1) * 128], env.id_b[:]), inc="pe")
            ctx.wait("dve", pev)
            ctx.wait("dve", arena_pe)
            ev = ctx.op("dve", lambda e, tt=tt: e.tensor_copy(vtok[tt], env.PB[:, 0:128]), inc="dve")
        vtok_ev = ev
        ctx.wait("pe", l1a)
        ctx.wait("pe", l1)
        ctx.wait("pe", l3a)
        ctx.wait("pe", l3)
        bank_free = [None] * 4
        for hh in range(2):
            ho = hh * T
            for qt in range(NT):
                nkc = qt // 4 + 1
                for kc in range(nkc):
                    pk = PS[kc % 4]
                    ctx.wait("pe", bank_free[kc % 4])
                    mm = ctx.op("pe", lambda e, ho=ho, qt=qt, kc=kc, pk=pk: e.matmul(
                        pk[:, :], qa[0:64, ho + qt * 128:ho + (qt + 1) * 128],
                        ka[0:64, ho + kc * 512:ho + (kc + 1) * 512],
                        start=True, stop=True), inc="pe")
                    ctx.wait("dve", mm)
                    if kc == nkc - 1:
                        m = qt - 4 * kc
                        ctx.op("dve", lambda e, pk=pk, m=m: e.tensor_add(
                            pk[:, :], pk[:, :], cmq[:, m * 512:(m + 1) * 512]))
                        ctx.drain("dve")
                    ctx.drain("dve")
                    if kc == 0:
                        vm = ctx.op("dve", lambda e, pk=pk: e.reduce_max(
                            env.SR[:, 30:31], pk[:, :], AXX), inc="dve")
                    else:
                        ctx.op("dve", lambda e, pk=pk: e.reduce_max(
                            env.SR[:, 31:32], pk[:, :], AXX))
                        ctx.drain("dve")
                        vm = ctx.op("dve", lambda e: e.tensor_max(
                            env.SR[:, 30:31], env.SR[:, 30:31], env.SR[:, 31:32]), inc="dve")
                    bank_free[kc % 4] = vm
                ctx.drain("dve")
                vneg = ctx.op("dve", lambda e: e.tensor_scalar(
                    env.SR[:, 30:31], env.SR[:, 30:31], -1.0, None, ALU.mult), inc="dve")
                ctx.wait("pe", vneg)
                ctx.wait("pe", nq_ev)
                ptr = ctx.op("pe", lambda e: e.transpose(
                    PS[4][0:1, 0:128], env.SR[:, 30:31], env.id_f[:, :]), inc="pe")
                ctx.wait("dve", ptr)
                nq_ev = ctx.op("dve", lambda e, ho=ho, qt=qt: e.tensor_copy(
                    qa[64:65, ho + qt * 128:ho + (qt + 1) * 128], PS[4][0:1, 0:128]), inc="dve")
        ctx.wait("pe", vtok_ev)
        ctx.wait("pe", vhp)
        ctx.wait("pe", nq_ev)
        for qc in range(TC):
            nkt = 4 * qc + 4
            pz = PS[5] if qc % 2 == 0 else PS[2]
            ctx.wait("pe", zst_ev[qc % 2])
            score_free = [None, None]
            pbuf_free = [None] * 4
            pv_last = None
            for hh in range(2):
                ho = hh * T
                psum_s = PS[6] if hh == 0 else PS[3]
                ctx.wait("pe", sum_free[hh])

                def emit_sc(kt, ho=ho, qc=qc):
                    pk = PS[kt % 2]
                    ctx.wait("pe", score_free[kt % 2])
                    mm = ctx.op("pe", lambda e, ho=ho, kt=kt, qc=qc, pk=pk: e.matmul(
                        pk[:, :], ka[0:65, ho + kt * 128:ho + (kt + 1) * 128],
                        qa[0:65, ho + qc * 512:ho + (qc + 1) * 512],
                        start=True, stop=True), inc="pe")
                    ctx.wait("act", mm)
                    m = kt - 4 * qc
                    if 0 <= m < 4:
                        ctx.wait("dve", mm)
                        vmadd = ctx.op("dve", lambda e, pk=pk, m=m: e.tensor_add(
                            pk[:, :], pk[:, :], cmk[:, m * 512:(m + 1) * 512]), inc="dve")
                        ctx.wait("act", vmadd)
                    ctx.wait("act", pbuf_free[kt % 4])
                    vex = ctx.op("act", lambda e, pk=pk, b=pbuf[kt % 4]: e.activation(
                        b, pk[:, :], AF.Exp, scale=0.125), inc="act")
                    score_free[kt % 2] = vex
                    return vex

                vexs = {0: emit_sc(0)}
                for kt in range(nkt):
                    if kt + 1 < nkt:
                        vexs[kt + 1] = emit_sc(kt + 1)
                    ctx.wait("pe", vexs[kt])
                    ctx.op("pe", lambda e, kt=kt, hh=hh, b=pbuf[kt % 4], pz=pz, nkt=nkt: e.matmul(
                        pz[hh * 64:(hh + 1) * 64, :], vtok[kt][:, hh * 64:(hh + 1) * 64], b,
                        start=(kt == 0), stop=(kt == nkt - 1),
                        tile_position=(0, 64) if hh else None))
                    mms = ctx.op("pe", lambda e, kt=kt, b=pbuf[kt % 4], ps_=psum_s, nkt=nkt: e.matmul(
                        ps_[0:1, :], env.ones_c_b[:, 0:1], b,
                        start=(kt == 0), stop=(kt == nkt - 1)), inc="pe")
                    pbuf_free[kt % 4] = mms
                pv_last = mms
                ctx.wait("dve", mms)
                ctx.drain("dve")
                ctx.op("dve", lambda e, hh=hh, ps_=psum_s: e.tensor_copy(
                    sumr[hh][0:1, :], ps_[0:1, :]))
                ctx.drain("dve")
                vrc = ctx.op("dve", lambda e, hh=hh: e.reciprocal(
                    sumr[hh][0:1, :], sumr[hh][0:1, :]), inc="dve")
                sum_free[hh] = vrc
            ctx.wait("pe", vrc)
            ctx.wait("pe", rbc_ev)
            ctx.op("pe", lambda e: e.matmul(
                PS[4][:, :], chalf[0][0:1, :], sumr[0][0:1, :], start=True, stop=False))
            mmb = ctx.op("pe", lambda e: e.matmul(
                PS[4][:, :], chalf[1][0:1, :], sumr[1][0:1, :], start=False, stop=True), inc="pe")
            ctx.wait("dve", mmb)
            ctx.wait("dve", pv_last)
            vb = ctx.op("dve", lambda e: e.tensor_copy(rbc, PS[4][:, :]), inc="dve")
            rbc_ev = vb
            ctx.drain("dve")
            ctx.wait("dve", zst)
            vz = ctx.op("dve", lambda e, pz=pz: e.tensor_mul(zev, pz[:, :], rbc), inc="dve")
            zst_ev[qc % 2] = vz
            ctx.wait("sp", vz)
            zst = env.qsp.issue(lambda e, hp=hp, qc=qc: e.dma_start(
                out=env.s_qT[hp * 128:(hp + 1) * 128, qc * 512:(qc + 1) * 512], in_=zev))
        arena_pe = ctx.milestone("pe")
        arena_dve = ctx.milestone("dve")
    ctx.barrier("l0b_attn", queues=env.ALLQ)
    arO = work_arena(env)
    wo = [arO.bf(1024) for _ in range(8)]
    zbuf = [arO.bf(4096) for _ in range(2)]
    outs = [arO.f32(512) for _ in range(4)]
    boc = env.SR[:, 0:8]
    ldo = []
    for hp in range(8):
        for j in range(2):
            ldo.append(env.qwt.issue(lambda e, hp=hp, j=j: e.dma_start(
                out=wo[hp][j * 64:(j + 1) * 64, :], in_=env.D["L0b_WO"][2 * hp + j, :, :])))
    tbo = load_vec_cols(env, env.D["L0b_bO"], boc, 8)
    for t in ldo:
        ctx.wait("pe", t)
    zld = [None, None]
    ev_done = [None, None]
    st_done = [None] * 4
    for qc in range(TC):
        sl = qc % 2
        ctx.wait("sp", zld[sl])
        zl = env.qsp.issue(lambda e, qc=qc, sl=sl: e.dma_start(
            out=zbuf[sl].rearrange("p (hp q) -> p hp q", q=512),
            in_=env.s_qT[:, qc * 512:(qc + 1) * 512].rearrange("(hp p) q -> p hp q", p=128)))
        ctx.wait("pe", zl)
        for mt in range(8):
            pk = PS[mt % 2]
            ctx.wait("pe", ev_done[mt % 2])
            for hp in range(8):
                mm = ctx.op("pe", lambda e, hp=hp, mt=mt, sl=sl, pk=pk: e.matmul(
                    pk[:, :], wo[hp][:, mt * 128:(mt + 1) * 128],
                    zbuf[sl][:, hp * 512:(hp + 1) * 512],
                    start=(hp == 0), stop=(hp == 7)), inc="pe" if hp == 7 else None)
            o_i = (qc * 8 + mt) % 4
            ctx.wait("dve", mm)
            ctx.wait("dve", st_done[o_i])
            ctx.wait("dve", tbo)
            ev_done[mt % 2] = ctx.op("dve", lambda e, pk=pk, o=outs[o_i], mt=mt: e.tensor_scalar(
                o, pk[:, :], boc[:, mt:mt + 1], None, ALU.add), inc="dve")
            ctx.wait("pool", ev_done[mt % 2])
            st_done[o_i] = env.qac.issue(lambda e, o=outs[o_i], mt=mt, qc=qc: e.dma_start(
                out=env.resid[mt * 128:(mt + 1) * 128, qc * 512:(qc + 1) * 512], in_=o,
                accum_op=ALU.add))
        zld[sl] = ctx.milestone("pe")
    ctx.barrier("l0b", queues=env.ALLQ)



EMITTERS = {
    "l1f1": emit_l1_ffn,
    "l1f2": emit_l1_ffn,
    "l1a": emit_l1_attn_a,
    "l0a": emit_l0_attn_a,
    "l0f1": emit_l0_ffn,
    "l0f2": emit_l0_ffn,
    "l0b": emit_l0_attn_b,
}


EMITTERS["l1b"] = lambda env: emit_l1_attn_b(env)


EMITTERS["l1b"] = lambda env: emit_l1_attn_b(env)


def emit_l1_attn_b(env):
    ctx, T, TC, NT = env.ctx, env.T, env.TC, env.NT
    PS = env.PS
    ar0 = work_arena(env)
    rowsA = ar0.f32(T)
    rowsB = ar0.f32(T)
    ar_ln = Arena(env.AR, ar0.off, env.ARCOLS)
    emit_ln(env, ar_ln, keep_rows=(rowsA, rowsB))

    ar1 = Arena(env.AR, ar0.off, env.ARCOLS)
    wl = [ar1.bf(1024) for _ in range(2)]
    wzx = [ar1.bf(4096) for _ in range(8)]
    oev = [ar1.bf(512) for _ in range(4)]
    ld = [None, None]
    tr_ev = None
    for rt in range(32):
        sl = rt % 2
        ctx.wait("pool", tr_ev)
        ld[sl] = env.qwt.issue(lambda e, rt=rt, s=wl[sl]: e.dma_start(
            out=s, in_=env.D["L1b_Win"][rt * 128:(rt + 1) * 128, :]))
        ctx.wait("pe", ld[sl])
        for kd in range(8):
            ctx.wait("pe", tr_ev)
            pev = ctx.op("pe", lambda e, s=wl[sl], kd=kd: e.transpose(
                env.PB[:, 0:128], s[:, kd * 128:(kd + 1) * 128], env.id_b[:]), inc="pe")
            ctx.wait("dve", pev)
            tr_ev = ctx.op("dve", lambda e, kd=kd, rt=rt: e.tensor_copy(
                wzx[kd][:, rt * 128:(rt + 1) * 128], env.PB[:, 0:128]), inc="dve")
    ctx.wait("pe", tr_ev)
    ev_done = [None, None]
    st_done = [None] * 4
    for tt in range(NT):
        for rc in range(8):
            pk = PS[rc % 2]
            ctx.wait("pe", ev_done[rc % 2])
            for kd in range(8):
                mm = ctx.op("pe", lambda e, tt=tt, rc=rc, kd=kd, pk=pk: e.matmul(
                    pk[:, :], env.xn[kd][:, tt * 128:(tt + 1) * 128],
                    wzx[kd][:, rc * 512:(rc + 1) * 512],
                    start=(kd == 0), stop=(kd == 7)), inc="pe" if kd == 7 else None)
            o_i = (tt * 8 + rc) % 4
            ctx.wait("dve", mm)
            ctx.wait("dve", st_done[o_i])
            ev_done[rc % 2] = ctx.op("dve", lambda e, pk=pk, o=oev[o_i]: e.tensor_copy(
                o, pk[:, :]), inc="dve")
            ctx.wait("sp", ev_done[rc % 2])
            dst = env.s_ztok if rc < 4 else env.s_xtok
            c0 = (rc % 4) * 512
            st_done[o_i] = env.qsp.issue(lambda e, o=oev[o_i], tt=tt, c0=c0, dst=dst: e.dma_start(
                out=dst[tt * 128:(tt + 1) * 128, c0:c0 + 512], in_=o))
    ctx.barrier("l1b_s1a", queues=env.ALLQ)

    ar2 = Arena(env.AR, ar0.off, env.ARCOLS)
    wsc_l = ar2.f32(1024)
    wsc = [ar2.f32(448) for _ in range(8)]
    sevac = [ar2.f32(512) for _ in range(2)]
    bcm = ar2.f32(512)
    bcr = ar2.f32(512)
    rr = ar2.f32(512)
    xf32 = [ar2.f32(512) for _ in range(2)]
    blocks = [(4096, 64, 0, 2), (4097, 63, 32, 2), (4160, 64, 64, 1),
              (4224, 64, 128, 2), (4225, 63, 160, 2), (4288, 64, 192, 1),
              (4448, 32, 256, 1), (4448, 32, 288, 1), (4352, 32, 320, 1),
              (4384, 32, 352, 1), (4416, 32, 384, 1)]
    ld_st = None
    tr_ev = None
    for (r0, span, dcol, step) in blocks:
        ctx.wait("sp", tr_ev)
        if step == 2:
            n = 32
            src = env.D["L1b_Win"][r0:r0 + span, :].rearrange(
                "(d two) k -> d two k", two=2)[:, 0] if span % 2 == 0 else None
            if src is None:
                src = env.D["L1b_Win"][r0:r0 + span + 1, :].rearrange(
                    "(d two) k -> d two k", two=2)[:, 0]
        else:
            n = span
            src = env.D["L1b_Win"][r0:r0 + span, :]
        ld_st = env.qsp.issue(lambda e, src=src, n=n: e.dma_start(out=wsc_l[0:n, :], in_=src))
        ctx.wait("pe", ld_st)
        for kd in range(8):
            ctx.wait("pe", tr_ev)
            pev = ctx.op("pe", lambda e, n=n, kd=kd: e.transpose(
                PS[4][:, 0:n], wsc_l[0:n, kd * 128:(kd + 1) * 128], env.id_f[0:n, 0:n]), inc="pe")
            ctx.wait("dve", pev)
            tr_ev = ctx.op("dve", lambda e, n=n, kd=kd, dcol=dcol: e.tensor_copy(
                wsc[kd][:, dcol:dcol + n], PS[4][:, 0:n]), inc="dve")
    ctx.wait("pe", tr_ev)
    GROUPS = ((0, 128), (128, 128), (256, 128), (384, 32))
    ev = [None, None]
    stv = [None, None]
    nrm_prev = None
    ld2 = [None, None]
    for tc in range(TC):
        ctx.drain("dve")
        vr = ctx.op("dve", lambda e, tc=tc: e.reciprocal(
            rr[0:1, :], rowsB[0:1, tc * 512:(tc + 1) * 512]), inc="dve")
        ctx.wait("pe", vr)
        ctx.op("pe", lambda e, tc=tc: e.matmul(
            PS[5][:, :], env.ones_r_f[0:1, :], rowsA[0:1, tc * 512:(tc + 1) * 512],
            start=True, stop=True))
        mmq = ctx.op("pe", lambda e: e.matmul(
            PS[6][:, :], env.ones_r_f[0:1, :], rr[0:1, :], start=True, stop=True), inc="pe")
        ctx.wait("dve", mmq)
        ctx.op("dve", lambda e: e.tensor_copy(bcm, PS[5][:, :]))
        ctx.drain("dve")
        vb = ctx.op("dve", lambda e: e.tensor_copy(bcr, PS[6][:, :]), inc="dve")
        ctx.wait("pe", vb)
        for kd in range(8):
            sl2 = kd % 2
            ctx.wait("sp", nrm_prev)
            ld2[sl2] = env.qsp.issue(lambda e, kd=kd, tc=tc, s=xf32[sl2]: e.dma_start(
                out=s, in_=env.resid[kd * 128:(kd + 1) * 128, tc * 512:(tc + 1) * 512]))
            ctx.wait("dve", ld2[sl2])
            ctx.drain("dve")
            ctx.op("dve", lambda e, s=xf32[sl2]: e.tensor_add(s, s, bcm))
            ctx.drain("dve")
            vn2 = ctx.op("dve", lambda e, s=xf32[sl2]: e.tensor_mul(s, s, bcr), inc="dve")
            ctx.wait("pe", vn2)
            for st_i, (c0, nr) in enumerate(GROUPS):
                mm = ctx.op("pe", lambda e, kd=kd, c0=c0, nr=nr, st_i=st_i, s=xf32[sl2]: e.matmul(
                    PS[st_i][0:nr, :], wsc[kd][:, c0:c0 + nr], s,
                    start=(kd == 0), stop=(kd == 7)), inc="pe" if st_i == 3 else None)
            nrm_prev = mm
        for st_i, (c0, nr) in enumerate(GROUPS):
            ctx.wait("dve", mm)
            ctx.wait("dve", stv[st_i % 2])
            ev[st_i % 2] = ctx.op("dve", lambda e, sl=st_i % 2, st_i=st_i, nr=nr: e.tensor_copy(
                sevac[sl][0:nr, :], PS[st_i][0:nr, :]), inc="dve")
            ctx.wait("sp", ev[st_i % 2])
            stv[st_i % 2] = env.qsp.issue(lambda e, sl=st_i % 2, c0=c0, nr=nr, tc=tc: e.dma_start(
                out=env.s_sch[c0:c0 + nr, tc * 512:(tc + 1) * 512], in_=sevac[sl][0:nr, :]))
            if st_i < 3:
                ctx.wait("pe", ev[st_i % 2])
    ctx.barrier("l1b_s1b", queues=env.ALLQ)

    arp = Arena(env.AR, 0, env.ARCOLS)
    bnt = arp.bf(T)
    cnt = arp.bf(T)
    lgct = arp.f32(32 * NT)
    cat = arp.f32(32 * NT)
    gt = arp.f32(32 * NT)
    sht = arp.f32(32 * NT)
    wq2 = arp.f32(32)
    PERSIST_END = arp.off

    T2 = min(1024, T)
    NSP = T // T2
    C2 = T2 // 512
    dtbc = env.SR[0:32, 100:101]
    dskc = env.SR[0:32, 101:102]
    bnwc = env.SR[:, 102:103]
    cnwc_ = env.SR[:, 103:104]
    l1 = env.qsp.issue(lambda e: e.dma_start(
        out=env.stage[0:32, 0:1], in_=env.D["L1b_dtb"].rearrange("(a b) -> a b", b=1)))
    l2 = env.qsp.issue(lambda e: e.dma_start(
        out=env.stage[0:32, 1:2], in_=env.D["L1b_Dsk"].rearrange("(a b) -> a b", b=1)))
    ctx.wait("dve", l1)
    ctx.wait("dve", l2)
    vc = ctx.op("dve", lambda e: e.tensor_copy(env.SR[0:32, 100:102], env.stage[0:32, 0:2]), inc="dve")
    for vecn, col in (("L1b_Bnw", 102), ("L1b_Cnw", 103)):
        for (o0, nn, p0, stp) in ((0, 64, 0, 2), (1, 63, 32, 2), (64, 64, 64, 1)):
            if stp == 2:
                src = env.D[vecn][o0:o0 + nn + (nn % 2)].rearrange("(d two) -> d two", two=2)[:, 0:1]
                nrows = 32
            else:
                src = env.D[vecn][o0:o0 + nn].rearrange("(d o) -> d o", o=1)
                nrows = nn
            env.qsp.issue(lambda e, src=src, col=col, p0=p0, nr=nrows: e.dma_start(
                out=env.SR[p0:p0 + nr, col:col + 1], in_=src,
                allow_slow_non_contiguous=True))
    ctx.barrier("l1b_s2pre", queues=env.ALLQ)

    ar3 = Arena(env.AR, PERSIST_END, env.ARCOLS)
    W2b = T2 + 8
    schh = ar3.f32(448 if False else 0)
    sB = ar3.f32(T2)
    sC = ar3.f32(T2)
    sRD = ar3.f32(W2b)
    sAT = ar3.f32(W2b)
    bDT = ar3.f32(W2b)
    bASP = ar3.f32(W2b)
    bTR = ar3.f32(W2b)
    bLAM = ar3.f32(W2b)
    bEN = ar3.f32(W2b)
    bG = ar3.f32(W2b)
    bCA = ar3.f32(W2b)
    bLGC = ar3.f32(W2b)
    bSC = ar3.f32(W2b)
    bZ = ar3.f32(W2b)
    bAN = ar3.f32(W2b)
    bRA = ar3.f32(W2b)
    sinE = ar3.f32(T2)
    cosE = ar3.f32(T2)
    sqb = ar3.bf(T2)
    rmsr = ar3.f32(T2)
    ibc = ar3.f32(T2)
    p2b = ar3.f32(T2)
    bDM = ar3.f32(T2)
    t1s = ar3.f32(512)
    t2s = ar3.f32(512)
    flip2 = ar3.bf(128)
    carry = ar3.f32(4)

    lf2 = env.qwt.issue(lambda e: e.dma_start(out=flip2, in_=env.D["c_ropeflip"][:, :]))
    ctx.op("pool", lambda e: e.memset(sinE[64:128, 0:T2], 0.0))
    vp = ctx.op("pool", lambda e: e.memset(cosE[64:128, 0:T2], 1.0), inc="pool")
    wq_ev = None
    for mi, vecn in enumerate(("L1b_Bb", "L1b_Cb")):
        ctx.wait("sp", env.stage_free)
        toks = []
        for (o0, nn, p0, stp) in ((0, 64, 0, 2), (1, 63, 32, 2), (64, 64, 64, 1)):
            if stp == 2:
                src = env.D[vecn][:, o0:o0 + nn + (nn % 2)].rearrange(
                    "h (d two) -> h d two", two=2)[:, :, 0]
                ncols = 32
            else:
                src = env.D[vecn][:, o0:o0 + nn]
                ncols = nn
            toks.append(env.qsp.issue(lambda e, src=src, p0=p0, nc=ncols: e.dma_start(
                out=env.stage[0:32, p0:p0 + nc], in_=src,
                allow_slow_non_contiguous=True)))
        for t in toks:
            ctx.wait("pe", t)
        ctx.wait("pe", env.lvc_evac)
        pev = ctx.op("pe", lambda e: e.transpose(
            PS[4][:, 0:32], env.stage[0:32, 0:128], env.id_f[0:32, 0:32]), inc="pe")
        env.stage_free = pev
        ctx.wait("dve", pev)
        if mi == 0:
            wq_ev = ctx.op("dve", lambda e: e.tensor_copy(env.stage[:, 64:96].bitcast(F32)[:, 0:16] if False else env.SR[:, 104:120], PS[4][:, 0:32]) if False else e.tensor_copy(env.SR[:, 104:136], PS[4][:, 0:32]), inc="dve")
            env.lvc_evac = wq_ev
        else:
            ctx.wait("dve", wq_ev)
            wq_ev = ctx.op("dve", lambda e: e.tensor_mul(wq2, PS[4][:, 0:32], env.SR[:, 104:136]), inc="dve")
            env.lvc_evac = wq_ev
    bnt_done = None
    ctx.wait("pe", wq_ev)
    ctx.wait("dve", vp)
    span_dve = None
    span_act = None
    for sp in range(NSP):
        t0 = sp * T2
        ext = 1 if (sp + 1 < NSP) else 0
        ctx.wait("sp", span_dve)
        ctx.wait("sp", span_act)
        lds = []
        lds.append(env.qsp.issue(lambda e, t0=t0: e.dma_start(
            out=sB, in_=env.s_sch[0:128, t0:t0 + T2])))
        lds.append(env.qsp.issue(lambda e, t0=t0: e.dma_start(
            out=sC, in_=env.s_sch[128:256, t0:t0 + T2])))
        lds.append(env.qsp.issue(lambda e, t0=t0, ext=ext: e.dma_start(
            out=sRD[0:96, 0:T2 + ext], in_=env.s_sch[256:352, t0:t0 + T2 + ext])))
        lds.append(env.qsp.issue(lambda e, t0=t0, ext=ext: e.dma_start(
            out=sAT[0:64, 0:T2 + ext], in_=env.s_sch[352:416, t0:t0 + T2 + ext])))
        for t in lds:
            ctx.wait("dve", t)
            ctx.wait("act", t)
        W = T2 + ext
        dtr = sRD[64:96, 0:W]

        def softplus(dst, srcap, w, biascol=None, addconst=None):
            scr = bSC[0:32, 0:w]
            if biascol is not None:
                v = ctx.op("dve", lambda e: e.tensor_scalar(scr, srcap, biascol, None, ALU.add), inc="dve")
            else:
                v = ctx.op("dve", lambda e: e.tensor_copy(scr, srcap), inc="dve")
            ctx.wait("act", v)
            ctx.op("act", lambda e: e.activation(dst, scr, AF.Abs))
            ctx.drain("act")
            ctx.op("act", lambda e: e.activation(dst, dst, AF.Exp, scale=-1.0))
            ctx.drain("act")
            va = ctx.op("act", lambda e: e.activation(dst, dst, AF.Ln, bias=1.0), inc="act")
            ctx.wait("dve", va)
            vm = ctx.op("dve", lambda e: e.tensor_scalar_max(scr, scr, 0.0), inc="dve")
            ctx.drain("dve")
            v2 = ctx.op("dve", lambda e: e.tensor_add(dst, dst, scr), inc="dve")
            ctx.drain("dve")
            if addconst is not None:
                v2 = ctx.op("dve", lambda e: e.tensor_scalar(dst, dst, addconst, None, ALU.add), inc="dve")
                ctx.drain("dve")
            return v2

        ctx.wait("dve", vc)
        ctx.wait("dve", bnt_done)
        ctx.drain("dve")
        softplus(bDT[0:32, 0:W], dtr, W, biascol=dtbc)
        softplus(bASP[0:32, 0:W], sAT[0:32, 0:W], W, addconst=1e-4)
        ctx.op("dve", lambda e, W=W: e.scalar_tensor_tensor(
            bASP[0:32, 0:W], bDT[0:32, 0:W], -1.0, bASP[0:32, 0:W], ALU.mult, ALU.mult))
        ctx.drain("dve")
        v = ctx.op("dve", lambda e, W=W: e.tensor_copy(bTR[0:32, 0:W], sAT[32:64, 0:W]), inc="dve")
        ctx.wait("act", v)
        va = ctx.op("act", lambda e, W=W: e.activation(bTR[0:32, 0:W], bTR[0:32, 0:W], AF.Sigmoid), inc="act")
        ctx.wait("dve", va)
        ctx.op("dve", lambda e, W=W: e.tensor_mul(bLAM[0:32, 0:W], bDT[0:32, 0:W], bTR[0:32, 0:W]))
        ctx.drain("dve")
        ctx.op("dve", lambda e, W=W: e.tensor_scalar(bLAM[0:32, 0:W], bLAM[0:32, 0:W], 0.5, None, ALU.mult))
        ctx.drain("dve")
        ctx.op("dve", lambda e, W=W: e.tensor_sub(bDT[0:32, 0:W], bDT[0:32, 0:W], bLAM[0:32, 0:W]))
        ctx.drain("dve")
        v = ctx.op("dve", lambda e, W=W: e.tensor_copy(bEN[0:32, 0:W], bASP[0:32, 0:W]), inc="dve")
        ctx.wait("act", v)
        va = ctx.op("act", lambda e, W=W: e.activation(bEN[0:32, 0:W], bEN[0:32, 0:W], AF.Exp, scale=-1.0), inc="act")
        ctx.wait("dve", va)
        ctx.op("dve", lambda e, W=W: e.tensor_mul(bSC[0:32, 0:W], bLAM[0:32, 0:W], bEN[0:32, 0:W]))
        ctx.drain("dve")
        ctx.op("dve", lambda e, W=W: e.tensor_copy(bG[0:32, 0:T2], bDT[0:32, 0:T2]))
        ctx.drain("dve")
        nsh = T2 - 1 + ext
        ctx.op("dve", lambda e, nsh=nsh: e.tensor_add(
            bG[0:32, 0:nsh], bG[0:32, 0:nsh], bSC[0:32, 1:1 + nsh]))
        ctx.drain("dve")
        if sp == 0:
            ctx.op("pool", lambda e: e.memset(bZ[0:64, 0:T2], 0.0), inc="pool")
            ctx.wait("dve", ctx.milestone("pool"))
        init1 = 0.0 if sp == 0 else carry[0:32, 0:1]
        ctx.op("dve", lambda e, init1=init1: e.tensor_tensor_scan(
            bCA[0:32, 0:T2], bASP[0:32, 0:T2], bZ[0:32, 0:T2], init1, ALU.add, ALU.add))
        ctx.drain("dve")
        init2 = 0.0 if sp == 0 else carry[0:64, 2:3]
        ctx.op("dve", lambda e, init2=init2: e.tensor_tensor_scan(
            bAN[0:64, 0:T2], sRD[0:64, 0:T2], bZ[0:64, 0:T2], init2, ALU.add, ALU.add))
        ctx.drain("dve")
        ctx.op("dve", lambda e: e.tensor_copy(carry[0:32, 0:1], bCA[0:32, T2 - 1:T2]))
        ctx.drain("dve")
        ctx.op("dve", lambda e: e.tensor_copy(carry[0:64, 2:3], bAN[0:64, T2 - 1:T2]))
        ctx.drain("dve")
        v = ctx.op("dve", lambda e: e.tensor_copy(bLGC[0:32, 0:T2], bG[0:32, 0:T2]), inc="dve")
        ctx.wait("act", v)
        va = ctx.op("act", lambda e: e.activation(bLGC[0:32, 0:T2], bLGC[0:32, 0:T2], AF.Ln), inc="act")
        ctx.wait("dve", va)
        ctx.op("dve", lambda e: e.tensor_sub(bLGC[0:32, 0:T2], bLGC[0:32, 0:T2], bCA[0:32, 0:T2]))
        ctx.drain("dve")
        kbuf = bSC[0:64, 0:T2] if False else None
        ctx.op("dve", lambda e: e.tensor_scalar(bRA[0:64, 0:T2], bAN[0:64, 0:T2], 0.15915494309189535, None, ALU.mult))
        ctx.drain("dve")
        ctx.op("dve", lambda e: e.tensor_scalar(bRA[0:64, 0:T2], bRA[0:64, 0:T2], MAGIC, MAGIC, ALU.add, ALU.subtract))
        ctx.drain("dve")
        ctx.op("dve", lambda e: e.scalar_tensor_tensor(
            sinE[0:64, 0:T2], bRA[0:64, 0:T2], -CW1, bAN[0:64, 0:T2], ALU.mult, ALU.add))
        ctx.drain("dve")
        ctx.op("dve", lambda e: e.scalar_tensor_tensor(
            sinE[0:64, 0:T2], bRA[0:64, 0:T2], -CW2, sinE[0:64, 0:T2], ALU.mult, ALU.add))
        ctx.drain("dve")
        v = ctx.op("dve", lambda e: e.scalar_tensor_tensor(
            bRA[0:64, 0:T2], bRA[0:64, 0:T2], -CW3, sinE[0:64, 0:T2], ALU.mult, ALU.add), inc="dve")
        ctx.wait("act", v)
        vs = ctx.op("act", lambda e: e.activation(sinE[0:64, 0:T2], bRA[0:64, 0:T2], AF.Sin), inc="act")
        ctx.wait("dve", vs)
        ctx.op("dve", lambda e: e.tensor_scalar(bRA[0:64, 0:T2], bRA[0:64, 0:T2],
                                                1.5707963267948966, None, ALU.add))
        ctx.drain("dve")
        ctx.op("dve", lambda e: e.tensor_scalar(cosE[0:64, 0:T2], bRA[0:64, 0:T2],
                                                3.141592653589793, None, ALU.is_gt))
        ctx.drain("dve")
        v = ctx.op("dve", lambda e: e.scalar_tensor_tensor(
            bRA[0:64, 0:T2], cosE[0:64, 0:T2], -6.283185307179586, bRA[0:64, 0:T2],
            ALU.mult, ALU.add), inc="dve")
        ctx.wait("act", v)
        vc2 = ctx.op("act", lambda e: e.activation(cosE[0:64, 0:T2], bRA[0:64, 0:T2], AF.Sin), inc="act")
        ctx.wait("dve", vc2)
        rope_done = None
        for (srcT, dstT, wcol) in ((sB, bnt, bnwc), (sC, cnt, cnwc_)):
            va = ctx.op("act", lambda e, s=srcT: e.activation(sqb, s[0:128, 0:T2], AF.Square), inc="act")
            ctx.wait("pe", va)
            ctx.wait("pe", rope_done)
            ve = None
            for c in range(C2):
                mm = ctx.op("pe", lambda e, c=c: e.matmul(
                    PS[4][0:1, :], env.ones_c_b[:, 0:1], sqb[:, c * 512:(c + 1) * 512],
                    start=True, stop=True), inc="pe")
                ctx.wait("dve", mm)
                ve = ctx.op("dve", lambda e, c=c: e.tensor_copy(
                    rmsr[0:1, c * 512:(c + 1) * 512], PS[4][0:1, :]), inc="dve")
                ctx.wait("pe", ve)
            ctx.wait("act", ve)
            va = ctx.op("act", lambda e: e.activation(
                rmsr[0:1, 0:T2], rmsr[0:1, 0:T2], AF.Sqrt, scale=1.0 / 128.0, bias=LN_EPS), inc="act")
            ctx.wait("dve", va)
            vr = ctx.op("dve", lambda e: e.reciprocal(rmsr[0:1, 0:T2], rmsr[0:1, 0:T2]), inc="dve")
            ctx.wait("pe", vr)
            vb = None
            for c in range(C2):
                mm = ctx.op("pe", lambda e, c=c: e.matmul(
                    PS[4][:, :], env.ones_r_f[0:1, :], rmsr[0:1, c * 512:(c + 1) * 512],
                    start=True, stop=True), inc="pe")
                ctx.wait("dve", mm)
                vb = ctx.op("dve", lambda e, c=c: e.tensor_copy(
                    ibc[:, c * 512:(c + 1) * 512], PS[4][:, :]), inc="dve")
                ctx.wait("pe", vb)
            ctx.drain("dve")
            vn = ctx.op("dve", lambda e, s=srcT, wcol=wcol: e.scalar_tensor_tensor(
                sqb, s[0:128, 0:T2], wcol, ibc, ALU.mult, ALU.mult), inc="dve")
            ctx.wait("pe", vn)
            ctx.wait("pe", lf2)
            vvv = None
            for c in range(C2):
                mm = ctx.op("pe", lambda e, c=c: e.matmul(
                    PS[5][:, :], flip2, sqb[:, c * 512:(c + 1) * 512], start=True, stop=True), inc="pe")
                ctx.wait("dve", mm)
                ctx.op("dve", lambda e, c=c: e.tensor_mul(
                    t1s, sqb[:, c * 512:(c + 1) * 512], cosE[:, c * 512:(c + 1) * 512]))
                ctx.drain("dve")
                ctx.op("dve", lambda e, c=c: e.tensor_mul(
                    t2s, PS[5][:, :], sinE[:, c * 512:(c + 1) * 512]))
                ctx.drain("dve")
                vvv = ctx.op("dve", lambda e, d=dstT, c=c, t0=t0: e.tensor_add(
                    d[:, t0 + c * 512:t0 + (c + 1) * 512], t1s, t2s), inc="dve")
                ctx.wait("pe", vvv)
            ctx.wait("act", vvv)
            rope_done = vvv
        ctx.drain("dve")
        v = ctx.op("dve", lambda e, t0=t0: e.tensor_mul(
            p2b, bnt[:, t0:t0 + T2], cnt[:, t0:t0 + T2]), inc="dve")
        ctx.wait("pe", v)
        ctx.wait("pe", wq_ev)
        vdm = None
        for c in range(C2):
            mm = ctx.op("pe", lambda e, c=c: e.matmul(
                PS[6][0:32, :], wq2, p2b[:, c * 512:(c + 1) * 512], start=True, stop=True), inc="pe")
            ctx.wait("dve", mm)
            vdm = ctx.op("dve", lambda e, c=c: e.tensor_copy(
                bDM[0:32, c * 512:(c + 1) * 512], PS[6][0:32, :]), inc="dve")
            ctx.wait("pe", vdm)
        ctx.drain("dve")
        ctx.op("dve", lambda e: e.tensor_mul(bDM[0:32, 0:T2], bDM[0:32, 0:T2], bDT[0:32, 0:T2]))
        ctx.drain("dve")
        ctx.op("dve", lambda e: e.tensor_scalar(bDM[0:32, 0:T2], bDM[0:32, 0:T2], dskc[0:32, 0:1], None, ALU.add))
        ctx.drain("dve")
        tok_ev = None
        for i0 in range(T2 // 128):
            i = (t0 // 128) + i0
            for (srcap, dst) in ((bLGC[0:32, :], lgct), (bCA[0:32, :], cat),
                                 (bG[0:32, :], gt), (bDM[0:32, :], sht)):
                ctx.wait("pe", tok_ev)
                pev = ctx.op("pe", lambda e, s=srcap, i0=i0: e.transpose(
                    PS[4][:, 0:32], s[:, i0 * 128:(i0 + 1) * 128], env.id_f[0:32, 0:32]), inc="pe")
                ctx.wait("dve", pev)
                tok_ev = ctx.op("dve", lambda e, d=dst, i=i: e.tensor_copy(
                    d[:, 32 * i:32 * (i + 1)], PS[4][:, 0:32]), inc="dve")
        bnt_done = tok_ev
        span_dve = ctx.milestone("dve")
        span_act = ctx.milestone("act")
    ctx.barrier("l1b_s2", queues=env.ALLQ)

    ar5 = Arena(env.AR, PERSIST_END, env.ARCOLS)
    wol = [ar5.bf(2048) for _ in range(2)]
    woT = [ar5.bf(1024) for _ in range(16)]
    bntok = [ar5.bf(128) for _ in range(NT)]
    xa = [ar5.bf(2048) for _ in range(2)]
    za = [ar5.bf(2048) for _ in range(2)]
    yt = [ar5.bf(2048) for _ in range(2)]
    ytT = [ar5.bf(512) for _ in range(16)]
    hS = ar5.f32(2048)
    hB = ar5.bf(2048)
    lstr = ar5.f32(128)
    carow = ar5.f32(128)
    e2_4 = ar5.f32(512)
    cnw4 = ar5.bf(512)
    F4 = ar5.bf(512)
    xcs4 = ar5.bf(256)
    rep4 = ar5.f32(512)
    rhs32 = ar5.f32(32)
    sel8s = ar5.f32(1024)
    selms = ar5.f32(4)
    onz32 = ar5.f32(128)
    caeA = [ar5.f32(32) for _ in range(2)]
    dt_all = ar5.f32(32)
    edc_all = ar5.f32(32)
    cc_all = ar5.f32(32)
    ys1 = ar5.f32(64)
    ys2 = ar5.f32(64)
    zsa = [ar5.f32(2048) for _ in range(2)]
    outs = [ar5.f32(512) for _ in range(2)]
    lw = []
    tr_ev = None
    for mt in range(8):
        sl = mt % 2
        ctx.wait("pool", tr_ev)
        ldm = env.qwt.issue(lambda e, mt=mt, s=wol[sl]: e.dma_start(
            out=s, in_=env.D["L1b_Wout"][mt * 128:(mt + 1) * 128, :]))
        ctx.wait("pe", ldm)
        for rc in range(16):
            ctx.wait("pe", tr_ev)
            pev = ctx.op("pe", lambda e, s=wol[sl], rc=rc: e.transpose(
                env.PB[:, 0:128], s[:, rc * 128:(rc + 1) * 128], env.id_b[:]), inc="pe")
            ctx.wait("dve", pev)
            tr_ev = ctx.op("dve", lambda e, rc=rc, mt=mt: e.tensor_copy(
                woT[rc][:, mt * 128:(mt + 1) * 128], env.PB[:, 0:128]), inc="dve")
    lls = env.qsp.issue(lambda e: e.dma_start(out=lstr, in_=env.D["c_lstrict"][:, :]))
    lsel = env.qsp.issue(lambda e: e.dma_start(out=sel8s[0:32, :], in_=env.D["c_sel8"][:, :]))
    lselm = env.qsp.issue(lambda e: e.dma_start(out=selms[0:32, :], in_=env.D["c_selmask"][:, :]))
    for i in range(NT):
        ctx.wait("pe", tr_ev)
        pev = ctx.op("pe", lambda e, i=i: e.transpose(
            env.PB[:, 0:128], bnt[:, i * 128:(i + 1) * 128], env.id_b[:]), inc="pe")
        ctx.wait("dve", pev)
        tr_ev = ctx.op("dve", lambda e, i=i: e.tensor_copy(bntok[i], env.PB[:, 0:128]), inc="dve")
    vh0 = ctx.op("pool", lambda e: e.memset(hS[:, :], 0.0), inc="pool")
    vo1 = ctx.op("pool", lambda e: e.memset(onz32[0:32, :], 1.0), inc="pool")
    ctx.wait("pe", tr_ev)
    ctx.wait("dve", lls)
    ctx.wait("dve", vh0)
    ctx.wait("dve", lselm)
    ctx.wait("pe", lsel)
    ctx.wait("pe", vo1)
    xld = [None, None]
    xfree = [None, None]
    zsa_free = [None, None]
    ost = [None, None]
    oev_t = [None, None]
    grp_pe = None
    pre6_tok = None
    vcaea_tok = None
    vexp_de = None
    add_tok = [None, None]
    vyi_tok = None
    hst_tok = None
    v2_guard = [None, None]
    for i in range(NT):
        sl = i % 2
        cur, prv = caeA[i % 2], caeA[(i + 1) % 2]
        ctx.wait("sp", xfree[sl])
        xld[sl] = env.qsp.issue(lambda e, i=i, s=xa[sl]: e.dma_start(
            out=s, in_=env.s_xtok[i * 128:(i + 1) * 128, :]))
        zl = env.qsp.issue(lambda e, i=i, s=za[sl]: e.dma_start(
            out=s, in_=env.s_ztok[i * 128:(i + 1) * 128, :]))
        ctx.wait("dve", xld[sl])
        ctx.wait("act", zl)
        ctx.wait("act", zsa_free[sl])
        vzs = ctx.op("act", lambda e, sl=sl: e.activation(zsa[sl], za[sl], AF.Silu), inc="act")
        ctx.wait("dve", vzs)
        ctx.wait("pe", pre6_tok)
        ptr32 = ctx.op("pe", lambda e, i=i: e.transpose(
            PS[6][0:32, 0:128], cat[:, 32 * i:32 * (i + 1)], env.id_f[:, :]), inc="pe")
        ctx.wait("dve", ptr32)
        vcr32 = ctx.op("dve", lambda e: e.tensor_copy(carow[0:32, 0:128], PS[6][0:32, 0:128]), inc="dve")
        ctx.drain("dve")
        vr32 = ctx.op("dve", lambda e: e.tensor_scalar(
            rhs32[0:32, 0:32], env.id_f[0:32, 0:32], carow[0:32, 127:128], None, ALU.mult), inc="dve")
        vrep = None
        for j in range(4):
            vrep = ctx.op("dve", lambda e, j=j: e.tensor_scalar(
                rep4[0:32, 128 * j:128 * (j + 1)], carow[0:32, 0:128], selms[0:32, j:j + 1],
                None, ALU.mult), inc="dve" if j == 3 else None)
        ctx.wait("pe", vr32)
        mm_cae = ctx.op("pe", lambda e: e.matmul(
            PS[6][:, 0:32], onz32[0:32, :], rhs32[0:32, 0:32], start=True, stop=True), inc="pe")
        ctx.wait("dve", mm_cae)
        vcaea = ctx.op("dve", lambda e, cur=cur: e.tensor_copy(cur, PS[6][:, 0:32]), inc="dve")
        vcaea_tok = vcaea
        ctx.drain("dve")
        if i > 0:
            ctx.op("dve", lambda e, cur=cur, prv=prv: e.tensor_sub(dt_all, cur, prv))
            v = ctx.op("dve", lambda e, i=i, prv=prv: e.tensor_sub(
                edc_all, cat[:, 32 * i:32 * (i + 1)], prv), inc="dve")
            ctx.wait("act", v)
            ctx.op("act", lambda e: e.activation(dt_all, dt_all, AF.Exp))
            vexp_de = ctx.op("act", lambda e: e.activation(edc_all, edc_all, AF.Exp), inc="act")
        v = ctx.op("dve", lambda e, i=i, cur=cur: e.tensor_sub(
            cc_all, cur, cat[:, 32 * i:32 * (i + 1)]), inc="dve")
        ctx.wait("act", v)
        vca = ctx.op("act", lambda e: e.activation(cc_all, cc_all, AF.Exp), inc="act")
        ctx.wait("dve", vca)
        ctx.op("dve", lambda e, i=i: e.tensor_mul(cc_all, cc_all, gt[:, 32 * i:32 * (i + 1)]))
        ctx.drain("dve")
        for c in range(8):
            pb = PS[3 + (c % 2)]
            if c == 0:
                ctx.wait("pe", oev_t[0])
                ctx.wait("pe", oev_t[1])
                ctx.wait("pe", vrep)
            ctx.wait("pe", v2_guard[c % 2])
            bc_tok = ctx.op("pe", lambda e, c=c, pb=pb: e.matmul(
                pb[:, 0:512], sel8s[0:32, 128 * c:128 * (c + 1)], rep4[0:32, 0:512],
                start=True, stop=True), inc="pe")
            ctx.wait("dve", bc_tok)
            if i > 0 and c == 0:
                ctx.wait("dve", vexp_de)
            v2l = None
            for j in range(4):
                h = 4 * c + j
                v2l = ctx.op("dve", lambda e, j=j, h=h, i=i, pb=pb: e.scalar_tensor_tensor(
                    e2_4[:, 128 * j:128 * (j + 1)], pb[:, 128 * j:128 * (j + 1)],
                    lgct[:, 32 * i + h:32 * i + h + 1], lstr, ALU.add, ALU.add), inc="dve")
            v2_guard[c % 2] = v2l
            ctx.wait("act", v2l)
            vE4 = ctx.op("act", lambda e: e.activation(e2_4, e2_4, AF.Exp), inc="act")
            vcwl = None
            for j in range(4):
                h = 4 * c + j
                vcwl = ctx.op("dve", lambda e, j=j, h=h, i=i: e.tensor_scalar(
                    cnw4[:, 128 * j:128 * (j + 1)], cnt[:, i * 128:(i + 1) * 128],
                    wq2[:, h:h + 1], None, ALU.mult), inc="dve")
            vxl = None
            for j in range(4):
                h = 4 * c + j
                vxl = ctx.op("dve", lambda e, j=j, h=h, s=xa[sl]: e.tensor_scalar(
                    xcs4[:, 64 * j:64 * (j + 1)], s[:, h * 64:(h + 1) * 64],
                    cc_all[:, h:h + 1], None, ALU.mult), inc="dve")
            ctx.wait("pe", vcwl)
            mmg4 = ctx.op("pe", lambda e, i=i: e.matmul(
                PS[5][:, 0:512], bnt[:, i * 128:(i + 1) * 128], cnw4[:, 0:512],
                start=True, stop=True), inc="pe")
            ctx.wait("dve", mmg4)
            ctx.wait("dve", vE4)
            vF4 = ctx.op("dve", lambda e: e.tensor_mul(F4, PS[5][:, 0:512], e2_4), inc="dve")
            for j in range(4):
                h = 4 * c + j
                yb = PS[0] if h % 2 == 0 else PS[6]
                hsl = hS[:, 64 * h:64 * (h + 1)]
                hbl = hB[:, 64 * h:64 * (h + 1)]
                sh_col = sht[:, 32 * i + h:32 * i + h + 1]
                if j == 0:
                    ctx.wait("pe", vF4)
                    ctx.wait("pe", xld[sl])
                    ctx.wait("pe", vxl)
                if h == 1:
                    ctx.wait("pe", vcaea_tok)
                ctx.wait("pe", add_tok[h % 2])
                mmy = ctx.op("pe", lambda e, j=j, h=h, s=xa[sl], yb=yb: e.matmul(
                    yb[0:128, 0:64], F4[:, 128 * j:128 * (j + 1)], s[:, h * 64:(h + 1) * 64],
                    start=True, stop=True), inc="pe")
                if i > 0:
                    ctx.wait("pe", vyi_tok)
                    mmyi = ctx.op("pe", lambda e, j=j, hbl=hbl: e.matmul(
                        PS[1][0:128, 0:64], cnw4[:, 128 * j:128 * (j + 1)], hbl,
                        start=True, stop=True), inc="pe")
                ctx.wait("pe", hst_tok)
                mmh = ctx.op("pe", lambda e, j=j, i=i: e.matmul(
                    PS[2][0:128, 0:64], bntok[i], xcs4[:, 64 * j:64 * (j + 1)],
                    start=True, stop=True), inc="pe")
                ctx.wait("dve", mmh)
                if i > 0:
                    hst_tok = ctx.op("dve", lambda e, h=h, hsl=hsl: e.scalar_tensor_tensor(
                        hsl, hsl, dt_all[:, h:h + 1], PS[2][0:128, 0:64], ALU.mult, ALU.add), inc="dve")
                    ctx.wait("dve", mmyi)
                    vyi_tok = ctx.op("dve", lambda e, h=h: e.tensor_scalar(
                        ys1, PS[1][0:128, 0:64], edc_all[:, h:h + 1], None, ALU.mult), inc="dve")
                    ctx.drain("dve")
                    ctx.op("dve", lambda e, s=xa[sl], h=h, sh=sh_col: e.scalar_tensor_tensor(
                        ys2, s[:, h * 64:(h + 1) * 64], sh, ys1, ALU.mult, ALU.add))
                    ctx.op("dve", lambda e, hsl=hsl, hbl=hbl: e.tensor_copy(hbl, hsl))
                else:
                    hst_tok = ctx.op("dve", lambda e, hsl=hsl: e.tensor_copy(
                        hsl, PS[2][0:128, 0:64]), inc="dve")
                    ctx.drain("dve")
                    ctx.op("dve", lambda e, s=xa[sl], h=h, sh=sh_col: e.tensor_scalar(
                        ys2, s[:, h * 64:(h + 1) * 64], sh, None, ALU.mult))
                    ctx.op("dve", lambda e, hsl=hsl, hbl=hbl: e.tensor_copy(hbl, hsl))
                ctx.drain("dve")
                ctx.wait("dve", mmy)
                add_tok[h % 2] = ctx.op("dve", lambda e, yb=yb: e.tensor_add(
                    ys1, ys2, yb[0:128, 0:64]), inc="dve")
                ctx.drain("dve")
                vy = ctx.op("dve", lambda e, d=yt[sl], h=h, sl=sl: e.tensor_mul(
                    d[:, h * 64:(h + 1) * 64], ys1, zsa[sl][:, h * 64:(h + 1) * 64]), inc="dve")
                ctx.drain("dve")
        pre6_tok = add_tok[1]
        ctx.wait("pe", vy)
        for rc in range(16):
            if i % 4 == 0 and rc == 0:
                ctx.wait("pe", grp_pe)
            ctx.wait("pe", tr_ev)
            pev = ctx.op("pe", lambda e, rc=rc, sl=sl: e.transpose(
                env.PB[:, 0:128], yt[sl][:, rc * 128:(rc + 1) * 128], env.id_b[:]), inc="pe")
            ctx.wait("dve", pev)
            tr_ev = ctx.op("dve", lambda e, rc=rc, i=i: e.tensor_copy(
                ytT[rc][:, (i % 4) * 128:((i % 4) + 1) * 128], env.PB[:, 0:128]), inc="dve")
        xfree[sl] = ctx.milestone("dve")
        zsa_free[sl] = ctx.milestone("dve")
        if i % 4 == 3:
            qg = i // 4
            ctx.wait("pe", tr_ev)
            for mt in range(8):
                pk = PS[3 + (mt % 2)]
                ctx.wait("pe", oev_t[mt % 2])
                for rc in range(16):
                    mm = ctx.op("pe", lambda e, rc=rc, mt=mt, pk=pk: e.matmul(
                        pk[:, :], woT[rc][:, mt * 128:(mt + 1) * 128], ytT[rc],
                        start=(rc == 0), stop=(rc == 15)), inc="pe" if rc == 15 else None)
                ctx.wait("dve", mm)
                ctx.wait("dve", ost[mt % 2])
                oev_t[mt % 2] = ctx.op("dve", lambda e, pk=pk, o=outs[mt % 2]: e.tensor_copy(
                    o, pk[:, :]), inc="dve")
                ctx.wait("pool", oev_t[mt % 2])
                ost[mt % 2] = env.qac.issue(lambda e, o=outs[mt % 2], mt=mt, qg=qg: e.dma_start(
                    out=env.resid[mt * 128:(mt + 1) * 128, qg * 512:(qg + 1) * 512], in_=o,
                    accum_op=ALU.add))
            grp_pe = ctx.milestone("pe")
    ctx.barrier("l1b", queues=env.ALLQ)


NAME_MAP = {
    "L0a_Wq": "L0_attn_a_W_q",
    "L0f_Wr": "L0_ffn_Wr", "L0f_br": "L0_ffn_br",
    "L0f_g": "L0_ffn_ln_g", "L0f_b": "L0_ffn_ln_b",
    "L0f_W1": "L0_ffn_W1", "L0f_b1": "L0_ffn_b1",
    "L0f_W2": "L0_ffn_W2", "L0f_b2": "L0_ffn_b2",
    "L0b_WQ": "L0_attn_b_W_Q", "L0b_bQ": "L0_attn_b_b_Q",
    "L0b_WK": "L0_attn_b_W_K", "L0b_bK": "L0_attn_b_b_K",
    "L0b_WV": "L0_attn_b_W_V", "L0b_bV": "L0_attn_b_b_V",
    "L0b_WO": "L0_attn_b_W_O", "L0b_bO": "L0_attn_b_b_O",
    "L1a_W1": "L1_attn_a_W1", "L1a_W2": "L1_attn_a_W2",
    "L1a_bias": "L1_attn_a_bias", "L1a_et": "L1_attn_a_embeddings_table",
    "L1f_CB": "L1_ffn_CB", "L1f_SCB": "L1_ffn_SCB", "L1f_bias": "L1_ffn_bias",
    "L1b_Win": "L1_attn_b_W_in", "L1b_dtb": "L1_attn_b_dt_bias",
    "L1b_Bb": "L1_attn_b_B_bias", "L1b_Cb": "L1_attn_b_C_bias",
    "L1b_Bnw": "L1_attn_b_B_norm_w", "L1b_Cnw": "L1_attn_b_C_norm_w",
    "L1b_Dsk": "L1_attn_b_D_skip", "L1b_Wout": "L1_attn_b_W_out",
}


def build_consts(T):
    c = {}
    c["c_eye"] = np.eye(128, dtype=np.float32)
    pos = np.arange(T, dtype=np.float32)
    j = np.arange(64)
    freq = (10000.0 ** ((j % 32) / 32.0)).astype(np.float32)
    ang = pos[None, :] / freq[:, None]
    cos = np.cos(ang).astype(np.float32)
    sin = np.sin(ang).astype(np.float32)
    c["c_rope_cos"] = np.concatenate([cos, cos], axis=0)
    c["c_rope_sin"] = np.concatenate([sin, sin], axis=0)
    F = np.zeros((128, 128), np.float32)
    for base in (0, 64):
        for r in range(32):
            F[base + r, base + r + 32] = -1.0
            F[base + r + 32, base + r] = 1.0
    c["c_ropeflip"] = np.ascontiguousarray(F.T)
    qk = np.zeros((4 * 128, 512), np.float32)
    kq = np.zeros((4 * 128, 512), np.float32)
    for m in range(4):
        q_idx = np.arange(128)[:, None] + m * 128
        k_idx = np.arange(512)[None, :]
        qk[m * 128:(m + 1) * 128] = np.where(k_idx <= q_idx, 0.0, NEG)
        k_idx2 = np.arange(128)[:, None] + m * 128
        q_idx2 = np.arange(512)[None, :]
        kq[m * 128:(m + 1) * 128] = np.where(q_idx2 >= k_idx2, 0.0, NEG)
    c["c_cmask_qk"] = qk
    c["c_cmask_kq"] = kq
    s_ = np.arange(128)[:, None]
    t_ = np.arange(128)[None, :]
    c["c_lstrict"] = np.where(t_ >= s_ + 1, 0.0, NEG).astype(np.float32)
    k32_ = np.arange(32)
    sel8_ = np.zeros((32, 1024), dtype=np.float32)
    for cc_ in range(8):
        sel8_[4 * cc_:4 * cc_ + 4, 128 * cc_:128 * (cc_ + 1)] = 1.0
    c["c_sel8"] = sel8_
    c["c_selmask"] = (k32_[:, None] % 4 == np.arange(4)[None, :]).astype(np.float32)
    return c


_CACHE = {}


def kernel(**inputs):
    from concourse.bass_utils import run_bass_kernel_spmd
    T = 4096
    if "nc" not in _CACHE:
        nc, env = build(T=T, parts=ALL_PARTS)
        _CACHE["nc"] = nc
        _CACHE["env"] = env
    nc = _CACHE["nc"]
    base = {k: np.ascontiguousarray(np.asarray(inputs[v], dtype=np.float32))
            for k, v in NAME_MAP.items()}
    base.update(build_consts(T))
    x = np.asarray(inputs["L0_attn_a_hidden_states"], dtype=np.float32)
    in_maps = []
    for b in range(8):
        m = dict(base)
        m["x"] = np.ascontiguousarray(x[b])
        in_maps.append(m)
    res = run_bass_kernel_spmd(nc, in_maps, core_ids=list(range(8)))
    out = np.stack([np.asarray(r["out"]) for r in res.results], axis=0)
    return out.astype(np.float32)


# version 10 (final)
# speedup vs baseline: 1.3376x; 1.0653x over previous
import numpy as np
import ml_dtypes
from contextlib import ExitStack
import concourse.bass as bass
import concourse.mybir as mybir

F32 = mybir.dt.float32
BF16 = mybir.dt.bfloat16
AF = mybir.ActivationFunctionType
ALU = mybir.AluOpType

MAGIC = 12582912.0
CW1, CW2, CW3 = 6.28271484375, 0.0004704594612121582, 3.968374073792802e-09
NEG = -1e30
LN_EPS = 1e-5


class Ctx:
    ENGINES = ("pe", "dve", "act", "pool", "sp")

    def __init__(self, nc, es):
        self.nc = nc
        self.es = es
        self.q = {e: [] for e in self.ENGINES}
        self.sems = {}
        self.cnt = {}
        for s in ("pe", "dve", "act", "pool", "bar"):
            self.mksem(s)

    def mksem(self, s):
        self.sems[s] = self.es.enter_context(self.nc.semaphore(s))
        self.cnt[s] = 0

    def op(self, eng, fn, inc=None, incn=1):
        if inc is None:
            self.q[eng].append(fn)
            return None
        h = self.sems[inc]
        self.cnt[inc] += incn
        self.q[eng].append(lambda e, fn=fn, h=h, n=incn: fn(e).then_inc(h, n))
        return (inc, self.cnt[inc])

    def wait(self, eng, tok):
        if tok is None:
            return
        s, v = tok
        if v <= 0:
            return
        h = self.sems[s]
        self.q[eng].append(lambda e, h=h, v=v: e.wait_ge(h, v))

    def drain(self, eng):
        self.q[eng].append(lambda e: e.drain())

    def milestone(self, eng):
        return (eng, self.cnt[eng])

    def barrier(self, name="", queues=()):
        for q in queues:
            q.flush()
        h = self.sems["bar"]
        tgt = self.cnt["bar"] + len(self.ENGINES)
        for eng in self.ENGINES:
            self.q[eng].append(lambda e: e.drain())
            self.q[eng].append(lambda e, h=h: e.nop().then_inc(h, 1))
            self.q[eng].append(lambda e, h=h, t=tgt: e.wait_ge(h, t))
        self.cnt["bar"] = tgt

    def finalize(self):
        with self.nc.Block() as block:
            @block.tensor
            def _(e):
                for f in self.q["pe"]:
                    f(e)

            @block.vector
            def _(e):
                for f in self.q["dve"]:
                    f(e)

            @block.scalar
            def _(e):
                for f in self.q["act"]:
                    f(e)

            @block.gpsimd
            def _(e):
                for f in self.q["pool"]:
                    f(e)

            @block.sync
            def _(e):
                for f in self.q["sp"]:
                    f(e)


class DmaQueue:

    def __init__(self, ctx, eng, name, nlanes):
        self.ctx, self.eng = ctx, eng
        self.lanes = []
        for i in range(nlanes):
            s = f"{name}{i}"
            ctx.mksem(s)
            self.lanes.append(s)
        self.i = 0

    def issue(self, fn):
        s = self.lanes[self.i % len(self.lanes)]
        self.i += 1
        self.ctx.wait(self.eng, (s, self.ctx.cnt[s]))
        return self.ctx.op(self.eng, fn, inc=s, incn=16)

    def flush(self):
        for s in self.lanes:
            self.ctx.wait(self.eng, (s, self.ctx.cnt[s]))


class Arena:
    def __init__(self, ar, base, limit):
        self.ar, self.base, self.limit, self.off = ar, base, limit, base

    def bf(self, cols):
        o = self.off
        assert o + cols <= self.limit, f"arena overflow {o}+{cols}>{self.limit}"
        self.off = o + cols
        return self.ar[:, o:o + cols]

    def f32(self, words):
        if self.off % 2:
            self.off += 1
        o = self.off
        assert o + 2 * words <= self.limit, f"arena overflow {o}+{2*words}>{self.limit}"
        self.off = o + 2 * words
        return self.ar[:, o:o + 2 * words].bitcast(F32)


class Env:
    pass


ALL_PARTS = ("l0a", "l0f1", "l0b", "l0f2", "l1a", "l1f1", "l1b", "l1f2")


def build(T=4096, parts=ALL_PARTS):
    nc = bass.Bass("TRN2", target_bir_lowering=False)
    es = ExitStack()
    es.__enter__()
    env = Env()
    env.T, env.TC, env.NT = T, T // 512, T // 128
    D = {}

    def din(name, shape, dt=F32):
        D[name] = nc.dram_tensor(name, list(shape), dt, kind="ExternalInput").ap()

    din("x", (T, 1024))
    din("L0a_Wq", (1024, 1024))
    din("L0f_Wr", (1024, 8)); din("L0f_br", (8,))
    din("L0f_g", (1024,)); din("L0f_b", (1024,))
    din("L0f_W1", (8, 1024, 4096)); din("L0f_b1", (8, 4096))
    din("L0f_W2", (8, 4096, 1024)); din("L0f_b2", (8, 1024))
    din("L0b_WQ", (16, 1024, 64)); din("L0b_bQ", (16, 64))
    din("L0b_WK", (16, 1024, 64)); din("L0b_bK", (16, 64))
    din("L0b_WV", (16, 1024, 64)); din("L0b_bV", (16, 64))
    din("L0b_WO", (16, 64, 1024)); din("L0b_bO", (1024,))
    din("L1a_W1", (1024, 1024)); din("L1a_W2", (1024, 1024))
    din("L1a_bias", (1024,)); din("L1a_et", (21, 1024))
    din("L1f_CB", (1024, 1024)); din("L1f_SCB", (1024,)); din("L1f_bias", (1024,))
    din("L1b_Win", (4480, 1024)); din("L1b_dtb", (32,))
    din("L1b_Bb", (32, 128)); din("L1b_Cb", (32, 128))
    din("L1b_Bnw", (128,)); din("L1b_Cnw", (128,)); din("L1b_Dsk", (32,))
    din("L1b_Wout", (1024, 2048))
    din("c_eye", (128, 128))
    din("c_rope_cos", (128, T)); din("c_rope_sin", (128, T)); din("c_ropeflip", (128, 128))
    din("c_cmask_qk", (4 * 128, 512)); din("c_cmask_kq", (4 * 128, 512))
    din("c_lstrict", (128, 128))
    din("c_sel8", (32, 1024)); din("c_selmask", (32, 4))
    env.D = D

    env.out = nc.dram_tensor("out", [T, 1024], F32, kind="ExternalOutput").ap()
    env.resid = nc.dram_tensor("resid", [1024, T], F32, kind="Internal").ap()
    env.xn_hbm = nc.dram_tensor("xn_hbm", [1024, T], BF16, kind="Internal").ap()
    env.s_qT = nc.dram_tensor("s_qT", [1024, T], BF16, kind="Internal").ap()
    env.s_kT = nc.dram_tensor("s_kT", [1024, T], BF16, kind="Internal").ap()
    env.s_vT = nc.dram_tensor("s_vT", [1024, T], BF16, kind="Internal").ap()
    env.s_zT = nc.dram_tensor("s_zT", [2048, T], BF16, kind="Internal").ap()
    env.s_xtok = nc.dram_tensor("s_xtok", [T, 2048], BF16, kind="Internal").ap()
    env.s_aT = nc.dram_tensor("s_aT", [T, T], BF16, kind="Internal").ap()
    env.s_sch = nc.dram_tensor("s_sch", [448, T], F32, kind="Internal").ap()
    env.dbg = nc.dram_tensor("dbg", [128, 8192], F32, kind="Internal").ap()
    env.s_ztok = nc.dram_tensor("s_ztok", [T, 2048], BF16, kind="Internal").ap()

    ctx = Ctx(nc, es)
    env.ctx, env.nc = ctx, nc
    env.qsp = DmaQueue(ctx, "sp", "qs", 6)
    env.qwt = DmaQueue(ctx, "pool", "qw", 6)
    env.qac = DmaQueue(ctx, "pool", "qa", 4)
    env.ALLQ = (env.qsp, env.qwt, env.qac)

    ARCOLS = 101376
    env.AR = es.enter_context(nc.sbuf_tensor("AR", [128, ARCOLS], BF16))
    env.SR = es.enter_context(nc.sbuf_tensor("SR", [128, 512], F32))
    env.id_f = es.enter_context(nc.sbuf_tensor("id_f", [128, 128], F32))
    env.id_b = es.enter_context(nc.sbuf_tensor("id_b", [128, 128], BF16))
    env.ones_r_f = es.enter_context(nc.sbuf_tensor("ones_r_f", [1, 128], F32))
    env.ones_c_b = es.enter_context(nc.sbuf_tensor("ones_c_b", [128, 1], BF16))
    env.cpool = es.enter_context(nc.sbuf_tensor("cpool", [128, 16], F32))
    env.stage = es.enter_context(nc.sbuf_tensor("stage", [128, 128], F32))
    env.ones_c_f = es.enter_context(nc.sbuf_tensor("ones_c_f", [128, 1], F32))
    env.stage_free = None
    env.lvc_evac = None
    env.PS = [es.enter_context(nc.psum_tensor(f"ps{i}", [128, 512], F32)) for i in range(7)]
    env.PB = es.enter_context(nc.psum_tensor("pb", [128, 512], BF16))

    env.CONSTS = [LN_EPS, 1.0, 0.5, np.pi / 2, 1e-4, 2.0]
    for j, v in enumerate(env.CONSTS):
        nc.const_aps.register(F32, float(v), env.cpool[:, j:j + 1])

    env.xn = [env.AR[:, i * T:(i + 1) * T] for i in range(8)]
    env.WORK_BASE = 8 * T
    env.ARCOLS = ARCOLS

    emit_prelude(env)
    emit_transpose_in(env)
    for p in parts:
        EMITTERS[p](env)
    emit_transpose_out(env)
    ctx.barrier("end", queues=env.ALLQ)
    ctx.finalize()
    env.es = es
    return nc, env


def work_arena(env):
    return Arena(env.AR, env.WORK_BASE, env.ARCOLS)


def emit_prelude(env):
    ctx = env.ctx
    env.qsp.issue(lambda e: e.dma_start(out=env.id_f[:], in_=env.D["c_eye"][:]))
    tok = env.qwt.issue(lambda e: e.dma_start(out=env.id_b[:], in_=env.D["c_eye"][:]))
    ctx.op("pool", lambda e: e.memset(env.ones_r_f[:], 1.0))
    ctx.op("pool", lambda e: e.memset(env.ones_c_b[:], 1.0))
    ctx.op("pool", lambda e: e.memset(env.ones_c_f[:], 1.0))
    for j, v in enumerate(env.CONSTS):
        ctx.op("pool", lambda e, j=j, v=v: e.memset(env.cpool[:, j:j + 1], float(v)), inc="pool")
    ctx.barrier("prelude", queues=env.ALLQ)


def emit_transpose_in(env):
    ctx = env.ctx
    ar = work_arena(env)
    xin = [ar.f32(1024) for _ in range(2)]
    ob = [ar.f32(128) for _ in range(4)]
    tr_done_for_slot = [None, None]
    ev_done = [None, None]
    st_done = [None] * 4
    for tt in range(env.NT):
        sl = tt % 2
        ctx.wait("sp", tr_done_for_slot[sl])
        ld = env.qsp.issue(lambda e, tt=tt, s=xin[sl]: e.dma_start(
            out=s, in_=env.D["x"][tt * 128:(tt + 1) * 128, :]))
        ctx.wait("pe", ld)
        for dt_ in range(8):
            ob_i = (tt * 8 + dt_) % 4
            pk = env.PS[dt_ % 2]
            ctx.wait("pe", ev_done[dt_ % 2])
            pev = ctx.op("pe", lambda e, s=xin[sl], dt_=dt_, pk=pk: e.transpose(
                pk[:, 0:128], s[:, dt_ * 128:(dt_ + 1) * 128], env.id_f[:]), inc="pe")
            ctx.wait("dve", pev)
            ctx.wait("dve", st_done[ob_i])
            ev_done[dt_ % 2] = ctx.op("dve", lambda e, pk=pk, o=ob[ob_i]: e.tensor_copy(
                o, pk[:, 0:128]), inc="dve")
            ctx.wait("sp", ev_done[dt_ % 2])
            st_done[ob_i] = env.qsp.issue(lambda e, o=ob[ob_i], dt_=dt_, tt=tt: e.dma_start(
                out=env.resid[dt_ * 128:(dt_ + 1) * 128, tt * 128:(tt + 1) * 128], in_=o))
        tr_done_for_slot[sl] = ctx.milestone("pe")
    ctx.barrier("transpose_in", queues=env.ALLQ)


def emit_transpose_out(env):
    ctx = env.ctx
    ar = work_arena(env)
    strip = [ar.f32(env.T) for _ in range(2)]
    ob = [ar.f32(128) for _ in range(4)]
    tr_done_for_slot = [None, None]
    ev_done = [None, None]
    st_done = [None] * 4
    for dt_ in range(8):
        sl = dt_ % 2
        ctx.wait("sp", tr_done_for_slot[sl])
        ld = env.qsp.issue(lambda e, dt_=dt_, s=strip[sl]: e.dma_start(
            out=s, in_=env.resid[dt_ * 128:(dt_ + 1) * 128, :]))
        ctx.wait("pe", ld)
        for tt in range(env.NT):
            ob_i = (dt_ * env.NT + tt) % 4
            pk = env.PS[tt % 2]
            ctx.wait("pe", ev_done[tt % 2])
            pev = ctx.op("pe", lambda e, s=strip[sl], tt=tt, pk=pk: e.transpose(
                pk[:, 0:128], s[:, tt * 128:(tt + 1) * 128], env.id_f[:]), inc="pe")
            ctx.wait("dve", pev)
            ctx.wait("dve", st_done[ob_i])
            ev_done[tt % 2] = ctx.op("dve", lambda e, pk=pk, o=ob[ob_i]: e.tensor_copy(
                o, pk[:, 0:128]), inc="dve")
            ctx.wait("sp", ev_done[tt % 2])
            st_done[ob_i] = env.qsp.issue(lambda e, o=ob[ob_i], dt_=dt_, tt=tt: e.dma_start(
                out=env.out[tt * 128:(tt + 1) * 128, dt_ * 128:(dt_ + 1) * 128], in_=o))
        tr_done_for_slot[sl] = ctx.milestone("pe")
    ctx.barrier("transpose_out", queues=env.ALLQ)


def emit_ln(env, ar, src_bf16_tiles=None, gamma_beta_cols=None, dst_tiles=None,
            to_hbm=None, keep_rows=None):
    ctx, T, TC = env.ctx, env.T, env.TC
    dst = dst_tiles or env.xn
    NH = 2 if T >= 2048 else 1
    HW = T // NH
    CHW = HW // 512
    xsq = [ar.bf(HW) for _ in range(8)]
    bc_mu = ar.f32(HW)
    bc_rs = ar.f32(HW)
    s1 = ar.f32(T)
    s2 = ar.f32(T)
    PS = env.PS

    if src_bf16_tiles is None:
        ld = [env.qwt.issue(lambda e, k=k: e.dma_start(
            out=dst[k], in_=env.resid[k * 128:(k + 1) * 128, :])) for k in range(8)]
        srcs = dst
    else:
        ld = [None] * 8
        srcs = src_bf16_tiles

    ev1 = ev2 = None
    stats_pe = None
    for h in range(NH):
        sqd = [None] * 8
        for k in range(8):
            ctx.wait("act", ld[k])
            ctx.wait("act", stats_pe)
            sqd[k] = ctx.op("act", lambda e, k=k, h=h: e.activation(
                xsq[k], srcs[k][:, h * HW:(h + 1) * HW], AF.Square), inc="act")
        for k in range(8):
            ctx.wait("pe", sqd[k])
            ctx.wait("pe", ld[k])
        ctx.wait("pe", ctx.milestone("pool"))
        for c in range(CHW):
            ctx.wait("pe", ev1)
            ctx.wait("pe", ev2)
            for k in range(8):
                ctx.op("pe", lambda e, s=srcs[k], c=c, k=k, h=h: e.matmul(
                    PS[4][0:1, :], env.ones_c_b[:, 0:1],
                    s[:, h * HW + c * 512:h * HW + (c + 1) * 512],
                    start=(k == 0), stop=(k == 7)), inc="pe" if k == 7 else None)
                mm = ctx.op("pe", lambda e, c=c, k=k: e.matmul(
                    PS[5][0:1, :], env.ones_c_b[:, 0:1], xsq[k][:, c * 512:(c + 1) * 512],
                    start=(k == 0), stop=(k == 7)), inc="pe" if k == 7 else None)
            ctx.wait("dve", mm)
            gc0 = h * HW + c * 512
            ev1 = ctx.op("dve", lambda e, g=gc0: e.tensor_copy(
                s1[0:1, g:g + 512], PS[4][0:1, :]), inc="dve")
            ev2 = ctx.op("dve", lambda e, g=gc0: e.tensor_copy(
                s2[0:1, g:g + 512], PS[5][0:1, :]), inc="dve")
        stats_pe = ctx.milestone("pe")
    ctx.wait("dve", ev1)
    ctx.wait("dve", ev2)
    ctx.op("dve", lambda e: e.tensor_scalar(s1[0:1, 0:T], s1[0:1, 0:T], 1.0 / 1024.0, None, ALU.mult))
    ctx.drain("dve")
    ctx.op("dve", lambda e: e.tensor_scalar(s2[0:1, 0:T], s2[0:1, 0:T], 1.0 / 1024.0, None, ALU.mult))
    ctx.drain("dve")
    vsub = None
    for h in range(NH):
        ctx.op("dve", lambda e, h=h: e.tensor_mul(
            bc_mu[0:1, 0:HW], s1[0:1, h * HW:(h + 1) * HW], s1[0:1, h * HW:(h + 1) * HW]))
        ctx.drain("dve")
        vsub = ctx.op("dve", lambda e, h=h: e.tensor_sub(
            s2[0:1, h * HW:(h + 1) * HW], s2[0:1, h * HW:(h + 1) * HW], bc_mu[0:1, 0:HW]),
            inc="dve")
        ctx.drain("dve")
    if keep_rows is not None:
        ctx.op("dve", lambda e: e.tensor_scalar(
            keep_rows[0][0:1, 0:T], s1[0:1, 0:T], -1.0, None, ALU.mult))
        ctx.drain("dve")
    ctx.wait("act", vsub)
    va = ctx.op("act", lambda e: e.activation(s2[0:1, 0:T], s2[0:1, 0:T], AF.Sqrt, bias=LN_EPS), inc="act")
    ctx.wait("dve", va)
    if keep_rows is not None:
        ctx.op("dve", lambda e: e.tensor_copy(keep_rows[1][0:1, 0:T], s2[0:1, 0:T]))
        ctx.drain("dve")
    vd = ctx.op("dve", lambda e: e.reciprocal(s2[0:1, 0:T], s2[0:1, 0:T]), inc="dve")
    ctx.wait("pe", vd)
    nrm_last = None
    nrm_done = [[None] * 8 for _ in range(NH)]
    st_tok = [None] * 8
    for h in range(NH):
        bce = None
        for c in range(CHW):
            if h > 0 or c > 0:
                ctx.wait("pe", nrm_last)
            ctx.op("pe", lambda e, h=h, c=c: e.matmul(
                PS[4][:, :], env.ones_r_f[0:1, :], s2[0:1, h * HW + c * 512:h * HW + (c + 1) * 512],
                start=True, stop=True), inc="pe")
            mm = ctx.op("pe", lambda e, h=h, c=c: e.matmul(
                PS[5][:, :], env.ones_r_f[0:1, :], s1[0:1, h * HW + c * 512:h * HW + (c + 1) * 512],
                start=True, stop=True), inc="pe")
            ctx.wait("dve", mm)
            ctx.op("dve", lambda e, c=c: e.tensor_copy(bc_rs[:, c * 512:(c + 1) * 512], PS[4][:, :]), inc="dve")
            bce = ctx.op("dve", lambda e, c=c: e.tensor_copy(bc_mu[:, c * 512:(c + 1) * 512], PS[5][:, :]), inc="dve")
            for k in range(8):
                src_k = srcs[k]
                gsp = h * HW + c * 512
                ctx.drain("dve")
                ctx.op("dve", lambda e, s=src_k, c=c, g=gsp: e.tensor_sub(
                    PS[6][:, :], s[:, g:g + 512], bc_mu[:, c * 512:(c + 1) * 512]))
                ctx.drain("dve")
                if gamma_beta_cols is None:
                    nrm_done[h][k] = ctx.op("dve", lambda e, k=k, c=c, g=gsp: e.tensor_mul(
                        dst[k][:, g:g + 512], PS[6][:, :], bc_rs[:, c * 512:(c + 1) * 512]),
                        inc="dve")
                else:
                    gcl, bcl = gamma_beta_cols
                    ctx.op("dve", lambda e, c=c: e.tensor_mul(
                        PS[6][:, :], PS[6][:, :], bc_rs[:, c * 512:(c + 1) * 512]))
                    ctx.drain("dve")
                    nrm_done[h][k] = ctx.op("dve", lambda e, k=k, g=gsp, gg=gcl[k], bb=bcl[k]: e.tensor_scalar(
                        dst[k][:, g:g + 512], PS[6][:, :], gg, bb, ALU.mult, ALU.add),
                        inc="dve")
                ctx.drain("dve")
            nrm_last = ctx.milestone("dve")
    if to_hbm is not None:
        for k in range(8):
            ctx.wait("sp", nrm_done[NH - 1][k])
            ctx.wait("sp", nrm_done[0][k])
            st_tok[k] = env.qsp.issue(lambda e, k=k: e.dma_start(
                out=to_hbm[k * 128:(k + 1) * 128, :], in_=dst[k]))
    ctx.barrier("ln", queues=env.ALLQ)


def load_vec_cols(env, dram_vec, dst_block, n, pk=None):
    ctx = env.ctx
    pk = pk if pk is not None else env.PS[6]
    ctx.wait("sp", env.stage_free)
    ld = env.qsp.issue(lambda e: e.dma_start(
        out=env.stage[0:n, 0:128], in_=dram_vec.rearrange("(j p) -> j p", p=128)))
    ctx.wait("pe", ld)
    ctx.wait("pe", env.lvc_evac)
    pev = ctx.op("pe", lambda e: e.transpose(pk[:, 0:n], env.stage[0:n, 0:128],
                                             env.id_f[0:n, 0:n]), inc="pe")
    env.stage_free = pev
    ctx.wait("dve", pev)
    tok = ctx.op("dve", lambda e: e.tensor_copy(dst_block, pk[:, 0:n]), inc="dve")
    env.lvc_evac = tok
    return tok


def load_cols(env, ctx, dram_vec, dst_sr_cols, n=8):
    toks = []
    for k in range(n):
        toks.append(env.qsp.issue(lambda e, k=k: e.dma_start(
            out=dst_sr_cols[k], in_=dram_vec[k * 128:(k + 1) * 128].rearrange("(p o) -> p o", o=1))))
    return toks


def emit_l1_ffn(env):
    ctx, T, TC = env.ctx, env.T, env.TC
    ar = work_arena(env)
    emit_ln(env, ar)
    ar2 = work_arena(env)
    cbl = [ar2.bf(1024) for _ in range(8)]
    cbt = [ar2.bf(1024) for _ in range(8)]
    outs = [ar2.f32(512) for _ in range(4)]
    scb = [env.SR[:, k:k + 1] for k in range(8)]
    bia = [env.SR[:, 8 + k:9 + k] for k in range(8)]
    ld_cb = []
    for ntile in range(8):
        ld_cb.append(env.qwt.issue(lambda e, n=ntile: e.dma_start(
            out=cbl[n], in_=env.D["L1f_CB"][n * 128:(n + 1) * 128, :])))
    t1 = load_cols(env, ctx, env.D["L1f_SCB"], scb)
    t2 = load_cols(env, ctx, env.D["L1f_bias"], bia)
    for t in t1 + t2:
        ctx.wait("act", t)
    va = ctx.op("act", lambda e: e.activation(
        env.SR[:, 0:8], env.SR[:, 0:8], AF.Copy, scale=1.0 / 127.0), inc="act")
    ev = None
    for ntile in range(8):
        ctx.wait("pe", ld_cb[ntile])
        for k in range(8):
            ctx.wait("pe", ev)
            pev = ctx.op("pe", lambda e, n=ntile, k=k: e.transpose(
                env.PB[:, 0:128], cbl[n][:, k * 128:(k + 1) * 128], env.id_b[:]), inc="pe")
            ctx.wait("dve", pev)
            ev = ctx.op("dve", lambda e, n=ntile, k=k: e.tensor_copy(
                cbt[k][:, n * 128:(n + 1) * 128], env.PB[:, 0:128]), inc="dve")
    ctx.wait("pe", ev)
    ctx.wait("pe", va)
    ctx.wait("dve", va)
    ev_done = [None, None]
    st_done = [None] * 4
    for c in range(TC):
        for ntile in range(8):
            pk = env.PS[(c * 8 + ntile) % 2]
            ctx.wait("pe", ev_done[(c * 8 + ntile) % 2])
            for k in range(8):
                mm = ctx.op("pe", lambda e, n=ntile, k=k, c=c, pk=pk: e.matmul(
                    pk[:, :], cbt[k][:, n * 128:(n + 1) * 128],
                    env.xn[k][:, c * 512:(c + 1) * 512],
                    start=(k == 0), stop=(k == 7)), inc="pe" if k == 7 else None)
            o_i = (c * 8 + ntile) % 4
            ctx.wait("dve", mm)
            ctx.wait("dve", st_done[o_i])
            ev_done[(c * 8 + ntile) % 2] = ctx.op("dve", lambda e, pk=pk, o=outs[o_i], n=ntile: e.tensor_scalar(
                o, pk[:, :], scb[n], bia[n], ALU.mult, ALU.add), inc="dve")
            ctx.wait("pool", ev_done[(c * 8 + ntile) % 2])
            st_done[o_i] = env.qac.issue(lambda e, o=outs[o_i], n=ntile, c=c: e.dma_start(
                out=env.resid[n * 128:(n + 1) * 128, c * 512:(c + 1) * 512], in_=o,
                accum_op=ALU.add))
    ctx.barrier("l1ffn", queues=env.ALLQ)


def emit_l1_attn_a(env):
    ctx, T, TC = env.ctx, env.T, env.TC
    ar = work_arena(env)
    emit_ln(env, ar)
    ar2 = work_arena(env)
    w1 = [ar2.bf(1024) for _ in range(8)]
    w2 = [ar2.bf(1024) for _ in range(8)]
    etl = ar2.f32(1024)
    ett = [ar2.f32(21) for _ in range(8)]
    accD = ar2.f32(512)
    accP = ar2.f32(512)
    h = [ar2.bf(512) for _ in range(8)]
    outs = [ar2.f32(512) for _ in range(4)]
    bia = [env.SR[:, 16 + k:17 + k] for k in range(8)]

    ldw = []
    for k in range(8):
        ldw.append(env.qwt.issue(lambda e, k=k: e.dma_start(
            out=w1[k], in_=env.D["L1a_W1"][k * 128:(k + 1) * 128, :])))
    for k in range(8):
        ldw.append(env.qwt.issue(lambda e, k=k: e.dma_start(
            out=w2[k], in_=env.D["L1a_W2"][k * 128:(k + 1) * 128, :])))
    ldb = load_cols(env, ctx, env.D["L1a_bias"], bia)
    lde = env.qsp.issue(lambda e: e.dma_start(out=etl[0:21, :], in_=env.D["L1a_et"][:, :]))
    ctx.wait("pe", lde)
    ev = None
    for k in range(8):
        pk = env.PS[k % 2]
        ctx.wait("pe", ev)
        pev = ctx.op("pe", lambda e, k=k, pk=pk: e.transpose(
            pk[:, 0:21], etl[0:21, k * 128:(k + 1) * 128], env.id_f[0:21, 0:21]), inc="pe")
        ctx.wait("dve", pev)
        ev = ctx.op("dve", lambda e, k=k, pk=pk: e.tensor_copy(ett[k], pk[:, 0:21]), inc="dve")
    conv_barrier = ev

    mm_prev = None
    ev_done = [None, None]
    st_done = [None] * 4
    h_done = {}
    for c in range(TC):
        for k in range(8):
            eng = "dve"
            acc = accD if k % 2 == 0 else accP
            ctx.wait(eng, conv_barrier)
            ctx.wait(eng, mm_prev)
            ctx.drain(eng)
            ctx.op(eng, lambda e, acc=acc: e.memset(acc, 0.0))
            ctx.drain(eng)
            for j in range(1, 21):
                lo = c * 512 - j
                if lo + 512 <= 0:
                    continue
                srccol = max(0, lo)
                dstcol = c * 512 - j - srccol
                width = 512 - (srccol - lo)
                dst0 = 512 - width
                ctx.op(eng, lambda e, acc=acc, k=k, j=j, srccol=srccol, w=width, d0=dst0:
                       e.scalar_tensor_tensor(acc[:, d0:d0 + w], env.xn[k][:, srccol:srccol + w],
                                              ett[k][:, j:j + 1], acc[:, d0:d0 + w],
                                              ALU.mult, ALU.add))
                ctx.drain(eng)
            h_done[(c, k)] = ctx.op(eng, lambda e, acc=acc, k=k: e.tensor_copy(h[k], acc),
                                    inc=eng)
        for t in ldw:
            ctx.wait("pe", t)
        for k in range(8):
            ctx.wait("pe", h_done[(c, k)])
        for mt in range(8):
            pk = env.PS[(c * 8 + mt) % 2]
            ctx.wait("pe", ev_done[(c * 8 + mt) % 2])
            for k in range(8):
                ctx.op("pe", lambda e, k=k, mt=mt, c=c, pk=pk: e.matmul(
                    pk[:, :], w1[k][:, mt * 128:(mt + 1) * 128],
                    env.xn[k][:, c * 512:(c + 1) * 512],
                    start=(k == 0), stop=False))
            for k in range(8):
                mm = ctx.op("pe", lambda e, k=k, mt=mt, c=c, pk=pk: e.matmul(
                    pk[:, :], w2[k][:, mt * 128:(mt + 1) * 128], h[k],
                    start=False, stop=(k == 7)), inc="pe" if k == 7 else None)
            o_i = (c * 8 + mt) % 4
            ctx.wait("dve", mm)
            ctx.wait("dve", st_done[o_i])
            for t in ldb:
                ctx.wait("dve", t)
            ev_done[(c * 8 + mt) % 2] = ctx.op("dve", lambda e, pk=pk, o=outs[o_i], mt=mt:
                e.tensor_scalar(o, pk[:, :], bia[mt], None, ALU.add), inc="dve")
            ctx.wait("pool", ev_done[(c * 8 + mt) % 2])
            st_done[o_i] = env.qac.issue(lambda e, o=outs[o_i], mt=mt, c=c: e.dma_start(
                out=env.resid[mt * 128:(mt + 1) * 128, c * 512:(c + 1) * 512], in_=o,
                accum_op=ALU.add))
        mm_prev = ctx.milestone("pe")
    ctx.barrier("l1attn_a", queues=env.ALLQ)


def emit_l0_attn_a(env):
    ctx, T, TC, NT = env.ctx, env.T, env.TC, env.NT
    PS = env.PS
    ar = work_arena(env)
    emit_ln(env, ar)
    ar2 = work_arena(env)
    wqt = [ar2.bf(1024) for _ in range(8)]
    gbl = [ar2.bf(1024) for _ in range(8)]
    xtok = [ar2.bf(1024) for _ in range(NT)]
    ar3 = Arena(env.AR, ar2.off, env.ARCOLS)
    wql = [ar3.bf(1024) for _ in range(8)]
    ldw = []
    for ot in range(8):
        ldw.append(env.qwt.issue(lambda e, o=ot: e.dma_start(
            out=wql[o], in_=env.D["L0a_Wq"][o * 128:(o + 1) * 128, :])))
    ev = None
    for ot in range(8):
        ctx.wait("pe", ldw[ot])
        for hc in range(8):
            ctx.wait("pe", ev)
            pev = ctx.op("pe", lambda e, o=ot, hc=hc: e.transpose(
                env.PB[:, 0:128], wql[o][:, hc * 128:(hc + 1) * 128], env.id_b[:]), inc="pe")
            ctx.wait("dve", pev)
            ev = ctx.op("dve", lambda e, o=ot, hc=hc: e.tensor_copy(
                wqt[hc][:, o * 128:(o + 1) * 128], env.PB[:, 0:128]), inc="dve")
    for oc in range(8):
        for tt in range(NT):
            ctx.wait("pe", ev)
            pev = ctx.op("pe", lambda e, oc=oc, tt=tt: e.transpose(
                env.PB[:, 0:128], env.xn[oc][:, tt * 128:(tt + 1) * 128], env.id_b[:]), inc="pe")
            ctx.wait("dve", pev)
            ev = ctx.op("dve", lambda e, oc=oc, tt=tt: e.tensor_copy(
                xtok[tt][:, oc * 128:(oc + 1) * 128], env.PB[:, 0:128]), inc="dve")
    ctx.wait("pe", ev)
    gevb = [None] * 7
    gmm = [None] * 8
    for bh in range(2):
        for grp in (range(0, 7), range(7, 8)):
            for tt in range(NT):
                for ai in grp:
                    bk = ai % 7
                    if tt == 0:
                        ctx.wait("pe", gevb[bk])
                    gmm[ai] = ctx.op("pe", lambda e, tt=tt, ai=ai, bk=bk, bh=bh: e.matmul(
                        PS[bk][:, 0:512], xtok[tt][:, ai * 128:(ai + 1) * 128],
                        xtok[tt][:, bh * 512:(bh + 1) * 512],
                        start=(tt == 0), stop=(tt == NT - 1)),
                        inc="pe" if tt == NT - 1 else None)
            for ai in grp:
                bk = ai % 7
                ctx.wait("dve", gmm[ai])
                ctx.drain("dve")
                gevb[bk] = ctx.op("dve", lambda e, ai=ai, bk=bk, bh=bh: e.tensor_copy(
                    gbl[ai][:, bh * 512:(bh + 1) * 512], PS[bk][:, 0:512]), inc="dve")
    gev_last = gevb[0]
    ar4 = Arena(env.AR, ar2.off, env.ARCOLS)
    qt = [ar4.bf(512) for _ in range(8)]
    outs = [ar4.f32(512) for _ in range(4)]
    qt_done = [None] * 8
    ev_done = [None, None]
    st_done = [None] * 4
    ymm_last = None
    ctx.wait("pe", gev_last)
    for c in range(TC):
        for ot in range(8):
            pk = PS[ot % 2]
            ctx.wait("pe", qt_done[ot - 2] if ot >= 2 else None)
            for kh in range(8):
                mm = ctx.op("pe", lambda e, o=ot, kh=kh, c=c, pk=pk: e.matmul(
                    pk[:, :], wqt[kh][:, o * 128:(o + 1) * 128],
                    env.xn[kh][:, c * 512:(c + 1) * 512],
                    start=(kh == 0), stop=(kh == 7)), inc="pe" if kh == 7 else None)
            ctx.wait("act", mm)
            ctx.wait("act", ymm_last)
            qt_done[ot] = ctx.op("act", lambda e, o=ot, pk=pk: e.copy(qt[o], pk[:, :]), inc="act")
        for ot in range(8):
            ctx.wait("pe", qt_done[ot])
        for dt_ in range(8):
            pk = PS[2 + dt_ % 2]
            ctx.wait("pe", ev_done[dt_ % 2])
            for ai in range(8):
                mm = ctx.op("pe", lambda e, ai=ai, dt_=dt_, pk=pk: e.matmul(
                    pk[:, :], gbl[ai][:, dt_ * 128:(dt_ + 1) * 128], qt[ai],
                    start=(ai == 0), stop=(ai == 7)), inc="pe" if ai == 7 else None)
            ymm_last = mm
            o_i = dt_ % 4
            ctx.wait("dve", mm)
            ctx.wait("dve", st_done[o_i])
            ev_done[dt_ % 2] = ctx.op("dve", lambda e, pk=pk, o=outs[o_i]: e.tensor_copy(
                o, pk[:, :]), inc="dve")
            ctx.wait("pool", ev_done[dt_ % 2])
            st_done[o_i] = env.qac.issue(lambda e, o=outs[o_i], dt_=dt_, c=c: e.dma_start(
                out=env.resid[dt_ * 128:(dt_ + 1) * 128, c * 512:(c + 1) * 512], in_=o,
                accum_op=ALU.add))
    ctx.barrier("l0a", queues=env.ALLQ)


def emit_l0_ffn(env):
    ctx, T, TC, NT = env.ctx, env.T, env.TC, env.NT
    TCM = T // 256
    PS = env.PS
    AXX = mybir.AxisListType.X
    MGT_COLS = 2 * 8 * NT + 8
    arp = Arena(env.AR, env.ARCOLS - MGT_COLS, env.ARCOLS)
    mgt = arp.f32(8 * NT)
    ar_rows = Arena(env.AR, env.WORK_BASE, env.ARCOLS - MGT_COLS)
    rowsA = ar_rows.f32(T)
    rowsB = ar_rows.f32(T)
    ar_ln = Arena(env.AR, ar_rows.off, env.ARCOLS - MGT_COLS)
    emit_ln(env, ar_ln, keep_rows=(rowsA, rowsB))
    arR = Arena(env.AR, ar_rows.off, env.ARCOLS - MGT_COLS)
    wr = arR.f32(64)
    brrow = arR.f32(8)
    wsumr = arR.f32(8)
    lgc = [arR.f32(512) for _ in range(2)]
    ltok = arR.f32(8 * NT)
    rstdtok = arR.f32(NT)
    emt = arR.f32(8)
    cls = arR.f32(4)
    xfR = [arR.f32(T) for _ in range(2)]
    c_lmax, c_nlm, c_sum, c_gate = (cls[:, j:j + 1] for j in range(4))

    ldwr = env.qsp.issue(lambda e: e.dma_start(
        out=wr.rearrange("p (k e) -> p k e", e=8),
        in_=env.D["L0f_Wr"].rearrange("(k p) e -> p k e", p=128)))
    ldbr = env.qsp.issue(lambda e: e.dma_start(
        out=brrow[0:1, 0:8], in_=env.D["L0f_br"].rearrange("(o e) -> o e", o=1)))
    ctx.wait("pe", ldwr)
    for k in range(8):
        mm = ctx.op("pe", lambda e, k=k: e.matmul(
            PS[5][0:1, 0:8], env.ones_c_f[:, 0:1], wr[:, k * 8:(k + 1) * 8],
            start=(k == 0), stop=(k == 7)), inc="pe" if k == 7 else None)
    ctx.wait("dve", mm)
    vw = ctx.op("dve", lambda e: e.tensor_copy(wsumr[0:1, 0:8], PS[5][0:1, 0:8]), inc="dve")
    rtr = None
    for tt in range(NT):
        ctx.wait("pe", rtr)
        pev = ctx.op("pe", lambda e, tt=tt: e.transpose(
            PS[6][:, 0:1], rowsB[0:1, tt * 128:(tt + 1) * 128], env.id_f[0:1, 0:1]), inc="pe")
        ctx.wait("dve", pev)
        rtr = ctx.op("dve", lambda e, tt=tt: e.tensor_copy(
            rstdtok[:, tt:tt + 1], PS[6][:, 0:1]), inc="dve")
    ctx.wait("dve", rtr)
    rtr = ctx.op("dve", lambda e: e.reciprocal(rstdtok[:, 0:NT], rstdtok[:, 0:NT]), inc="dve")
    import os as _os
    NH = 2 if TC >= int(_os.environ.get("RNH", "4")) else 1
    CH = TC // NH
    HW = T // NH
    ctx.wait("pe", vw)
    ctx.wait("pe", ldbr)
    slot_pe = [None, None]
    lg_evac = {}
    tok_ev = None
    for h in range(NH):
        ld = [None, None]
        for k in range(8):
            sl = k % 2
            ctx.wait("sp", slot_pe[sl])
            ld[sl] = env.qsp.issue(lambda e, k=k, h=h, s=xfR[sl]: e.dma_start(
                out=s[:, 0:HW], in_=env.resid[k * 128:(k + 1) * 128, h * HW:(h + 1) * HW]))
            ctx.wait("pe", ld[sl])
            if k == 0:
                for cl in range(CH):
                    ctx.wait("pe", lg_evac.get(cl))
            for cl in range(CH):
                mm = ctx.op("pe", lambda e, k=k, cl=cl, s=xfR[sl]: e.matmul(
                    PS[cl][0:8, :], wr[:, k * 8:(k + 1) * 8], s[:, cl * 512:(cl + 1) * 512],
                    start=(k == 0), stop=False), inc="pe" if cl == CH - 1 else None)
            slot_pe[sl] = mm
        for cl in range(CH):
            span0 = h * HW + cl * 512
            ctx.op("pe", lambda e, cl=cl, s0=span0: e.matmul(
                PS[cl][0:8, :], wsumr[0:1, 0:8], rowsA[0:1, s0:s0 + 512],
                start=False, stop=False))
            mm = ctx.op("pe", lambda e, cl=cl, s0=span0: e.matmul(
                PS[cl][0:8, :], brrow[0:1, 0:8], rowsB[0:1, s0:s0 + 512],
                start=False, stop=True), inc="pe")
            ctx.wait("dve", mm)
            vlg = ctx.op("dve", lambda e, cl=cl, b=lgc[cl % 2]: e.tensor_copy(
                b[0:8, :], PS[cl][0:8, :]), inc="dve")
            lg_evac[cl] = vlg
            for j in range(4):
                tt = (span0 // 128) + j
                ctx.wait("pe", vlg)
                ctx.wait("pe", tok_ev)
                pev = ctx.op("pe", lambda e, j=j, b=lgc[cl % 2]: e.transpose(
                    PS[4][:, 0:8], b[0:8, j * 128:(j + 1) * 128], env.id_f[0:8, 0:8]), inc="pe")
                ctx.wait("dve", pev)
                tok_ev = ctx.op("dve", lambda e, tt=tt: e.tensor_copy(
                    ltok[:, tt * 8:(tt + 1) * 8], PS[4][:, 0:8]), inc="dve")
    ctx.wait("dve", rtr)
    ctx.wait("dve", tok_ev)
    mg_done = None
    for tt in range(NT):
        lt = ltok[:, tt * 8:(tt + 1) * 8]
        ctx.drain("dve")
        ctx.op("dve", lambda e, lt=lt, tt=tt: e.tensor_scalar(
            lt, lt, rstdtok[:, tt:tt + 1], None, ALU.mult))
        ctx.drain("dve")
        ctx.op("dve", lambda e, lt=lt: e.reduce_max(c_lmax, lt, AXX))
        ctx.drain("dve")
        va = ctx.op("dve", lambda e: e.tensor_scalar(c_nlm, c_lmax, -1.0, None, ALU.mult),
                    inc="dve")
        ctx.wait("act", va)
        vex = ctx.op("act", lambda e, lt=lt: e.activation(
            emt[:, 0:8], lt, AF.Exp, bias=c_nlm), inc="act")
        ctx.wait("dve", vex)
        ctx.op("dve", lambda e: e.reduce_sum(c_sum, emt[:, 0:8], AXX))
        ctx.drain("dve")
        ctx.op("dve", lambda e: e.reciprocal(c_gate, c_sum))
        ctx.drain("dve")
        mg_done = ctx.op("dve", lambda e, lt=lt, tt=tt: e.tensor_scalar(
            mgt[:, tt * 8:(tt + 1) * 8], lt, c_lmax, c_gate, ALU.is_equal, ALU.mult),
            inc="dve")
    ctx.barrier("l0f_router", queues=env.ALLQ)
    gtok = load_vec_cols(env, env.D["L0f_g"], env.SR[:, 24:32], 8)
    btok = load_vec_cols(env, env.D["L0f_b"], env.SR[:, 32:40], 8)
    ctx.wait("dve", gtok)
    ctx.wait("dve", btok)
    gcols = [env.SR[:, 24 + k:25 + k] for k in range(8)]
    bcols = [env.SR[:, 32 + k:33 + k] for k in range(8)]
    ar_I = Arena(env.AR, ar_rows.off, env.ARCOLS - MGT_COLS)
    emit_ln(env, ar_I, src_bf16_tiles=env.xn, gamma_beta_cols=(gcols, bcols),
            dst_tiles=env.xn, to_hbm=env.xn_hbm)
    ar_E = Arena(env.AR, 0, env.ARCOLS - MGT_COLS)
    w1 = [ar_E.bf(4096) for _ in range(8)]
    w2 = [ar_E.bf(1024) for _ in range(32)]
    hT = [ar_E.bf(256) for _ in range(32)]
    xinA = [ar_E.bf(2048) for _ in range(2)]
    outs = [ar_E.f32(256) for _ in range(4)]
    mrow = ar_E.f32(T)
    mbc = [ar_E.f32(256) for _ in range(2)]
    b1c = [env.SR[:, 48 + j:49 + j] for j in range(32)]
    b2c = [env.SR[:, 40 + j:41 + j] for j in range(8)]
    h_ev = [None, None]
    y_ev = [None, None]
    st_done = [None] * 4
    mb_ev = [None, None]
    xin_pe = [None, None]
    mrow_done = None
    w_pe = None
    bias_users = None
    for ex in range(8):
        ldw1, ldw2 = [], []
        if ex > 0:
            ctx.wait("pool", w_pe)
            ctx.wait("sp", bias_users)
        for kd in range(8):
            ldw1.append(env.qwt.issue(lambda e, ex=ex, kd=kd: e.dma_start(
                out=w1[kd], in_=env.D["L0f_W1"][ex, kd * 128:(kd + 1) * 128, :])))
        for ft in range(32):
            ldw2.append(env.qwt.issue(lambda e, ex=ex, ft=ft: e.dma_start(
                out=w2[ft], in_=env.D["L0f_W2"][ex, ft * 128:(ft + 1) * 128, :])))
        t1 = load_vec_cols(env, env.D["L0f_b1"][ex, :], env.SR[:, 48:80], 32, pk=PS[6])
        t2 = load_vec_cols(env, env.D["L0f_b2"][ex, :], env.SR[:, 40:48], 8, pk=PS[6])
        ctx.wait("pe", mg_done)
        for tt in range(NT):
            ctx.wait("pe", mrow_done)
            pev = ctx.op("pe", lambda e, tt=tt, ex=ex: e.transpose(
                PS[5][0:1, 0:128], mgt[:, tt * 8 + ex:tt * 8 + ex + 1], env.id_f[:, :]), inc="pe")
            ctx.wait("dve", pev)
            mrow_done = ctx.op("dve", lambda e, tt=tt: e.tensor_copy(
                mrow[0:1, tt * 128:(tt + 1) * 128], PS[5][0:1, 0:128]), inc="dve")
        for t in ldw1 + ldw2:
            ctx.wait("pe", t)
        ctx.wait("act", t1)
        ctx.wait("dve", t2)
        ctx.wait("pe", mrow_done)
        for tc in range(TCM):
            sl = tc % 2
            ctx.wait("sp", xin_pe[sl])
            ldx = env.qsp.issue(lambda e, tc=tc, sl=sl: e.dma_start(
                out=xinA[sl].rearrange("p (k t) -> p k t", t=256),
                in_=env.xn_hbm[:, tc * 256:(tc + 1) * 256].rearrange("(k p) t -> p k t", p=128)))
            ctx.wait("pe", ldx)
            ctx.wait("pe", mb_ev[sl])
            mmb = ctx.op("pe", lambda e, tc=tc: e.matmul(
                PS[4][:, 0:256], env.ones_r_f[0:1, :], mrow[0:1, tc * 256:(tc + 1) * 256],
                start=True, stop=True), inc="pe")
            ctx.wait("dve", mmb)
            mb_ev[sl] = ctx.op("dve", lambda e, sl=sl: e.tensor_copy(
                mbc[sl], PS[4][:, 0:256]), inc="dve")
            for ft in range(32):
                pk = PS[ft % 2]
                ctx.wait("pe", h_ev[ft % 2])
                for kd in range(8):
                    mm = ctx.op("pe", lambda e, ft=ft, kd=kd, sl=sl, pk=pk: e.matmul(
                        pk[:, 0:256], w1[kd][:, ft * 128:(ft + 1) * 128],
                        xinA[sl][:, kd * 256:(kd + 1) * 256],
                        start=(kd == 0), stop=(kd == 7)), inc="pe" if kd == 7 else None)
                ctx.wait("act", mm)
                h_ev[ft % 2] = ctx.op("act", lambda e, ft=ft, pk=pk: e.activation(
                    hT[ft], pk[:, 0:256], AF.Relu, bias=b1c[ft]), inc="act")
            for ft in range(32):
                ctx.wait("pe", h_ev[ft % 2])
            ctx.wait("dve", mb_ev[sl])
            for dt_ in range(8):
                pk = PS[2 + dt_ % 2]
                ctx.wait("pe", y_ev[dt_ % 2])
                for ft in range(32):
                    mm = ctx.op("pe", lambda e, ft=ft, dt_=dt_, pk=pk: e.matmul(
                        pk[:, 0:256], w2[ft][:, dt_ * 128:(dt_ + 1) * 128], hT[ft],
                        start=(ft == 0), stop=(ft == 31)), inc="pe" if ft == 31 else None)
                o_i = (tc * 8 + dt_) % 4
                ctx.wait("dve", mm)
                ctx.wait("dve", st_done[o_i])
                y_ev[dt_ % 2] = ctx.op("dve", lambda e, pk=pk, o=outs[o_i], dt_=dt_, sl=sl:
                    e.scalar_tensor_tensor(o, pk[:, 0:256], b2c[dt_], mbc[sl],
                                           ALU.add, ALU.mult), inc="dve")
                ctx.wait("pool", y_ev[dt_ % 2])
                st_done[o_i] = env.qac.issue(lambda e, o=outs[o_i], dt_=dt_, tc=tc: e.dma_start(
                    out=env.resid[dt_ * 128:(dt_ + 1) * 128, tc * 256:(tc + 1) * 256], in_=o,
                    accum_op=ALU.add))
            xin_pe[sl] = ctx.milestone("pe")
        w_pe = ctx.milestone("pe")
        bias_users = ctx.milestone("dve")
    ctx.barrier("l0f", queues=env.ALLQ)



def emit_l0_attn_b(env):
    ctx, T, TC, NT = env.ctx, env.T, env.TC, env.NT
    PS = env.PS
    AXX = mybir.AxisListType.X
    ar = work_arena(env)
    emit_ln(env, ar)
    arP = work_arena(env)
    cosb = arP.f32(T)
    sinb = arP.f32(T)
    flipb = arP.bf(128)
    wq = [arP.bf(128) for _ in range(8)]
    wk = [arP.bf(128) for _ in range(8)]
    wv = [arP.bf(128) for _ in range(8)]
    qb = arP.bf(512)
    t1 = arP.f32(512)
    t2 = arP.f32(512)
    rot = [arP.bf(512) for _ in range(2)]
    bqc = env.SR[:, 0:8]
    bkc = env.SR[:, 8:16]
    bvc = env.SR[:, 16:24]
    lc1 = env.qsp.issue(lambda e: e.dma_start(out=cosb, in_=env.D["c_rope_cos"][:, :]))
    lc2 = env.qsp.issue(lambda e: e.dma_start(out=sinb, in_=env.D["c_rope_sin"][:, :]))
    lc3 = env.qwt.issue(lambda e: e.dma_start(out=flipb, in_=env.D["c_ropeflip"][:, :]))
    tb1 = load_vec_cols(env, env.D["L0b_bQ"].rearrange("h d -> (h d)"), bqc, 8)
    tb2 = load_vec_cols(env, env.D["L0b_bK"].rearrange("h d -> (h d)"), bkc, 8)
    tb3 = load_vec_cols(env, env.D["L0b_bV"].rearrange("h d -> (h d)"), bvc, 8)
    ctx.wait("dve", lc1)
    ctx.wait("dve", lc2)
    ctx.wait("pe", lc3)
    w_pe = None
    st_tok = [None, None]
    pk_free = [None, None]
    for hp in range(8):
        ldq, ldk, ldv = [], [], []
        ctx.wait("pool", w_pe)
        for kd in range(8):
            for wt, nm, lst in ((wq, "L0b_WQ", ldq), (wk, "L0b_WK", ldk), (wv, "L0b_WV", ldv)):
                for j in range(2):
                    lst.append(env.qwt.issue(lambda e, wt=wt, nm=nm, kd=kd, j=j, hp=hp: e.dma_start(
                        out=wt[kd][:, j * 64:(j + 1) * 64],
                        in_=env.D[nm][2 * hp + j, kd * 128:(kd + 1) * 128, :])))
        for t in ldq + ldk + ldv:
            ctx.wait("pe", t)
        ctx.wait("dve", tb1); ctx.wait("dve", tb2); ctx.wait("dve", tb3)
        for tc in range(TC):
            for mat, wt, bc, dst, dorope in (
                    (0, wq, bqc, env.s_qT, True), (1, wk, bkc, env.s_kT, True),
                    (2, wv, bvc, env.s_vT, False)):
                pk = PS[mat % 2]
                ctx.wait("pe", pk_free[mat % 2])
                for kd in range(8):
                    mm = ctx.op("pe", lambda e, wt=wt, kd=kd, tc=tc, pk=pk: e.matmul(
                        pk[:, :], wt[kd], env.xn[kd][:, tc * 512:(tc + 1) * 512],
                        start=(kd == 0), stop=(kd == 7)), inc="pe" if kd == 7 else None)
                ctx.wait("dve", mm)
                ctx.drain("dve")
                va = ctx.op("dve", lambda e, pk=pk, bc=bc, hp=hp: e.tensor_scalar(
                    qb, pk[:, :], bc[:, hp:hp + 1], None, ALU.add), inc="dve")
                pk_free[mat % 2] = va
                if dorope:
                    pf = PS[4]
                    ctx.wait("pe", va)
                    mmf = ctx.op("pe", lambda e, pf=pf: e.matmul(
                        pf[:, :], flipb, qb, start=True, stop=True), inc="pe")
                    ctx.wait("dve", va)
                    ctx.wait("dve", mmf)
                    ctx.op("dve", lambda e, tc=tc: e.tensor_mul(
                        t1, qb, cosb[:, tc * 512:(tc + 1) * 512]))
                    ctx.drain("dve")
                    ctx.op("dve", lambda e, tc=tc, pf=pf: e.tensor_mul(
                        t2, pf[:, :], sinb[:, tc * 512:(tc + 1) * 512]))
                    ctx.drain("dve")
                    ctx.wait("dve", st_tok[mat % 2])
                    vr = ctx.op("dve", lambda e, mat=mat: e.tensor_add(rot[mat % 2], t1, t2), inc="dve")
                    src = rot[mat % 2]
                    ctx.wait("sp", vr)
                else:
                    ctx.wait("dve", va)
                    ctx.wait("dve", st_tok[0])
                    vr = ctx.op("dve", lambda e: e.tensor_copy(rot[0], qb), inc="dve")
                    src = rot[0]
                    ctx.wait("sp", vr)
                st_tok[mat % 2 if dorope else 0] = env.qsp.issue(
                    lambda e, src=src, dst=dst, hp=hp, tc=tc: e.dma_start(
                        out=dst[hp * 128:(hp + 1) * 128, tc * 512:(tc + 1) * 512], in_=src))
        w_pe = ctx.milestone("pe")
    ctx.barrier("l0b_proj", queues=env.ALLQ)
    arA = work_arena(env)
    ka = arA.bf(2 * T)
    va_ = arA.bf(T)
    qa = arA.bf(2 * T)
    vtok = [arA.bf(128) for _ in range(NT)]
    pbuf = [arA.bf(512) for _ in range(4)]
    sumr = [arA.f32(512), arA.f32(512)]
    rbc = arA.f32(512)
    chalf = [arA.f32(128), arA.f32(128)]
    zev = arA.bf(512)
    cmq = arA.f32(4 * 512)
    cmk = arA.f32(4 * 512)
    v1s = ctx.op("dve", lambda e: e.memset(ka[64:65, :], 1.0), inc="dve")
    ctx.wait("pe", v1s)
    ctx.op("pool", lambda e: e.memset(chalf[0][0:1, 0:64], 1.0))
    ctx.op("pool", lambda e: e.memset(chalf[0][0:1, 64:128], 0.0))
    ctx.op("pool", lambda e: e.memset(chalf[1][0:1, 0:64], 0.0))
    vhp = ctx.op("pool", lambda e: e.memset(chalf[1][0:1, 64:128], 1.0), inc="pool")
    lm1 = env.qsp.issue(lambda e: e.dma_start(
        out=cmq.rearrange("p (m s) -> p m s", s=512),
        in_=env.D["c_cmask_qk"].rearrange("(m p) s -> p m s", p=128)))
    lm2 = env.qsp.issue(lambda e: e.dma_start(
        out=cmk.rearrange("p (m s) -> p m s", s=512),
        in_=env.D["c_cmask_kq"].rearrange("(m p) s -> p m s", p=128)))
    ctx.wait("dve", lm1)
    ctx.wait("dve", lm2)
    zst = None
    zst_ev = [None, None]
    sum_free = [None, None]
    rbc_ev = None
    nq_ev = None
    arena_pe = None
    arena_dve = None
    for hp in range(8):
        ctx.wait("sp", arena_pe)
        ctx.wait("sp", zst)
        l1a = env.qsp.issue(lambda e, hp=hp: e.dma_start(
            out=ka[0:64, 0:T], in_=env.s_kT[hp * 128:hp * 128 + 64, :]))
        l1 = env.qsp.issue(lambda e, hp=hp: e.dma_start(
            out=ka[0:64, T:2 * T], in_=env.s_kT[hp * 128 + 64:(hp + 1) * 128, :]))
        l2 = env.qsp.issue(lambda e, hp=hp: e.dma_start(out=va_, in_=env.s_vT[hp * 128:(hp + 1) * 128, :]))
        l3a = env.qsp.issue(lambda e, hp=hp: e.dma_start(
            out=qa[0:64, 0:T], in_=env.s_qT[hp * 128:hp * 128 + 64, :]))
        l3 = env.qsp.issue(lambda e, hp=hp: e.dma_start(
            out=qa[0:64, T:2 * T], in_=env.s_qT[hp * 128 + 64:(hp + 1) * 128, :]))
        ctx.wait("pe", l2)
        ev = None
        for tt in range(NT):
            ctx.wait("pe", ev)
            pev = ctx.op("pe", lambda e, tt=tt: e.transpose(
                env.PB[:, 0:128], va_[:, tt * 128:(tt + 1) * 128], env.id_b[:]), inc="pe")
            ctx.wait("dve", pev)
            ctx.wait("dve", arena_pe)
            ev = ctx.op("dve", lambda e, tt=tt: e.tensor_copy(vtok[tt], env.PB[:, 0:128]), inc="dve")
        vtok_ev = ev
        ctx.wait("pe", l1a)
        ctx.wait("pe", l1)
        ctx.wait("pe", l3a)
        ctx.wait("pe", l3)
        bank_free = [None] * 4
        for hh in range(2):
            ho = hh * T
            for qt in range(NT):
                nkc = qt // 4 + 1
                for kc in range(nkc):
                    pk = PS[kc % 4]
                    ctx.wait("pe", bank_free[kc % 4])
                    mm = ctx.op("pe", lambda e, ho=ho, qt=qt, kc=kc, pk=pk: e.matmul(
                        pk[:, :], qa[0:64, ho + qt * 128:ho + (qt + 1) * 128],
                        ka[0:64, ho + kc * 512:ho + (kc + 1) * 512],
                        start=True, stop=True), inc="pe")
                    ctx.wait("dve", mm)
                    if kc == nkc - 1:
                        m = qt - 4 * kc
                        ctx.op("dve", lambda e, pk=pk, m=m: e.tensor_add(
                            pk[:, :], pk[:, :], cmq[:, m * 512:(m + 1) * 512]))
                        ctx.drain("dve")
                    ctx.drain("dve")
                    if kc == 0:
                        vm = ctx.op("dve", lambda e, pk=pk: e.reduce_max(
                            env.SR[:, 30:31], pk[:, :], AXX), inc="dve")
                    else:
                        ctx.op("dve", lambda e, pk=pk: e.reduce_max(
                            env.SR[:, 31:32], pk[:, :], AXX))
                        ctx.drain("dve")
                        vm = ctx.op("dve", lambda e: e.tensor_max(
                            env.SR[:, 30:31], env.SR[:, 30:31], env.SR[:, 31:32]), inc="dve")
                    bank_free[kc % 4] = vm
                ctx.drain("dve")
                vneg = ctx.op("dve", lambda e: e.tensor_scalar(
                    env.SR[:, 30:31], env.SR[:, 30:31], -1.0, None, ALU.mult), inc="dve")
                ctx.wait("pe", vneg)
                ctx.wait("pe", nq_ev)
                ptr = ctx.op("pe", lambda e: e.transpose(
                    PS[4][0:1, 0:128], env.SR[:, 30:31], env.id_f[:, :]), inc="pe")
                ctx.wait("dve", ptr)
                nq_ev = ctx.op("dve", lambda e, ho=ho, qt=qt: e.tensor_copy(
                    qa[64:65, ho + qt * 128:ho + (qt + 1) * 128], PS[4][0:1, 0:128]), inc="dve")
        ctx.wait("pe", vtok_ev)
        ctx.wait("pe", vhp)
        ctx.wait("pe", nq_ev)
        for qc in range(TC):
            nkt = 4 * qc + 4
            pz = PS[5] if qc % 2 == 0 else PS[2]
            ctx.wait("pe", zst_ev[qc % 2])
            score_free = [None, None]
            pbuf_free = [None] * 4
            pv_last = None
            for hh in range(2):
                ho = hh * T
                psum_s = PS[6] if hh == 0 else PS[3]
                ctx.wait("pe", sum_free[hh])

                def emit_sc(kt, ho=ho, qc=qc):
                    pk = PS[kt % 2]
                    ctx.wait("pe", score_free[kt % 2])
                    mm = ctx.op("pe", lambda e, ho=ho, kt=kt, qc=qc, pk=pk: e.matmul(
                        pk[:, :], ka[0:65, ho + kt * 128:ho + (kt + 1) * 128],
                        qa[0:65, ho + qc * 512:ho + (qc + 1) * 512],
                        start=True, stop=True), inc="pe")
                    ctx.wait("act", mm)
                    m = kt - 4 * qc
                    if 0 <= m < 4:
                        ctx.wait("dve", mm)
                        vmadd = ctx.op("dve", lambda e, pk=pk, m=m: e.tensor_add(
                            pk[:, :], pk[:, :], cmk[:, m * 512:(m + 1) * 512]), inc="dve")
                        ctx.wait("act", vmadd)
                    ctx.wait("act", pbuf_free[kt % 4])
                    vex = ctx.op("act", lambda e, pk=pk, b=pbuf[kt % 4]: e.activation(
                        b, pk[:, :], AF.Exp, scale=0.125), inc="act")
                    score_free[kt % 2] = vex
                    return vex

                vexs = {0: emit_sc(0)}
                for kt in range(nkt):
                    if kt + 1 < nkt:
                        vexs[kt + 1] = emit_sc(kt + 1)
                    ctx.wait("pe", vexs[kt])
                    ctx.op("pe", lambda e, kt=kt, hh=hh, b=pbuf[kt % 4], pz=pz, nkt=nkt: e.matmul(
                        pz[hh * 64:(hh + 1) * 64, :], vtok[kt][:, hh * 64:(hh + 1) * 64], b,
                        start=(kt == 0), stop=(kt == nkt - 1),
                        tile_position=(0, 64) if hh else None))
                    mms = ctx.op("pe", lambda e, kt=kt, b=pbuf[kt % 4], ps_=psum_s, nkt=nkt: e.matmul(
                        ps_[0:1, :], env.ones_c_b[:, 0:1], b,
                        start=(kt == 0), stop=(kt == nkt - 1)), inc="pe")
                    pbuf_free[kt % 4] = mms
                pv_last = mms
                ctx.wait("dve", mms)
                ctx.drain("dve")
                ctx.op("dve", lambda e, hh=hh, ps_=psum_s: e.tensor_copy(
                    sumr[hh][0:1, :], ps_[0:1, :]))
                ctx.drain("dve")
                vrc = ctx.op("dve", lambda e, hh=hh: e.reciprocal(
                    sumr[hh][0:1, :], sumr[hh][0:1, :]), inc="dve")
                sum_free[hh] = vrc
            ctx.wait("pe", vrc)
            ctx.wait("pe", rbc_ev)
            ctx.op("pe", lambda e: e.matmul(
                PS[4][:, :], chalf[0][0:1, :], sumr[0][0:1, :], start=True, stop=False))
            mmb = ctx.op("pe", lambda e: e.matmul(
                PS[4][:, :], chalf[1][0:1, :], sumr[1][0:1, :], start=False, stop=True), inc="pe")
            ctx.wait("dve", mmb)
            ctx.wait("dve", pv_last)
            vb = ctx.op("dve", lambda e: e.tensor_copy(rbc, PS[4][:, :]), inc="dve")
            rbc_ev = vb
            ctx.drain("dve")
            ctx.wait("dve", zst)
            vz = ctx.op("dve", lambda e, pz=pz: e.tensor_mul(zev, pz[:, :], rbc), inc="dve")
            zst_ev[qc % 2] = vz
            ctx.wait("sp", vz)
            zst = env.qsp.issue(lambda e, hp=hp, qc=qc: e.dma_start(
                out=env.s_qT[hp * 128:(hp + 1) * 128, qc * 512:(qc + 1) * 512], in_=zev))
        arena_pe = ctx.milestone("pe")
        arena_dve = ctx.milestone("dve")
    ctx.barrier("l0b_attn", queues=env.ALLQ)
    arO = work_arena(env)
    wo = [arO.bf(1024) for _ in range(8)]
    zbuf = [arO.bf(4096) for _ in range(2)]
    outs = [arO.f32(512) for _ in range(4)]
    boc = env.SR[:, 0:8]
    ldo = []
    for hp in range(8):
        for j in range(2):
            ldo.append(env.qwt.issue(lambda e, hp=hp, j=j: e.dma_start(
                out=wo[hp][j * 64:(j + 1) * 64, :], in_=env.D["L0b_WO"][2 * hp + j, :, :])))
    tbo = load_vec_cols(env, env.D["L0b_bO"], boc, 8)
    for t in ldo:
        ctx.wait("pe", t)
    zld = [None, None]
    ev_done = [None, None]
    st_done = [None] * 4
    for qc in range(TC):
        sl = qc % 2
        ctx.wait("sp", zld[sl])
        zl = env.qsp.issue(lambda e, qc=qc, sl=sl: e.dma_start(
            out=zbuf[sl].rearrange("p (hp q) -> p hp q", q=512),
            in_=env.s_qT[:, qc * 512:(qc + 1) * 512].rearrange("(hp p) q -> p hp q", p=128)))
        ctx.wait("pe", zl)
        for mt in range(8):
            pk = PS[mt % 2]
            ctx.wait("pe", ev_done[mt % 2])
            for hp in range(8):
                mm = ctx.op("pe", lambda e, hp=hp, mt=mt, sl=sl, pk=pk: e.matmul(
                    pk[:, :], wo[hp][:, mt * 128:(mt + 1) * 128],
                    zbuf[sl][:, hp * 512:(hp + 1) * 512],
                    start=(hp == 0), stop=(hp == 7)), inc="pe" if hp == 7 else None)
            o_i = (qc * 8 + mt) % 4
            ctx.wait("dve", mm)
            ctx.wait("dve", st_done[o_i])
            ctx.wait("dve", tbo)
            ev_done[mt % 2] = ctx.op("dve", lambda e, pk=pk, o=outs[o_i], mt=mt: e.tensor_scalar(
                o, pk[:, :], boc[:, mt:mt + 1], None, ALU.add), inc="dve")
            ctx.wait("pool", ev_done[mt % 2])
            st_done[o_i] = env.qac.issue(lambda e, o=outs[o_i], mt=mt, qc=qc: e.dma_start(
                out=env.resid[mt * 128:(mt + 1) * 128, qc * 512:(qc + 1) * 512], in_=o,
                accum_op=ALU.add))
        zld[sl] = ctx.milestone("pe")
    ctx.barrier("l0b", queues=env.ALLQ)



EMITTERS = {
    "l1f1": emit_l1_ffn,
    "l1f2": emit_l1_ffn,
    "l1a": emit_l1_attn_a,
    "l0a": emit_l0_attn_a,
    "l0f1": emit_l0_ffn,
    "l0f2": emit_l0_ffn,
    "l0b": emit_l0_attn_b,
}


EMITTERS["l1b"] = lambda env: emit_l1_attn_b(env)


EMITTERS["l1b"] = lambda env: emit_l1_attn_b(env)


def emit_l1_attn_b(env):
    ctx, T, TC, NT = env.ctx, env.T, env.TC, env.NT
    PS = env.PS
    ar0 = work_arena(env)
    rowsA = ar0.f32(T)
    rowsB = ar0.f32(T)
    ar_ln = Arena(env.AR, ar0.off, env.ARCOLS)
    emit_ln(env, ar_ln, keep_rows=(rowsA, rowsB))

    ar1 = Arena(env.AR, ar0.off, env.ARCOLS)
    wl = [ar1.bf(1024) for _ in range(2)]
    wzx = [ar1.bf(4096) for _ in range(8)]
    oev = [ar1.bf(512) for _ in range(4)]
    ld = [None, None]
    tr_ev = None
    for rt in range(32):
        sl = rt % 2
        ctx.wait("pool", tr_ev)
        ld[sl] = env.qwt.issue(lambda e, rt=rt, s=wl[sl]: e.dma_start(
            out=s, in_=env.D["L1b_Win"][rt * 128:(rt + 1) * 128, :]))
        ctx.wait("pe", ld[sl])
        for kd in range(8):
            ctx.wait("pe", tr_ev)
            pev = ctx.op("pe", lambda e, s=wl[sl], kd=kd: e.transpose(
                env.PB[:, 0:128], s[:, kd * 128:(kd + 1) * 128], env.id_b[:]), inc="pe")
            ctx.wait("dve", pev)
            tr_ev = ctx.op("dve", lambda e, kd=kd, rt=rt: e.tensor_copy(
                wzx[kd][:, rt * 128:(rt + 1) * 128], env.PB[:, 0:128]), inc="dve")
    ctx.wait("pe", tr_ev)
    ev_done = [None, None]
    st_done = [None] * 4
    for tt in range(NT):
        for rc in range(8):
            pk = PS[rc % 2]
            ctx.wait("pe", ev_done[rc % 2])
            for kd in range(8):
                mm = ctx.op("pe", lambda e, tt=tt, rc=rc, kd=kd, pk=pk: e.matmul(
                    pk[:, :], env.xn[kd][:, tt * 128:(tt + 1) * 128],
                    wzx[kd][:, rc * 512:(rc + 1) * 512],
                    start=(kd == 0), stop=(kd == 7)), inc="pe" if kd == 7 else None)
            o_i = (tt * 8 + rc) % 4
            ctx.wait("dve", mm)
            ctx.wait("dve", st_done[o_i])
            ev_done[rc % 2] = ctx.op("dve", lambda e, pk=pk, o=oev[o_i]: e.tensor_copy(
                o, pk[:, :]), inc="dve")
            ctx.wait("sp", ev_done[rc % 2])
            dst = env.s_ztok if rc < 4 else env.s_xtok
            c0 = (rc % 4) * 512
            st_done[o_i] = env.qsp.issue(lambda e, o=oev[o_i], tt=tt, c0=c0, dst=dst: e.dma_start(
                out=dst[tt * 128:(tt + 1) * 128, c0:c0 + 512], in_=o))
    ctx.barrier("l1b_s1a", queues=env.ALLQ)

    ar2 = Arena(env.AR, ar0.off, env.ARCOLS)
    wsc_l = ar2.f32(1024)
    wsc = [ar2.f32(448) for _ in range(8)]
    sevac = [ar2.f32(512) for _ in range(2)]
    bcm = ar2.f32(512)
    bcr = ar2.f32(512)
    rr = ar2.f32(512)
    xf32 = [ar2.f32(512) for _ in range(2)]
    blocks = [(4096, 64, 0, 2), (4097, 63, 32, 2), (4160, 64, 64, 1),
              (4224, 64, 128, 2), (4225, 63, 160, 2), (4288, 64, 192, 1),
              (4448, 32, 256, 1), (4448, 32, 288, 1), (4352, 32, 320, 1),
              (4384, 32, 352, 1), (4416, 32, 384, 1)]
    ld_st = None
    tr_ev = None
    for (r0, span, dcol, step) in blocks:
        ctx.wait("sp", tr_ev)
        if step == 2:
            n = 32
            src = env.D["L1b_Win"][r0:r0 + span, :].rearrange(
                "(d two) k -> d two k", two=2)[:, 0] if span % 2 == 0 else None
            if src is None:
                src = env.D["L1b_Win"][r0:r0 + span + 1, :].rearrange(
                    "(d two) k -> d two k", two=2)[:, 0]
        else:
            n = span
            src = env.D["L1b_Win"][r0:r0 + span, :]
        ld_st = env.qsp.issue(lambda e, src=src, n=n: e.dma_start(out=wsc_l[0:n, :], in_=src))
        ctx.wait("pe", ld_st)
        for kd in range(8):
            ctx.wait("pe", tr_ev)
            pev = ctx.op("pe", lambda e, n=n, kd=kd: e.transpose(
                PS[4][:, 0:n], wsc_l[0:n, kd * 128:(kd + 1) * 128], env.id_f[0:n, 0:n]), inc="pe")
            ctx.wait("dve", pev)
            tr_ev = ctx.op("dve", lambda e, n=n, kd=kd, dcol=dcol: e.tensor_copy(
                wsc[kd][:, dcol:dcol + n], PS[4][:, 0:n]), inc="dve")
    ctx.wait("pe", tr_ev)
    GROUPS = ((0, 128), (128, 128), (256, 128), (384, 32))
    ev = [None, None]
    stv = [None, None]
    nrm_prev = None
    ld2 = [None, None]
    for tc in range(TC):
        ctx.drain("dve")
        vr = ctx.op("dve", lambda e, tc=tc: e.reciprocal(
            rr[0:1, :], rowsB[0:1, tc * 512:(tc + 1) * 512]), inc="dve")
        ctx.wait("pe", vr)
        ctx.op("pe", lambda e, tc=tc: e.matmul(
            PS[5][:, :], env.ones_r_f[0:1, :], rowsA[0:1, tc * 512:(tc + 1) * 512],
            start=True, stop=True))
        mmq = ctx.op("pe", lambda e: e.matmul(
            PS[6][:, :], env.ones_r_f[0:1, :], rr[0:1, :], start=True, stop=True), inc="pe")
        ctx.wait("dve", mmq)
        ctx.op("dve", lambda e: e.tensor_copy(bcm, PS[5][:, :]))
        ctx.drain("dve")
        vb = ctx.op("dve", lambda e: e.tensor_copy(bcr, PS[6][:, :]), inc="dve")
        ctx.wait("pe", vb)
        for kd in range(8):
            sl2 = kd % 2
            ctx.wait("sp", nrm_prev)
            ld2[sl2] = env.qsp.issue(lambda e, kd=kd, tc=tc, s=xf32[sl2]: e.dma_start(
                out=s, in_=env.resid[kd * 128:(kd + 1) * 128, tc * 512:(tc + 1) * 512]))
            ctx.wait("dve", ld2[sl2])
            ctx.drain("dve")
            ctx.op("dve", lambda e, s=xf32[sl2]: e.tensor_add(s, s, bcm))
            ctx.drain("dve")
            vn2 = ctx.op("dve", lambda e, s=xf32[sl2]: e.tensor_mul(s, s, bcr), inc="dve")
            ctx.wait("pe", vn2)
            for st_i, (c0, nr) in enumerate(GROUPS):
                mm = ctx.op("pe", lambda e, kd=kd, c0=c0, nr=nr, st_i=st_i, s=xf32[sl2]: e.matmul(
                    PS[st_i][0:nr, :], wsc[kd][:, c0:c0 + nr], s,
                    start=(kd == 0), stop=(kd == 7)), inc="pe" if st_i == 3 else None)
            nrm_prev = mm
        for st_i, (c0, nr) in enumerate(GROUPS):
            ctx.wait("dve", mm)
            ctx.wait("dve", stv[st_i % 2])
            ev[st_i % 2] = ctx.op("dve", lambda e, sl=st_i % 2, st_i=st_i, nr=nr: e.tensor_copy(
                sevac[sl][0:nr, :], PS[st_i][0:nr, :]), inc="dve")
            ctx.wait("sp", ev[st_i % 2])
            stv[st_i % 2] = env.qsp.issue(lambda e, sl=st_i % 2, c0=c0, nr=nr, tc=tc: e.dma_start(
                out=env.s_sch[c0:c0 + nr, tc * 512:(tc + 1) * 512], in_=sevac[sl][0:nr, :]))
            if st_i < 3:
                ctx.wait("pe", ev[st_i % 2])
    ctx.barrier("l1b_s1b", queues=env.ALLQ)

    arp = Arena(env.AR, 0, env.ARCOLS)
    bnt = arp.bf(T)
    cnt = arp.bf(T)
    lgct = arp.f32(32 * NT)
    cat = arp.f32(32 * NT)
    gt = arp.f32(32 * NT)
    sht = arp.f32(32 * NT)
    wq2 = arp.f32(32)
    PERSIST_END = arp.off

    T2 = min(1024, T)
    NSP = T // T2
    C2 = T2 // 512
    dtbc = env.SR[0:32, 100:101]
    dskc = env.SR[0:32, 101:102]
    bnwc = env.SR[:, 102:103]
    cnwc_ = env.SR[:, 103:104]
    l1 = env.qsp.issue(lambda e: e.dma_start(
        out=env.stage[0:32, 0:1], in_=env.D["L1b_dtb"].rearrange("(a b) -> a b", b=1)))
    l2 = env.qsp.issue(lambda e: e.dma_start(
        out=env.stage[0:32, 1:2], in_=env.D["L1b_Dsk"].rearrange("(a b) -> a b", b=1)))
    ctx.wait("dve", l1)
    ctx.wait("dve", l2)
    vc = ctx.op("dve", lambda e: e.tensor_copy(env.SR[0:32, 100:102], env.stage[0:32, 0:2]), inc="dve")
    for vecn, col in (("L1b_Bnw", 102), ("L1b_Cnw", 103)):
        for (o0, nn, p0, stp) in ((0, 64, 0, 2), (1, 63, 32, 2), (64, 64, 64, 1)):
            if stp == 2:
                src = env.D[vecn][o0:o0 + nn + (nn % 2)].rearrange("(d two) -> d two", two=2)[:, 0:1]
                nrows = 32
            else:
                src = env.D[vecn][o0:o0 + nn].rearrange("(d o) -> d o", o=1)
                nrows = nn
            env.qsp.issue(lambda e, src=src, col=col, p0=p0, nr=nrows: e.dma_start(
                out=env.SR[p0:p0 + nr, col:col + 1], in_=src,
                allow_slow_non_contiguous=True))
    ctx.barrier("l1b_s2pre", queues=env.ALLQ)

    ar3 = Arena(env.AR, PERSIST_END, env.ARCOLS)
    W2b = T2 + 8
    schh = ar3.f32(448 if False else 0)
    sB = ar3.f32(T2)
    sC = ar3.f32(T2)
    sRD = ar3.f32(W2b)
    sAT = ar3.f32(W2b)
    bDT = ar3.f32(W2b)
    bASP = ar3.f32(W2b)
    bTR = ar3.f32(W2b)
    bLAM = ar3.f32(W2b)
    bEN = ar3.f32(W2b)
    bG = ar3.f32(W2b)
    bCA = ar3.f32(W2b)
    bLGC = ar3.f32(W2b)
    bSC = ar3.f32(W2b)
    bZ = ar3.f32(W2b)
    bAN = ar3.f32(W2b)
    bRA = ar3.f32(W2b)
    sinE = ar3.f32(T2)
    cosE = ar3.f32(T2)
    sqb = ar3.bf(T2)
    rmsr = ar3.f32(T2)
    ibc = ar3.f32(T2)
    p2b = ar3.f32(T2)
    bDM = ar3.f32(T2)
    t1s = ar3.f32(512)
    t2s = ar3.f32(512)
    flip2 = ar3.bf(128)
    carry = ar3.f32(4)

    lf2 = env.qwt.issue(lambda e: e.dma_start(out=flip2, in_=env.D["c_ropeflip"][:, :]))
    ctx.op("pool", lambda e: e.memset(sinE[64:128, 0:T2], 0.0))
    vp = ctx.op("pool", lambda e: e.memset(cosE[64:128, 0:T2], 1.0), inc="pool")
    wq_ev = None
    for mi, vecn in enumerate(("L1b_Bb", "L1b_Cb")):
        ctx.wait("sp", env.stage_free)
        toks = []
        for (o0, nn, p0, stp) in ((0, 64, 0, 2), (1, 63, 32, 2), (64, 64, 64, 1)):
            if stp == 2:
                src = env.D[vecn][:, o0:o0 + nn + (nn % 2)].rearrange(
                    "h (d two) -> h d two", two=2)[:, :, 0]
                ncols = 32
            else:
                src = env.D[vecn][:, o0:o0 + nn]
                ncols = nn
            toks.append(env.qsp.issue(lambda e, src=src, p0=p0, nc=ncols: e.dma_start(
                out=env.stage[0:32, p0:p0 + nc], in_=src,
                allow_slow_non_contiguous=True)))
        for t in toks:
            ctx.wait("pe", t)
        ctx.wait("pe", env.lvc_evac)
        pev = ctx.op("pe", lambda e: e.transpose(
            PS[4][:, 0:32], env.stage[0:32, 0:128], env.id_f[0:32, 0:32]), inc="pe")
        env.stage_free = pev
        ctx.wait("dve", pev)
        if mi == 0:
            wq_ev = ctx.op("dve", lambda e: e.tensor_copy(env.stage[:, 64:96].bitcast(F32)[:, 0:16] if False else env.SR[:, 104:120], PS[4][:, 0:32]) if False else e.tensor_copy(env.SR[:, 104:136], PS[4][:, 0:32]), inc="dve")
            env.lvc_evac = wq_ev
        else:
            ctx.wait("dve", wq_ev)
            wq_ev = ctx.op("dve", lambda e: e.tensor_mul(wq2, PS[4][:, 0:32], env.SR[:, 104:136]), inc="dve")
            env.lvc_evac = wq_ev
    bnt_done = None
    ctx.wait("pe", wq_ev)
    ctx.wait("dve", vp)
    span_dve = None
    span_act = None
    for sp in range(NSP):
        t0 = sp * T2
        ext = 1 if (sp + 1 < NSP) else 0
        ctx.wait("sp", span_dve)
        ctx.wait("sp", span_act)
        lds = []
        lds.append(env.qsp.issue(lambda e, t0=t0: e.dma_start(
            out=sB, in_=env.s_sch[0:128, t0:t0 + T2])))
        lds.append(env.qsp.issue(lambda e, t0=t0: e.dma_start(
            out=sC, in_=env.s_sch[128:256, t0:t0 + T2])))
        lds.append(env.qsp.issue(lambda e, t0=t0, ext=ext: e.dma_start(
            out=sRD[0:96, 0:T2 + ext], in_=env.s_sch[256:352, t0:t0 + T2 + ext])))
        lds.append(env.qsp.issue(lambda e, t0=t0, ext=ext: e.dma_start(
            out=sAT[0:64, 0:T2 + ext], in_=env.s_sch[352:416, t0:t0 + T2 + ext])))
        for t in lds:
            ctx.wait("dve", t)
            ctx.wait("act", t)
        W = T2 + ext
        dtr = sRD[64:96, 0:W]

        def softplus(dst, srcap, w, biascol=None, addconst=None):
            scr = bSC[0:32, 0:w]
            if biascol is not None:
                v = ctx.op("dve", lambda e: e.tensor_scalar(scr, srcap, biascol, None, ALU.add), inc="dve")
            else:
                v = ctx.op("dve", lambda e: e.tensor_copy(scr, srcap), inc="dve")
            ctx.wait("act", v)
            ctx.op("act", lambda e: e.activation(dst, scr, AF.Abs))
            ctx.drain("act")
            ctx.op("act", lambda e: e.activation(dst, dst, AF.Exp, scale=-1.0))
            ctx.drain("act")
            va = ctx.op("act", lambda e: e.activation(dst, dst, AF.Ln, bias=1.0), inc="act")
            ctx.wait("dve", va)
            vm = ctx.op("dve", lambda e: e.tensor_scalar_max(scr, scr, 0.0), inc="dve")
            ctx.drain("dve")
            v2 = ctx.op("dve", lambda e: e.tensor_add(dst, dst, scr), inc="dve")
            ctx.drain("dve")
            if addconst is not None:
                v2 = ctx.op("dve", lambda e: e.tensor_scalar(dst, dst, addconst, None, ALU.add), inc="dve")
                ctx.drain("dve")
            return v2

        ctx.wait("dve", vc)
        ctx.wait("dve", bnt_done)
        ctx.drain("dve")
        softplus(bDT[0:32, 0:W], dtr, W, biascol=dtbc)
        softplus(bASP[0:32, 0:W], sAT[0:32, 0:W], W, addconst=1e-4)
        ctx.op("dve", lambda e, W=W: e.scalar_tensor_tensor(
            bASP[0:32, 0:W], bDT[0:32, 0:W], -1.0, bASP[0:32, 0:W], ALU.mult, ALU.mult))
        ctx.drain("dve")
        v = ctx.op("dve", lambda e, W=W: e.tensor_copy(bTR[0:32, 0:W], sAT[32:64, 0:W]), inc="dve")
        ctx.wait("act", v)
        va = ctx.op("act", lambda e, W=W: e.activation(bTR[0:32, 0:W], bTR[0:32, 0:W], AF.Sigmoid), inc="act")
        ctx.wait("dve", va)
        ctx.op("dve", lambda e, W=W: e.tensor_mul(bLAM[0:32, 0:W], bDT[0:32, 0:W], bTR[0:32, 0:W]))
        ctx.drain("dve")
        ctx.op("dve", lambda e, W=W: e.tensor_scalar(bLAM[0:32, 0:W], bLAM[0:32, 0:W], 0.5, None, ALU.mult))
        ctx.drain("dve")
        ctx.op("dve", lambda e, W=W: e.tensor_sub(bDT[0:32, 0:W], bDT[0:32, 0:W], bLAM[0:32, 0:W]))
        ctx.drain("dve")
        v = ctx.op("dve", lambda e, W=W: e.tensor_copy(bEN[0:32, 0:W], bASP[0:32, 0:W]), inc="dve")
        ctx.wait("act", v)
        va = ctx.op("act", lambda e, W=W: e.activation(bEN[0:32, 0:W], bEN[0:32, 0:W], AF.Exp, scale=-1.0), inc="act")
        ctx.wait("dve", va)
        ctx.op("dve", lambda e, W=W: e.tensor_mul(bSC[0:32, 0:W], bLAM[0:32, 0:W], bEN[0:32, 0:W]))
        ctx.drain("dve")
        ctx.op("dve", lambda e, W=W: e.tensor_copy(bG[0:32, 0:T2], bDT[0:32, 0:T2]))
        ctx.drain("dve")
        nsh = T2 - 1 + ext
        ctx.op("dve", lambda e, nsh=nsh: e.tensor_add(
            bG[0:32, 0:nsh], bG[0:32, 0:nsh], bSC[0:32, 1:1 + nsh]))
        ctx.drain("dve")
        if sp == 0:
            ctx.op("pool", lambda e: e.memset(bZ[0:64, 0:T2], 0.0), inc="pool")
            ctx.wait("dve", ctx.milestone("pool"))
        init1 = 0.0 if sp == 0 else carry[0:32, 0:1]
        ctx.op("dve", lambda e, init1=init1: e.tensor_tensor_scan(
            bCA[0:32, 0:T2], bASP[0:32, 0:T2], bZ[0:32, 0:T2], init1, ALU.add, ALU.add))
        ctx.drain("dve")
        init2 = 0.0 if sp == 0 else carry[0:64, 2:3]
        ctx.op("dve", lambda e, init2=init2: e.tensor_tensor_scan(
            bAN[0:64, 0:T2], sRD[0:64, 0:T2], bZ[0:64, 0:T2], init2, ALU.add, ALU.add))
        ctx.drain("dve")
        ctx.op("dve", lambda e: e.tensor_copy(carry[0:32, 0:1], bCA[0:32, T2 - 1:T2]))
        ctx.drain("dve")
        ctx.op("dve", lambda e: e.tensor_copy(carry[0:64, 2:3], bAN[0:64, T2 - 1:T2]))
        ctx.drain("dve")
        v = ctx.op("dve", lambda e: e.tensor_copy(bLGC[0:32, 0:T2], bG[0:32, 0:T2]), inc="dve")
        ctx.wait("act", v)
        va = ctx.op("act", lambda e: e.activation(bLGC[0:32, 0:T2], bLGC[0:32, 0:T2], AF.Ln), inc="act")
        ctx.wait("dve", va)
        ctx.op("dve", lambda e: e.tensor_sub(bLGC[0:32, 0:T2], bLGC[0:32, 0:T2], bCA[0:32, 0:T2]))
        ctx.drain("dve")
        kbuf = bSC[0:64, 0:T2] if False else None
        ctx.op("dve", lambda e: e.tensor_scalar(bRA[0:64, 0:T2], bAN[0:64, 0:T2], 0.15915494309189535, None, ALU.mult))
        ctx.drain("dve")
        ctx.op("dve", lambda e: e.tensor_scalar(bRA[0:64, 0:T2], bRA[0:64, 0:T2], MAGIC, MAGIC, ALU.add, ALU.subtract))
        ctx.drain("dve")
        ctx.op("dve", lambda e: e.scalar_tensor_tensor(
            sinE[0:64, 0:T2], bRA[0:64, 0:T2], -CW1, bAN[0:64, 0:T2], ALU.mult, ALU.add))
        ctx.drain("dve")
        ctx.op("dve", lambda e: e.scalar_tensor_tensor(
            sinE[0:64, 0:T2], bRA[0:64, 0:T2], -CW2, sinE[0:64, 0:T2], ALU.mult, ALU.add))
        ctx.drain("dve")
        v = ctx.op("dve", lambda e: e.scalar_tensor_tensor(
            bRA[0:64, 0:T2], bRA[0:64, 0:T2], -CW3, sinE[0:64, 0:T2], ALU.mult, ALU.add), inc="dve")
        ctx.wait("act", v)
        vs = ctx.op("act", lambda e: e.activation(sinE[0:64, 0:T2], bRA[0:64, 0:T2], AF.Sin), inc="act")
        ctx.wait("dve", vs)
        ctx.op("dve", lambda e: e.tensor_scalar(bRA[0:64, 0:T2], bRA[0:64, 0:T2],
                                                1.5707963267948966, None, ALU.add))
        ctx.drain("dve")
        ctx.op("dve", lambda e: e.tensor_scalar(cosE[0:64, 0:T2], bRA[0:64, 0:T2],
                                                3.141592653589793, None, ALU.is_gt))
        ctx.drain("dve")
        v = ctx.op("dve", lambda e: e.scalar_tensor_tensor(
            bRA[0:64, 0:T2], cosE[0:64, 0:T2], -6.283185307179586, bRA[0:64, 0:T2],
            ALU.mult, ALU.add), inc="dve")
        ctx.wait("act", v)
        vc2 = ctx.op("act", lambda e: e.activation(cosE[0:64, 0:T2], bRA[0:64, 0:T2], AF.Sin), inc="act")
        ctx.wait("dve", vc2)
        rope_done = None
        for (srcT, dstT, wcol) in ((sB, bnt, bnwc), (sC, cnt, cnwc_)):
            va = ctx.op("act", lambda e, s=srcT: e.activation(sqb, s[0:128, 0:T2], AF.Square), inc="act")
            ctx.wait("pe", va)
            ctx.wait("pe", rope_done)
            ve = None
            for c in range(C2):
                mm = ctx.op("pe", lambda e, c=c: e.matmul(
                    PS[4][0:1, :], env.ones_c_b[:, 0:1], sqb[:, c * 512:(c + 1) * 512],
                    start=True, stop=True), inc="pe")
                ctx.wait("dve", mm)
                ve = ctx.op("dve", lambda e, c=c: e.tensor_copy(
                    rmsr[0:1, c * 512:(c + 1) * 512], PS[4][0:1, :]), inc="dve")
                ctx.wait("pe", ve)
            ctx.wait("act", ve)
            va = ctx.op("act", lambda e: e.activation(
                rmsr[0:1, 0:T2], rmsr[0:1, 0:T2], AF.Sqrt, scale=1.0 / 128.0, bias=LN_EPS), inc="act")
            ctx.wait("dve", va)
            vr = ctx.op("dve", lambda e: e.reciprocal(rmsr[0:1, 0:T2], rmsr[0:1, 0:T2]), inc="dve")
            ctx.wait("pe", vr)
            vb = None
            for c in range(C2):
                mm = ctx.op("pe", lambda e, c=c: e.matmul(
                    PS[4][:, :], env.ones_r_f[0:1, :], rmsr[0:1, c * 512:(c + 1) * 512],
                    start=True, stop=True), inc="pe")
                ctx.wait("dve", mm)
                vb = ctx.op("dve", lambda e, c=c: e.tensor_copy(
                    ibc[:, c * 512:(c + 1) * 512], PS[4][:, :]), inc="dve")
                ctx.wait("pe", vb)
            ctx.drain("dve")
            vn = ctx.op("dve", lambda e, s=srcT, wcol=wcol: e.scalar_tensor_tensor(
                sqb, s[0:128, 0:T2], wcol, ibc, ALU.mult, ALU.mult), inc="dve")
            ctx.wait("pe", vn)
            ctx.wait("pe", lf2)
            vvv = None
            for c in range(C2):
                mm = ctx.op("pe", lambda e, c=c: e.matmul(
                    PS[5][:, :], flip2, sqb[:, c * 512:(c + 1) * 512], start=True, stop=True), inc="pe")
                ctx.wait("dve", mm)
                ctx.op("dve", lambda e, c=c: e.tensor_mul(
                    t1s, sqb[:, c * 512:(c + 1) * 512], cosE[:, c * 512:(c + 1) * 512]))
                ctx.drain("dve")
                ctx.op("dve", lambda e, c=c: e.tensor_mul(
                    t2s, PS[5][:, :], sinE[:, c * 512:(c + 1) * 512]))
                ctx.drain("dve")
                vvv = ctx.op("dve", lambda e, d=dstT, c=c, t0=t0: e.tensor_add(
                    d[:, t0 + c * 512:t0 + (c + 1) * 512], t1s, t2s), inc="dve")
                ctx.wait("pe", vvv)
            ctx.wait("act", vvv)
            rope_done = vvv
        ctx.drain("dve")
        v = ctx.op("dve", lambda e, t0=t0: e.tensor_mul(
            p2b, bnt[:, t0:t0 + T2], cnt[:, t0:t0 + T2]), inc="dve")
        ctx.wait("pe", v)
        ctx.wait("pe", wq_ev)
        vdm = None
        for c in range(C2):
            mm = ctx.op("pe", lambda e, c=c: e.matmul(
                PS[6][0:32, :], wq2, p2b[:, c * 512:(c + 1) * 512], start=True, stop=True), inc="pe")
            ctx.wait("dve", mm)
            vdm = ctx.op("dve", lambda e, c=c: e.tensor_copy(
                bDM[0:32, c * 512:(c + 1) * 512], PS[6][0:32, :]), inc="dve")
            ctx.wait("pe", vdm)
        ctx.drain("dve")
        ctx.op("dve", lambda e: e.tensor_mul(bDM[0:32, 0:T2], bDM[0:32, 0:T2], bDT[0:32, 0:T2]))
        ctx.drain("dve")
        ctx.op("dve", lambda e: e.tensor_scalar(bDM[0:32, 0:T2], bDM[0:32, 0:T2], dskc[0:32, 0:1], None, ALU.add))
        ctx.drain("dve")
        tok_ev = None
        for i0 in range(T2 // 128):
            i = (t0 // 128) + i0
            for (srcap, dst) in ((bLGC[0:32, :], lgct), (bCA[0:32, :], cat),
                                 (bG[0:32, :], gt), (bDM[0:32, :], sht)):
                ctx.wait("pe", tok_ev)
                pev = ctx.op("pe", lambda e, s=srcap, i0=i0: e.transpose(
                    PS[4][:, 0:32], s[:, i0 * 128:(i0 + 1) * 128], env.id_f[0:32, 0:32]), inc="pe")
                ctx.wait("dve", pev)
                tok_ev = ctx.op("dve", lambda e, d=dst, i=i: e.tensor_copy(
                    d[:, 32 * i:32 * (i + 1)], PS[4][:, 0:32]), inc="dve")
        bnt_done = tok_ev
        span_dve = ctx.milestone("dve")
        span_act = ctx.milestone("act")
    ctx.barrier("l1b_s2", queues=env.ALLQ)

    ar5 = Arena(env.AR, PERSIST_END, env.ARCOLS)
    wol = [ar5.bf(2048) for _ in range(2)]
    woT = [ar5.bf(1024) for _ in range(16)]
    bntok = [ar5.bf(128) for _ in range(NT)]
    xa = [ar5.bf(2048) for _ in range(2)]
    za = [ar5.bf(2048) for _ in range(2)]
    yt = [ar5.bf(2048) for _ in range(2)]
    ytT = [ar5.bf(512) for _ in range(16)]
    hS = ar5.f32(2048)
    hB = ar5.bf(2048)
    lstr = ar5.f32(128)
    carow = ar5.f32(128)
    e2_4 = ar5.f32(512)
    cnw4 = ar5.bf(512)
    F4 = ar5.bf(512)
    xcs4 = ar5.bf(256)
    rep4 = ar5.f32(512)
    rhs32 = ar5.f32(32)
    sel8s = ar5.f32(1024)
    selms = ar5.f32(4)
    onz32 = ar5.f32(128)
    caeA = [ar5.f32(32) for _ in range(2)]
    dt_all = ar5.f32(32)
    edc_all = ar5.f32(32)
    cc_all = ar5.f32(32)
    ys1 = ar5.f32(64)
    ys2 = ar5.f32(64)
    zsa = [ar5.f32(2048) for _ in range(2)]
    outs = [ar5.f32(512) for _ in range(2)]
    lw = []
    tr_ev = None
    for mt in range(8):
        sl = mt % 2
        ctx.wait("pool", tr_ev)
        ldm = env.qwt.issue(lambda e, mt=mt, s=wol[sl]: e.dma_start(
            out=s, in_=env.D["L1b_Wout"][mt * 128:(mt + 1) * 128, :]))
        ctx.wait("pe", ldm)
        for rc in range(16):
            ctx.wait("pe", tr_ev)
            pev = ctx.op("pe", lambda e, s=wol[sl], rc=rc: e.transpose(
                env.PB[:, 0:128], s[:, rc * 128:(rc + 1) * 128], env.id_b[:]), inc="pe")
            ctx.wait("dve", pev)
            tr_ev = ctx.op("dve", lambda e, rc=rc, mt=mt: e.tensor_copy(
                woT[rc][:, mt * 128:(mt + 1) * 128], env.PB[:, 0:128]), inc="dve")
    lls = env.qsp.issue(lambda e: e.dma_start(out=lstr, in_=env.D["c_lstrict"][:, :]))
    lsel = env.qsp.issue(lambda e: e.dma_start(out=sel8s[0:32, :], in_=env.D["c_sel8"][:, :]))
    lselm = env.qsp.issue(lambda e: e.dma_start(out=selms[0:32, :], in_=env.D["c_selmask"][:, :]))
    for i in range(NT):
        ctx.wait("pe", tr_ev)
        pev = ctx.op("pe", lambda e, i=i: e.transpose(
            env.PB[:, 0:128], bnt[:, i * 128:(i + 1) * 128], env.id_b[:]), inc="pe")
        ctx.wait("dve", pev)
        tr_ev = ctx.op("dve", lambda e, i=i: e.tensor_copy(bntok[i], env.PB[:, 0:128]), inc="dve")
    vh0 = ctx.op("pool", lambda e: e.memset(hS[:, :], 0.0), inc="pool")
    vo1 = ctx.op("pool", lambda e: e.memset(onz32[0:32, :], 1.0), inc="pool")
    ctx.wait("pe", tr_ev)
    ctx.wait("dve", lls)
    ctx.wait("dve", vh0)
    ctx.wait("dve", lselm)
    ctx.wait("pe", lsel)
    ctx.wait("pe", vo1)
    xld = [None, None]
    xfree = [None, None]
    zsa_free = [None, None]
    ost = [None, None]
    oev_t = [None, None]
    grp_pe = None
    pre6_tok = None
    vcaea_tok = None
    vexp_de = None
    add_tok = [None, None]
    vyi_tok = None
    hst_tok = None
    v2_guard = [None, None]
    for i in range(NT):
        sl = i % 2
        cur, prv = caeA[i % 2], caeA[(i + 1) % 2]
        ctx.wait("sp", xfree[sl])
        xld[sl] = env.qsp.issue(lambda e, i=i, s=xa[sl]: e.dma_start(
            out=s, in_=env.s_xtok[i * 128:(i + 1) * 128, :]))
        zl = env.qsp.issue(lambda e, i=i, s=za[sl]: e.dma_start(
            out=s, in_=env.s_ztok[i * 128:(i + 1) * 128, :]))
        ctx.wait("dve", xld[sl])
        ctx.wait("act", zl)
        ctx.wait("act", zsa_free[sl])
        vzs = ctx.op("act", lambda e, sl=sl: e.activation(zsa[sl], za[sl], AF.Silu), inc="act")
        ctx.wait("dve", vzs)
        ctx.wait("pe", pre6_tok)
        ptr32 = ctx.op("pe", lambda e, i=i: e.transpose(
            PS[6][0:32, 0:128], cat[:, 32 * i:32 * (i + 1)], env.id_f[:, :]), inc="pe")
        ctx.wait("dve", ptr32)
        vcr32 = ctx.op("dve", lambda e: e.tensor_copy(carow[0:32, 0:128], PS[6][0:32, 0:128]), inc="dve")
        ctx.drain("dve")
        vr32 = ctx.op("dve", lambda e: e.tensor_scalar(
            rhs32[0:32, 0:32], env.id_f[0:32, 0:32], carow[0:32, 127:128], None, ALU.mult), inc="dve")
        vrep = None
        for j in range(4):
            vrep = ctx.op("dve", lambda e, j=j: e.tensor_scalar(
                rep4[0:32, 128 * j:128 * (j + 1)], carow[0:32, 0:128], selms[0:32, j:j + 1],
                None, ALU.mult), inc="dve" if j == 3 else None)
        ctx.wait("pe", vr32)
        mm_cae = ctx.op("pe", lambda e: e.matmul(
            PS[6][:, 0:32], onz32[0:32, :], rhs32[0:32, 0:32], start=True, stop=True), inc="pe")
        ctx.wait("dve", mm_cae)
        vcaea = ctx.op("dve", lambda e, cur=cur: e.tensor_copy(cur, PS[6][:, 0:32]), inc="dve")
        vcaea_tok = vcaea
        ctx.drain("dve")
        if i > 0:
            ctx.op("dve", lambda e, cur=cur, prv=prv: e.tensor_sub(dt_all, cur, prv))
            v = ctx.op("dve", lambda e, i=i, prv=prv: e.tensor_sub(
                edc_all, cat[:, 32 * i:32 * (i + 1)], prv), inc="dve")
            ctx.wait("act", v)
            ctx.op("act", lambda e: e.activation(dt_all, dt_all, AF.Exp))
            vexp_de = ctx.op("act", lambda e: e.activation(edc_all, edc_all, AF.Exp), inc="act")
        v = ctx.op("dve", lambda e, i=i, cur=cur: e.tensor_sub(
            cc_all, cur, cat[:, 32 * i:32 * (i + 1)]), inc="dve")
        ctx.wait("act", v)
        vca = ctx.op("act", lambda e: e.activation(cc_all, cc_all, AF.Exp), inc="act")
        ctx.wait("dve", vca)
        ctx.op("dve", lambda e, i=i: e.tensor_mul(cc_all, cc_all, gt[:, 32 * i:32 * (i + 1)]))
        ctx.drain("dve")
        for c in range(8):
            pb = PS[3 + (c % 2)]
            if c == 0:
                ctx.wait("pe", oev_t[0])
                ctx.wait("pe", oev_t[1])
                ctx.wait("pe", vrep)
            ctx.wait("pe", v2_guard[c % 2])
            bc_tok = ctx.op("pe", lambda e, c=c, pb=pb: e.matmul(
                pb[:, 0:512], sel8s[0:32, 128 * c:128 * (c + 1)], rep4[0:32, 0:512],
                start=True, stop=True), inc="pe")
            ctx.wait("dve", bc_tok)
            if i > 0 and c == 0:
                ctx.wait("dve", vexp_de)
            v2l = None
            for j in range(4):
                h = 4 * c + j
                v2l = ctx.op("dve", lambda e, j=j, h=h, i=i, pb=pb: e.scalar_tensor_tensor(
                    e2_4[:, 128 * j:128 * (j + 1)], pb[:, 128 * j:128 * (j + 1)],
                    lgct[:, 32 * i + h:32 * i + h + 1], lstr, ALU.add, ALU.add), inc="dve")
            v2_guard[c % 2] = v2l
            ctx.wait("act", v2l)
            vE4 = ctx.op("act", lambda e: e.activation(e2_4, e2_4, AF.Exp), inc="act")
            vcwl = None
            for j in range(4):
                h = 4 * c + j
                vcwl = ctx.op("dve", lambda e, j=j, h=h, i=i: e.tensor_scalar(
                    cnw4[:, 128 * j:128 * (j + 1)], cnt[:, i * 128:(i + 1) * 128],
                    wq2[:, h:h + 1], None, ALU.mult), inc="dve")
            vxl = None
            for j in range(4):
                h = 4 * c + j
                vxl = ctx.op("dve", lambda e, j=j, h=h, s=xa[sl]: e.tensor_scalar(
                    xcs4[:, 64 * j:64 * (j + 1)], s[:, h * 64:(h + 1) * 64],
                    cc_all[:, h:h + 1], None, ALU.mult), inc="dve")
            ctx.wait("pe", vcwl)
            mmg4 = ctx.op("pe", lambda e, i=i: e.matmul(
                PS[5][:, 0:512], bnt[:, i * 128:(i + 1) * 128], cnw4[:, 0:512],
                start=True, stop=True), inc="pe")
            ctx.wait("dve", mmg4)
            ctx.wait("dve", vE4)
            vF4 = ctx.op("dve", lambda e: e.tensor_mul(F4, PS[5][:, 0:512], e2_4), inc="dve")
            for j in range(4):
                h = 4 * c + j
                yb = PS[0] if h % 2 == 0 else PS[6]
                hsl = hS[:, 64 * h:64 * (h + 1)]
                hbl = hB[:, 64 * h:64 * (h + 1)]
                sh_col = sht[:, 32 * i + h:32 * i + h + 1]
                if j == 0:
                    ctx.wait("pe", vF4)
                    ctx.wait("pe", xld[sl])
                    ctx.wait("pe", vxl)
                if h == 1:
                    ctx.wait("pe", vcaea_tok)
                ctx.wait("pe", add_tok[h % 2])
                mmy = ctx.op("pe", lambda e, j=j, h=h, s=xa[sl], yb=yb: e.matmul(
                    yb[0:128, 0:64], F4[:, 128 * j:128 * (j + 1)], s[:, h * 64:(h + 1) * 64],
                    start=True, stop=True), inc="pe")
                if i > 0:
                    ctx.wait("pe", vyi_tok)
                    mmyi = ctx.op("pe", lambda e, j=j, hbl=hbl: e.matmul(
                        PS[1][0:128, 0:64], cnw4[:, 128 * j:128 * (j + 1)], hbl,
                        start=True, stop=True), inc="pe")
                ctx.wait("pe", hst_tok)
                mmh = ctx.op("pe", lambda e, j=j, i=i: e.matmul(
                    PS[2][0:128, 0:64], bntok[i], xcs4[:, 64 * j:64 * (j + 1)],
                    start=True, stop=True), inc="pe")
                ctx.wait("dve", mmh)
                if i > 0:
                    hst_tok = ctx.op("dve", lambda e, h=h, hsl=hsl: e.scalar_tensor_tensor(
                        hsl, hsl, dt_all[:, h:h + 1], PS[2][0:128, 0:64], ALU.mult, ALU.add), inc="dve")
                    ctx.wait("dve", mmyi)
                    vyi_tok = ctx.op("dve", lambda e, h=h: e.tensor_scalar(
                        ys1, PS[1][0:128, 0:64], edc_all[:, h:h + 1], None, ALU.mult), inc="dve")
                    ctx.drain("dve")
                    ctx.op("dve", lambda e, s=xa[sl], h=h, sh=sh_col: e.scalar_tensor_tensor(
                        ys2, s[:, h * 64:(h + 1) * 64], sh, ys1, ALU.mult, ALU.add))
                    ctx.op("dve", lambda e, hsl=hsl, hbl=hbl: e.tensor_copy(hbl, hsl))
                else:
                    hst_tok = ctx.op("dve", lambda e, hsl=hsl: e.tensor_copy(
                        hsl, PS[2][0:128, 0:64]), inc="dve")
                    ctx.drain("dve")
                    ctx.op("dve", lambda e, s=xa[sl], h=h, sh=sh_col: e.tensor_scalar(
                        ys2, s[:, h * 64:(h + 1) * 64], sh, None, ALU.mult))
                    ctx.op("dve", lambda e, hsl=hsl, hbl=hbl: e.tensor_copy(hbl, hsl))
                ctx.drain("dve")
                ctx.wait("dve", mmy)
                add_tok[h % 2] = ctx.op("dve", lambda e, yb=yb: e.tensor_add(
                    ys1, ys2, yb[0:128, 0:64]), inc="dve")
                ctx.drain("dve")
                vy = ctx.op("dve", lambda e, d=yt[sl], h=h, sl=sl: e.tensor_mul(
                    d[:, h * 64:(h + 1) * 64], ys1, zsa[sl][:, h * 64:(h + 1) * 64]), inc="dve")
                ctx.drain("dve")
        pre6_tok = add_tok[1]
        ctx.wait("pe", vy)
        for rc in range(16):
            if i % 4 == 0 and rc == 0:
                ctx.wait("pe", grp_pe)
            ctx.wait("pe", tr_ev)
            pev = ctx.op("pe", lambda e, rc=rc, sl=sl: e.transpose(
                env.PB[:, 0:128], yt[sl][:, rc * 128:(rc + 1) * 128], env.id_b[:]), inc="pe")
            ctx.wait("dve", pev)
            tr_ev = ctx.op("dve", lambda e, rc=rc, i=i: e.tensor_copy(
                ytT[rc][:, (i % 4) * 128:((i % 4) + 1) * 128], env.PB[:, 0:128]), inc="dve")
        xfree[sl] = ctx.milestone("dve")
        zsa_free[sl] = ctx.milestone("dve")
        if i % 4 == 3:
            qg = i // 4
            ctx.wait("pe", tr_ev)
            for mt in range(8):
                pk = PS[3 + (mt % 2)]
                ctx.wait("pe", oev_t[mt % 2])
                for rc in range(16):
                    mm = ctx.op("pe", lambda e, rc=rc, mt=mt, pk=pk: e.matmul(
                        pk[:, :], woT[rc][:, mt * 128:(mt + 1) * 128], ytT[rc],
                        start=(rc == 0), stop=(rc == 15)), inc="pe" if rc == 15 else None)
                ctx.wait("dve", mm)
                ctx.wait("dve", ost[mt % 2])
                oev_t[mt % 2] = ctx.op("dve", lambda e, pk=pk, o=outs[mt % 2]: e.tensor_copy(
                    o, pk[:, :]), inc="dve")
                ctx.wait("pool", oev_t[mt % 2])
                ost[mt % 2] = env.qac.issue(lambda e, o=outs[mt % 2], mt=mt, qg=qg: e.dma_start(
                    out=env.resid[mt * 128:(mt + 1) * 128, qg * 512:(qg + 1) * 512], in_=o,
                    accum_op=ALU.add))
            grp_pe = ctx.milestone("pe")
    ctx.barrier("l1b", queues=env.ALLQ)


NAME_MAP = {
    "L0a_Wq": "L0_attn_a_W_q",
    "L0f_Wr": "L0_ffn_Wr", "L0f_br": "L0_ffn_br",
    "L0f_g": "L0_ffn_ln_g", "L0f_b": "L0_ffn_ln_b",
    "L0f_W1": "L0_ffn_W1", "L0f_b1": "L0_ffn_b1",
    "L0f_W2": "L0_ffn_W2", "L0f_b2": "L0_ffn_b2",
    "L0b_WQ": "L0_attn_b_W_Q", "L0b_bQ": "L0_attn_b_b_Q",
    "L0b_WK": "L0_attn_b_W_K", "L0b_bK": "L0_attn_b_b_K",
    "L0b_WV": "L0_attn_b_W_V", "L0b_bV": "L0_attn_b_b_V",
    "L0b_WO": "L0_attn_b_W_O", "L0b_bO": "L0_attn_b_b_O",
    "L1a_W1": "L1_attn_a_W1", "L1a_W2": "L1_attn_a_W2",
    "L1a_bias": "L1_attn_a_bias", "L1a_et": "L1_attn_a_embeddings_table",
    "L1f_CB": "L1_ffn_CB", "L1f_SCB": "L1_ffn_SCB", "L1f_bias": "L1_ffn_bias",
    "L1b_Win": "L1_attn_b_W_in", "L1b_dtb": "L1_attn_b_dt_bias",
    "L1b_Bb": "L1_attn_b_B_bias", "L1b_Cb": "L1_attn_b_C_bias",
    "L1b_Bnw": "L1_attn_b_B_norm_w", "L1b_Cnw": "L1_attn_b_C_norm_w",
    "L1b_Dsk": "L1_attn_b_D_skip", "L1b_Wout": "L1_attn_b_W_out",
}


def build_consts(T):
    c = {}
    c["c_eye"] = np.eye(128, dtype=np.float32)
    pos = np.arange(T, dtype=np.float32)
    j = np.arange(64)
    freq = (10000.0 ** ((j % 32) / 32.0)).astype(np.float32)
    ang = pos[None, :] / freq[:, None]
    cos = np.cos(ang).astype(np.float32)
    sin = np.sin(ang).astype(np.float32)
    c["c_rope_cos"] = np.concatenate([cos, cos], axis=0)
    c["c_rope_sin"] = np.concatenate([sin, sin], axis=0)
    F = np.zeros((128, 128), np.float32)
    for base in (0, 64):
        for r in range(32):
            F[base + r, base + r + 32] = -1.0
            F[base + r + 32, base + r] = 1.0
    c["c_ropeflip"] = np.ascontiguousarray(F.T)
    qk = np.zeros((4 * 128, 512), np.float32)
    kq = np.zeros((4 * 128, 512), np.float32)
    for m in range(4):
        q_idx = np.arange(128)[:, None] + m * 128
        k_idx = np.arange(512)[None, :]
        qk[m * 128:(m + 1) * 128] = np.where(k_idx <= q_idx, 0.0, NEG)
        k_idx2 = np.arange(128)[:, None] + m * 128
        q_idx2 = np.arange(512)[None, :]
        kq[m * 128:(m + 1) * 128] = np.where(q_idx2 >= k_idx2, 0.0, NEG)
    c["c_cmask_qk"] = qk
    c["c_cmask_kq"] = kq
    s_ = np.arange(128)[:, None]
    t_ = np.arange(128)[None, :]
    c["c_lstrict"] = np.where(t_ >= s_ + 1, 0.0, NEG).astype(np.float32)
    k32_ = np.arange(32)
    sel8_ = np.zeros((32, 1024), dtype=np.float32)
    for cc_ in range(8):
        sel8_[4 * cc_:4 * cc_ + 4, 128 * cc_:128 * (cc_ + 1)] = 1.0
    c["c_sel8"] = sel8_
    c["c_selmask"] = (k32_[:, None] % 4 == np.arange(4)[None, :]).astype(np.float32)
    return c


_CACHE = {}


def kernel(**inputs):
    from concourse.bass_utils import run_bass_kernel_spmd
    T = 4096
    if "nc" not in _CACHE:
        nc, env = build(T=T, parts=ALL_PARTS)
        _CACHE["nc"] = nc
        _CACHE["env"] = env
    nc = _CACHE["nc"]
    base = {k: np.ascontiguousarray(np.asarray(inputs[v], dtype=np.float32))
            for k, v in NAME_MAP.items()}
    base.update(build_consts(T))
    x = np.asarray(inputs["L0_attn_a_hidden_states"], dtype=np.float32)
    in_maps = []
    for b in range(8):
        m = dict(base)
        m["x"] = np.ascontiguousarray(x[b])
        in_maps.append(m)
    res = run_bass_kernel_spmd(nc, in_maps, core_ids=list(range(8)))
    out = np.stack([np.asarray(r["out"]) for r in res.results], axis=0)
    return out.astype(np.float32)
